# version 06 (final)
# baseline (speedup 1.0000x reference)
import numpy as np
from contextlib import ExitStack
import concourse.bass as bass
import concourse.mybir as mybir
from concourse.bass_utils import run_bass_kernel_spmd

F32 = mybir.dt.float32
BF16 = mybir.dt.bfloat16
I32 = mybir.dt.int32
AF = mybir.ActivationFunctionType
ALU = mybir.AluOpType
AX = mybir.AxisListType

B, T, D = 4, 2048, 2048
TOK = 1024
NH, HD = 16, 128
I0 = 5632
E, NST, DTR, DC = 4096, 16, 128, 4
EH = 2048
I1 = 8192
NT_OWN = TOK // 128
NT_ALL = T // 128
ND = D // 128
MAGIC0 = 0x3A7C01D5
MAGIC1 = 0x5EC0FFEE
INV_SQRT_HD = float(1.0 / np.sqrt(128.0))
NEG_BIG = -1e30
ENGINES = ["tensor", "vector", "scalar", "gpsimd", "sync"]

_TAPS = {}


def own_rows(parity):
    return [(0, 512), (1536, 2048)] if parity == 0 else [(512, 1536)]


def q512_groups(parity):
    return [0, 3] if parity == 0 else [1, 2]


class P:
    def __init__(self, nc, es, taps):
        self.nc, self.es = nc, es
        self.block = None
        self.taps = list(taps)
        self.sems, self.cnt = {}, {}
        self.sb, self.ps, self.dram = {}, {}, {}
        self.phase = 0
        self.stack = es
        self.phase_new = []

    def sem(self, name):
        if name not in self.sems:
            self.sems[name] = self.nc.alloc_semaphore(name=name + "_" + str(self.phase))
            self.cnt[name] = 0
            if not name.startswith("ph"):
                self.phase_new.append(name)
        return self.sems[name]

    def inc(self, instr, name, n=1):
        instr.then_inc(self.sem(name), n)
        self.cnt[name] += n
        return self.cnt[name]

    def val(self, name):
        self.sem(name)
        return self.cnt[name]

    def sbuf(self, name, shape, dtype, local=False):
        if local == "group":
            st = self.gstack
        elif local:
            st = self.stack
        else:
            st = self.es
        h = st.enter_context(self.nc.sbuf_tensor(name, shape, dtype))
        self.sb[name] = h
        return h

    def psum(self, name, shape, dtype=F32, local=False):
        st = self.stack if local else self.es
        h = st.enter_context(self.nc.psum_tensor(name, shape, dtype))
        self.ps[name] = h
        return h

    def barrier(self):
        self.phase += 1
        ph = self.phase
        for nm in ENGINES:
            self.sem("ph_" + nm)
        self.sem("ph2")
        retiring = [n for n in self.phase_new]

        def mk(nm):
            def f(eng):
                if nm != "sync":
                    eng.drain()
                eng.nop().then_inc(self.sems["ph_" + nm], 1)
                for other in ENGINES:
                    eng.wait_ge(self.sems["ph_" + other], ph)
                if nm == "gpsimd":
                    eng.dma_reset()
                    for rn in retiring:
                        eng.sem_clear(self.sems[rn])
                    eng.nop().then_inc(self.sems["ph2"], 1)
                eng.wait_ge(self.sems["ph2"], ph)
            return f
        for nm in ENGINES:
            getattr(self.block, nm)(mk(nm))
            self.cnt["ph_" + nm] = ph
        self.cnt["ph2"] = ph
        for rn in retiring:
            self.nc.release_semaphore(self.sems[rn])
            del self.sems[rn]
            del self.cnt[rn]
        self.phase_new = []

    def branch_parity(self, eng, regname, emit_fn):
        snap = dict(self.cnt)
        with eng.register(regname) as r:
            eng.reg_load(r, self.sb["pari"][0:1, 0:1])
            with eng.If_eq(r, 0):
                emit_fn(0)
            after0 = dict(self.cnt)
            self.cnt = {**{k: 0 for k in after0}, **snap}
            with eng.Else():
                emit_fn(1)
        assert self.cnt == after0, (
            "branch sem mismatch: " + str({
                k: (self.cnt.get(k), after0.get(k))
                for k in set(self.cnt) | set(after0)
                if self.cnt.get(k) != after0.get(k)}))



def emit_layernorm_vec(p, v, xt_ap, out_f32_ap, tag, n):
    st = p.sb["stt"]
    v.reduce_sum(st[:, 0:1], xt_ap, AX.X)
    v.wait_ge(p.sem("sc_sq_" + tag), n + 1)
    v.drain()
    v.tensor_scalar(st[:, 2:3], st[:, 0:1], 1.0 / D, None, ALU.mult)
    v.tensor_scalar(st[:, 3:4], st[:, 1:2], 1.0 / D, None, ALU.mult)
    v.drain()
    v.tensor_tensor(st[:, 4:5], st[:, 2:3], st[:, 2:3], ALU.mult)
    v.drain()
    v.tensor_tensor(st[:, 5:6], st[:, 3:4], st[:, 4:5], ALU.subtract)
    v.drain()
    ins = v.tensor_scalar(st[:, 6:7], st[:, 5:6], 1e-5, None, ALU.add)
    p.inc(ins, "v_var_" + tag, 1)
    v.wait_ge(p.sem("sc_sqrt_" + tag), n + 1)
    v.reciprocal(st[:, 8:9], st[:, 7:8])
    v.drain()
    v.tensor_scalar(out_f32_ap, xt_ap, st[:, 2:3], st[:, 8:9],
                    ALU.subtract, ALU.mult)


def emit_layernorm_sca(p, s, xt_ap, tag, n):
    st = p.sb["stt"]
    ins = s.activation(p.sb["sqt"][:], xt_ap, AF.Square, accum_out=st[:, 1:2])
    p.inc(ins, "sc_sq_" + tag, 1)
    s.wait_ge(p.sem("v_var_" + tag), n + 1)
    ins = s.activation(st[:, 7:8], st[:, 6:7], AF.Sqrt)
    p.inc(ins, "sc_sqrt_" + tag, 1)



def phase_prologue(p):
    dram, block = p.dram, p.block

    @block.sync
    def _(sync):
        for nm, dst in [("cst", "cstt"), ("identb", "idb"), ("pari_in", "pari")]:
            ins = sync.dma_start(out=p.sb[dst][:], in_=dram[nm])
            p.inc(ins, "s_in", 16)

    @block.vector
    def _(v):
        v.wait_ge(p.sem("s_in"), p.val("s_in"))
        v.memset(p.sb["qsh"][:], 0.75)
        v.drain()
        v.tensor_copy(p.sb["prbi"][:], p.sb["qsh"][:])
        v.drain()
        ins = v.tensor_scalar(p.sb["qsh"][:], p.sb["prbi"][:], -0.5, 512.5,
                              ALU.mult, ALU.add)
        p.inc(ins, "v_pro", 1)



def phase_l0_norm(p):
    dram, block = p.dram, p.block
    p.sbuf("xst", [128, 2, D], F32, local=True)
    p.sbuf("xow", [128, 2, D], F32, local=True)
    p.sbuf("xn", [128, D], F32, local=True)
    p.sbuf("xnb", [128, D], BF16, local=True)

    @block.sync
    def _(sync):
        for n in range(NT_ALL):
            if n >= 2:
                sync.wait_ge(p.sem("v_xn_done"), n - 1)
            ins = sync.dma_start(out=p.sb["xst"][:, n % 2, :],
                                 in_=dram["x_all"][n * 128:(n + 1) * 128, :])
            p.inc(ins, "s_xall", 16)

    @block.scalar
    def _(s):
        for n in range(NT_ALL):
            s.wait_ge(p.sem("s_xall"), (n + 1) * 16)
            emit_layernorm_sca(p, s, p.sb["xst"][:, n % 2, :], "l0", n)

    @block.vector
    def _(v):
        for n in range(NT_ALL):
            v.wait_ge(p.sem("s_xall"), (n + 1) * 16)
            emit_layernorm_vec(p, v, p.sb["xst"][:, n % 2, :], p.sb["xn"][:],
                               "l0", n)
            v.drain()
            ins = v.tensor_copy(p.sb["xnb"][:], p.sb["xn"][:])
            p.inc(ins, "v_xnb", 1)
            v.drain()
            ins = v.nop()
            p.inc(ins, "v_xn_done", 1)
            for g in range(4):
                v.wait_ge(p.sem("t_tp"), n * 4 + g + 1)
                ps = p.ps["ps_t"] if g % 2 == 0 else p.ps["ps_a"]
                dst = p.sb["allT"][:, g * 4:(g + 1) * 4, n * 128:(n + 1) * 128]
                ins = v.tensor_copy(dst, ps[:, 0:512])
                p.inc(ins, "v_evac", 1)

    @block.tensor
    def _(t):
        for n in range(NT_ALL):
            t.wait_ge(p.sem("v_xnb"), n + 1)
            for g in range(4):
                k = n * 4 + g
                ps = p.ps["ps_t"] if g % 2 == 0 else p.ps["ps_a"]
                if k >= 2:
                    t.wait_ge(p.sem("v_evac"), k - 1)
                for j in range(4):
                    dt = g * 4 + j
                    ins = t.transpose(ps[:, j * 128:(j + 1) * 128],
                                      p.sb["xnb"][:, dt * 128:(dt + 1) * 128],
                                      p.sb["idb"][:])
                p.inc(ins, "t_tp", 1)

    @block.gpsimd
    def _(g):
        for n in range(NT_OWN):
            if n >= 2:
                g.wait_ge(p.sem("g_res"), n - 1)
            ins = g.dma_start(out=p.sb["xow"][:, n % 2, :],
                              in_=dram["x_own"][n * 128:(n + 1) * 128, :])
            p.inc(ins, "g_xown", 16)
            g.wait_ge(p.sem("g_xown"), p.val("g_xown"))
            ins = g.tensor_copy(p.sb["res"][:, n, :], p.sb["xow"][:, n % 2, :])
            p.inc(ins, "g_res", 1)



def phase_bitnet_scale(p):
    dram, block = p.dram, p.block
    p.sbuf("bst", [128, I1], F32, local=True)
    p.sbuf("babs", [128, I1], F32, local=True)
    p.sbuf("ones128", [128, 128], BF16, local=True)
    p.sbuf("bnsb", [128, 4], BF16, local=True)
    wl = [("wg1", I1, D), ("wu1", I1, D), ("wd1", D, I1)]
    tiles = [(src, H, ot) for (src, O, H) in wl for ot in range(O // 128)]

    @block.sync
    def _(sync):
        for i, (src, H, ot) in enumerate(tiles):
            if i >= 1:
                sync.wait_ge(p.sem("b_abs"), i)
            ins = sync.dma_start(out=p.sb["bst"][:, 0:H],
                                 in_=dram[src][ot * 128:(ot + 1) * 128, :])
            p.inc(ins, "b_ld", 16)

    @block.scalar
    def _(s):
        for i, (src, H, ot) in enumerate(tiles):
            s.wait_ge(p.sem("b_ld"), (i + 1) * 16)
            if i > 0:
                s.wait_ge(p.sem("b_acc"), i)
            ins = s.activation(p.sb["babs"][:, 0:H], p.sb["bst"][:, 0:H],
                               AF.Abs, accum_out=p.sb["stt"][:, 9:10])
            p.inc(ins, "b_abs", 1)

    @block.vector
    def _(v):
        wi = -1
        prev = None
        for i, (src, H, ot) in enumerate(tiles):
            if src != prev:
                wi += 1
                prev = src
                v.memset(p.sb["bns"][:, 2 * wi:2 * wi + 1], 0.0)
                v.drain()
            v.wait_ge(p.sem("b_abs"), i + 1)
            v.tensor_tensor(p.sb["bns"][:, 2 * wi:2 * wi + 1],
                            p.sb["bns"][:, 2 * wi:2 * wi + 1],
                            p.sb["stt"][:, 9:10], ALU.add)
            v.drain()
            ins = v.nop()
            p.inc(ins, "b_acc", 1)

    @block.vector
    def _(v):
        v.wait_ge(p.sem("b_acc"), len(tiles))
        v.memset(p.sb["ones128"][:], 1.0)
        v.drain()
        for wi in range(3):
            v.tensor_copy(p.sb["bnsb"][:, wi:wi + 1],
                          p.sb["bns"][:, 2 * wi:2 * wi + 1])
        v.drain()
        ins = v.nop()
        p.inc(ins, "b_accb", 1)

    @block.tensor
    def _(t):
        t.wait_ge(p.sem("b_accb"), 1)
        for wi in range(3):
            ins = t.matmul(p.ps["ps_s"][:, wi:wi + 1], p.sb["ones128"][:],
                           p.sb["bnsb"][:, wi:wi + 1], start=True, stop=True)
            p.inc(ins, "b_red", 1)

    @block.vector
    def _(v):
        v.wait_ge(p.sem("b_red"), 3)
        for wi in range(3):
            v.tensor_copy(p.sb["bns"][:, 2 * wi:2 * wi + 1],
                          p.ps["ps_s"][:, wi:wi + 1])
        v.drain()
        for wi, (src, O, H) in enumerate(wl):
            v.tensor_scalar(p.sb["bns"][:, 2 * wi:2 * wi + 1],
                            p.sb["bns"][:, 2 * wi:2 * wi + 1],
                            1.0 / (O * H), 1e-5, ALU.mult, ALU.max)
        v.drain()
        for wi in range(3):
            v.reciprocal(p.sb["bns"][:, 2 * wi + 1:2 * wi + 2],
                         p.sb["bns"][:, 2 * wi:2 * wi + 1])
        v.drain()
        ins = v.nop()
        p.inc(ins, "b_done", 1)



WPREP_L0 = [("Wq", "wqT", D, D, "plain"), ("Wk", "wkT", D, D, "plain"),
            ("Wv", "wvT", D, D, "plain"), ("Wo", "woT", D, D, "plain")]
WPREP_REST = [
    ("w_gu", "w1qT", 2 * I0, D, "rowq"),
    ("w_dn", "w2qT", D, I0, "rowq"),
    ("W_in", "winT", E, D, "plain"),
    ("W_skip_own", "wskT", EH, D, "plain"),
    ("W_out_own", "woutT", D, EH, "plain"),
    ("Wd1", "wd1T", 128, E, "plain"),
    ("Wd2_own", "wd2T", EH, 128, "plain"),
    ("WB", "wbT", 128, E, "plain"),
    ("WC", "wcT", 128, E, "plain"),
    ("wg1", "wg1T", I1, D, "bitnet", 0),
    ("wu1", "wu1T", I1, D, "bitnet", 1),
    ("wd1", "wdn1T", D, I1, "bitnet", 2),
]



def _tp_batches(H):
    nh = H // 128
    out = []
    i = 0
    while i < nh:
        c = min(4, nh - i)
        out.append((i, c))
        i += c
    return out

def phase_weight_prep(p, wlist, tag):
    dram, block = p.dram, p.block
    HMAX = max(w[3] for w in wlist)
    SEC = 2048
    p.sbuf("wst" + tag, [128, HMAX], F32, local=True)
    p.sbuf("wbb" + tag, [128, HMAX], BF16, local=True)
    p.sbuf("wyb" + tag, [128, SEC], F32, local=True)
    p.sbuf("wib" + tag, [128, SEC], I32, local=True)
    p.sbuf("wtg" + tag, [128, HMAX // 128, 128], BF16, local=True)
    p.sbuf("wqs" + tag, [128, 8], F32, local=True)
    S_LD, S_RDY, S_TP, S_EV, S_ST = ("wpld" + tag, "wprdy" + tag, "wptp" + tag,
                                     "wpev" + tag, "wpst" + tag)
    S_VQ, S_SQ = "wpvq" + tag, "wpsq" + tag

    tasks = []
    for w in wlist:
        src, dst, O, H, mode = w[:5]
        bix = w[5] if len(w) > 5 else None
        tasks.extend((src, dst, O, H, mode, bix, ot) for ot in range(O // 128))

    def secs(H):
        out = []
        w0 = 0
        while w0 < H:
            w = min(SEC, H - w0)
            out.append((w0, w))
            w0 += w
        return out
    ev_upto, tp_upto = [], []
    tot = tot2 = 0
    for (src, dst, O, H, mode, bix, ot) in tasks:
        tot += len(_tp_batches(H))
        tot2 += H // 128
        ev_upto.append(tot)
        tp_upto.append(tot2)

    @block.sync
    def _(sync):
        for i, (src, dst, O, H, mode, bix, ot) in enumerate(tasks):
            if i >= 1:
                sync.wait_ge(p.sem(S_RDY), i)
            ins = sync.dma_start(out=p.sb["wst" + tag][:, 0:H],
                                 in_=dram[src][ot * 128:(ot + 1) * 128, :])
            p.inc(ins, S_LD, 16)

    @block.vector
    def _(v):
        nsq = 0
        k = 0
        for i, (src, dst, O, H, mode, bix, ot) in enumerate(tasks):
            if mode == "rowq":
                v.wait_ge(p.sem(S_LD), (i + 1) * 16)
                wq = p.sb["wqs" + tag]
                v.reduce_max(wq[:, 0:1], p.sb["wst" + tag][:, 0:H],
                             AX.X, apply_absolute_value=True)
                v.drain()
                v.tensor_scalar(wq[:, 1:2], wq[:, 0:1], 1.0 / 127.0, 1e-8,
                                ALU.mult, ALU.max)
                v.drain()
                v.reciprocal(wq[:, 2:3], wq[:, 1:2])
                v.drain()
                ins = v.nop()
                p.inc(ins, S_VQ, 1)
            if mode in ("rowq", "bitnet"):
                sc = (p.sb["wqs" + tag][:, 1:2] if mode == "rowq"
                      else p.sb["bns"][:, 2 * bix:2 * bix + 1])
                for (w0, w) in secs(H):
                    nsq += 1
                    v.wait_ge(p.sem(S_SQ), nsq)
                    if i >= 1 and w0 == 0:
                        v.wait_ge(p.sem(S_TP), ev_upto[i - 1])
                    if mode == "bitnet":
                        v.tensor_scalar(p.sb["wyb" + tag][:, 0:w],
                                        p.sb["wib" + tag][:, 0:w],
                                        511.0, 513.0, ALU.max, ALU.min)
                        v.drain()
                        ins = v.tensor_scalar(
                            p.sb["wbb" + tag][:, w0:w0 + w],
                            p.sb["wyb" + tag][:, 0:w],
                            512.0, sc, ALU.subtract, ALU.mult)
                    else:
                        ins = v.tensor_scalar(
                            p.sb["wbb" + tag][:, w0:w0 + w],
                            p.sb["wib" + tag][:, 0:w],
                            512.0, sc, ALU.subtract, ALU.mult)
                    p.inc(ins, "wsec" + tag, 1)
                v.drain()
                ins = v.nop()
                p.inc(ins, S_RDY, 1)
            if i >= 1:
                v.wait_ge(p.sem(S_ST), i * 16)
            for (h0, cnt) in _tp_batches(H):
                v.wait_ge(p.sem(S_TP), k + 1)
                ps = p.ps["ps_t"] if k % 2 == 0 else p.ps["ps_a"]
                ins = v.tensor_copy(p.sb["wtg" + tag][:, h0:h0 + cnt, :],
                                    ps[:, 0:cnt * 128])
                p.inc(ins, S_EV, 1)
                k += 1

    @block.scalar
    def _(scl):
        nvq = nsq = 0
        first_bitnet = True
        for i, (src, dst, O, H, mode, bix, ot) in enumerate(tasks):
            if mode == "plain":
                scl.wait_ge(p.sem(S_LD), (i + 1) * 16)
                if i >= 1:
                    scl.wait_ge(p.sem(S_TP), ev_upto[i - 1])
                ins = scl.copy(p.sb["wbb" + tag][:, 0:H],
                               p.sb["wst" + tag][:, 0:H])
                p.inc(ins, S_RDY, 1)
            else:
                if mode == "rowq":
                    nvq += 1
                    scl.wait_ge(p.sem(S_VQ), nvq)
                    sc = p.sb["wqs" + tag][:, 2:3]
                else:
                    scl.wait_ge(p.sem(S_LD), (i + 1) * 16)
                    first_bitnet = False
                    sc = p.sb["bns"][:, 2 * bix + 1:2 * bix + 2]
                for (w0, w) in secs(H):
                    if nsq >= 1:
                        scl.wait_ge(p.sem("wsec" + tag), nsq)
                    scl.activation(p.sb["wyb" + tag][:, 0:w],
                                   p.sb["wst" + tag][:, w0:w0 + w],
                                   AF.Identity, scale=sc,
                                   bias=p.sb["qsh"][:, 0:1])
                    scl.drain()
                    ins = scl.copy(p.sb["wib" + tag][:, 0:w],
                                   p.sb["wyb" + tag][:, 0:w])
                    nsq += 1
                    p.inc(ins, S_SQ, 1)

    @block.tensor
    def _(t):
        k = 0
        for i, (src, dst, O, H, mode, bix, ot) in enumerate(tasks):
            t.wait_ge(p.sem(S_RDY), i + 1)
            for (h0, cnt) in _tp_batches(H):
                ps = p.ps["ps_t"] if k % 2 == 0 else p.ps["ps_a"]
                if k >= 2:
                    t.wait_ge(p.sem(S_EV), k - 1)
                for j in range(cnt):
                    ht = h0 + j
                    ins = t.transpose(ps[:, j * 128:(j + 1) * 128],
                                      p.sb["wbb" + tag][:, ht * 128:(ht + 1) * 128],
                                      p.sb["idb"][:])
                p.inc(ins, S_TP, 1)
                k += 1

    @block.gpsimd
    def _(g):
        for i, (src, dst, O, H, mode, bix, ot) in enumerate(tasks):
            g.wait_ge(p.sem(S_EV), ev_upto[i])
            ins = g.dma_start(out=dram[dst][:, ot, :, :].rearrange("h p q -> p h q"),
                              in_=p.sb["wtg" + tag][:, 0:H // 128, :])
            p.inc(ins, S_ST, 16)
        g.wait_ge(p.sem(S_ST), p.val(S_ST))




def phase_l0_attn(p):
    dram, block = p.dram, p.block
    p.sbuf("tri_sb", [128, 4 * 512], F32, local=True)
    p.sbuf("wbuf", [128, 8192], BF16, local=True)
    p.sbuf("kT", [128, T], BF16, local=True)
    p.sbuf("vT", [128, T], BF16, local=True)
    p.sbuf("vh", [128, 16, 128], BF16, local=True)
    p.sbuf("qTh", [128, TOK], BF16, local=True)
    p.sbuf("prb", [128, 4, T], BF16, local=True)
    p.sbuf("atT", [128, 4, 128], BF16, local=True)
    p.sbuf("zT", [128, NH, TOK], BF16, local=True)

    def wslice(i):
        return p.sb["wbuf"][:, i * 2048:(i + 1) * 2048].rearrange(
            "p (t q) -> p t q", t=16)

    def wo_view():
        return p.sb["wbuf"][:, 0:8192].rearrange("p (h o q) -> p h o q", h=16, o=4)

    @block.sync
    def _(sync):
        for qt in range(4):
            ins = sync.dma_start(out=p.sb["tri_sb"][:, qt * 512:(qt + 1) * 512],
                                 in_=dram["tri"][qt * 128:(qt + 1) * 128, :])
            p.inc(ins, "a_tri", 16)
        for h in range(NH):
            if h >= 1:
                sync.wait_ge(p.sem("a_wdone"), h)
            for wi, wname in enumerate(["wqT", "wkT", "wvT"]):
                ins = sync.dma_start(
                    out=wslice(wi),
                    in_=dram[wname][:, h, :, :].rearrange("t p q -> p t q"))
                p.inc(ins, "s_whd", 16)
        sync.wait_ge(p.sem("a_wdone"), NH)
        for oc in range(4):
            if oc >= 1:
                sync.wait_ge(p.sem("a_ao"), oc * 8)
            for j in range(4):
                ins = sync.dma_start(
                    out=wo_view()[:, :, j, :],
                    in_=dram["woT"][:, oc * 4 + j, :, :].rearrange(
                        "h p q -> p h q"))
                p.inc(ins, "s_wo", 16)

    def emit_attn_tensor(t, parity):
        qg = q512_groups(parity)
        for nm in ["a_pj", "a_pje", "a_tp", "a_tpe", "a_sc", "a_sce",
                   "a_zb", "a_ze", "a_prn", "a_nm", "a_pr", "a_ao", "a_aoe",
                   "a_wdone", "s_whd", "s_wo"]:
            p.sem(nm)
        for h in range(NH):
            t.wait_ge(p.sem("s_whd"), 48 * (h + 1))
            for wi, dstbuf in [(1, "kT"), (2, "vT")]:
                for kc in range(4):
                    ps = p.ps["ps_b"] if p.cnt["a_pj"] % 2 == 0 else p.ps["ps_c"]
                    if p.cnt["a_pj"] >= 2:
                        t.wait_ge(p.sem("a_pje"), p.cnt["a_pj"] - 1)
                    for ht in range(ND):
                        ins = t.matmul(ps[:, 0:512], wslice(wi)[:, ht, :],
                                       p.sb["allT"][:, ht, kc * 512:(kc + 1) * 512],
                                       start=(ht == 0), stop=(ht == ND - 1))
                    p.inc(ins, "a_pj", 1)
            for tc in range(2):
                g = qg[tc]
                ps = p.ps["ps_b"] if p.cnt["a_pj"] % 2 == 0 else p.ps["ps_c"]
                if p.cnt["a_pj"] >= 2:
                    t.wait_ge(p.sem("a_pje"), p.cnt["a_pj"] - 1)
                for ht in range(ND):
                    ins = t.matmul(ps[:, 0:512], wslice(0)[:, ht, :],
                                   p.sb["allT"][:, ht, g * 512:(g + 1) * 512],
                                   start=(ht == 0), stop=(ht == ND - 1))
                p.inc(ins, "a_pj", 1)
            t.drain()
            ins = t.nop()
            p.inc(ins, "a_wdone", 1)
            t.wait_ge(p.sem("a_pje"), 10 * h + 8)
            for g4 in range(4):
                ps = p.ps["ps_t"] if p.cnt["a_tp"] % 2 == 0 else p.ps["ps_a"]
                if p.cnt["a_tp"] >= 2:
                    t.wait_ge(p.sem("a_tpe"), p.cnt["a_tp"] - 1)
                for j in range(4):
                    kt = g4 * 4 + j
                    ins = t.transpose(ps[:, j * 128:(j + 1) * 128],
                                      p.sb["vT"][:, kt * 128:(kt + 1) * 128],
                                      p.sb["idb"][:])
                p.inc(ins, "a_tp", 1)
            t.wait_ge(p.sem("a_pje"), 10 * h + 10)
            for gl in range(2):
                g = qg[gl]
                nkc = g + 1
                for qt in range(4):
                    for kc in range(nkc):
                        m = p.cnt["a_sc"]
                        if m >= 1:
                            t.wait_ge(p.sem("a_sce"), m)
                        ins = t.matmul(
                            p.ps["ps_s"][:, 0:512],
                            p.sb["qTh"][:, (gl * 512 + qt * 128):(gl * 512 + qt * 128 + 128)],
                            p.sb["kT"][:, kc * 512:(kc + 1) * 512],
                            start=True, stop=True)
                        p.inc(ins, "a_sc", 1)
                t.wait_ge(p.sem("a_prn"), h * 8 + gl * 4 + 4)
                zgrp = h * 2 + gl
                if zgrp >= 1:
                    t.wait_ge(p.sem("a_ze"), zgrp)
                nkt = nkc * 4
                for kt in range(nkt):
                    ps = p.ps["ps_t"] if p.cnt["a_tp"] % 2 == 0 else p.ps["ps_a"]
                    if p.cnt["a_tp"] >= 2:
                        t.wait_ge(p.sem("a_tpe"), p.cnt["a_tp"] - 1)
                    for qt in range(4):
                        ins = t.transpose(ps[:, qt * 128:(qt + 1) * 128],
                                          p.sb["prb"][:, qt, kt * 128:(kt + 1) * 128],
                                          p.sb["idb"][:])
                    p.inc(ins, "a_tp", 1)
                    t.wait_ge(p.sem("a_tpe"), p.cnt["a_tp"])
                    for qt in range(4):
                        ins = t.matmul(p.ps["ps_z"][:, qt * 128:(qt + 1) * 128],
                                       p.sb["vh"][:, kt, :],
                                       p.sb["atT"][:, qt, :],
                                       start=(kt == 0 and qt == 0),
                                       stop=(kt == nkt - 1 and qt == 3))
                    p.inc(ins, "a_zb", 1)
        for oc in range(4):
            t.wait_ge(p.sem("s_wo"), 64 * (oc + 1))
            for tc8 in range(8):
                ps = p.ps["ps_b"] if p.cnt["a_ao"] % 2 == 0 else p.ps["ps_c"]
                if p.cnt["a_ao"] >= 2:
                    t.wait_ge(p.sem("a_aoe"), p.cnt["a_ao"] - 1)
                for h in range(NH):
                    ins = t.matmul(ps[:, 0:512],
                                   p.sb["zT"][:, h, tc8 * 128:(tc8 + 1) * 128],
                                   wo_view()[:, h, :, :],
                                   start=(h == 0), stop=(h == NH - 1))
                p.inc(ins, "a_ao", 1)

    @block.tensor
    def _(t):
        p.branch_parity(t, "tpar", lambda par: emit_attn_tensor(t, par))

    def emit_attn_scalar(scl, parity):
        qg = q512_groups(parity)
        zb = 0
        tp0 = 0
        att_cum = 0
        for h in range(NH):
            base = 10 * h
            for kc in range(4):
                scl.wait_ge(p.sem("a_pj"), base + kc + 1)
                ps = p.ps["ps_b"] if (base + kc) % 2 == 0 else p.ps["ps_c"]
                ins = scl.copy(p.sb["kT"][:, kc * 512:(kc + 1) * 512], ps[:, 0:512])
                p.inc(ins, "a_pje", 1)
            for kc in range(4):
                scl.wait_ge(p.sem("a_pj"), base + 4 + kc + 1)
                ps = p.ps["ps_b"] if (base + 4 + kc) % 2 == 0 else p.ps["ps_c"]
                ins = scl.copy(p.sb["vT"][:, kc * 512:(kc + 1) * 512], ps[:, 0:512])
                p.inc(ins, "a_pje", 1)
            for tc in range(2):
                scl.wait_ge(p.sem("a_pj"), base + 8 + tc + 1)
                ps = p.ps["ps_b"] if (base + 8 + tc) % 2 == 0 else p.ps["ps_c"]
                ins = scl.copy(p.sb["qTh"][:, tc * 512:(tc + 1) * 512], ps[:, 0:512])
                p.inc(ins, "a_pje", 1)
            for gl in range(2):
                g = qg[gl]
                w = (g + 1) * 512
                for qt in range(4):
                    scl.wait_ge(p.sem("a_nm"), h * 8 + gl * 4 + qt + 1)
                    if att_cum > 0:
                        scl.wait_ge(p.sem("a_tp"), tp0 + att_cum)
                    ins = scl.activation(p.sb["prb"][:, qt, 0:w],
                                         p.sb["sqt"][:, 0:w],
                                         AF.Exp, scale=p.sb["cstt"][:, 0:1],
                                         bias=p.sb["stt"][:, 10:11],
                                         accum_out=p.sb["stt"][:, 11:12])
                    p.inc(ins, "a_pr", 1)
                zb += (g + 1) * 4
                scl.wait_ge(p.sem("a_zb"), zb)
                ins = scl.copy(p.sb["zT"][:, h, gl * 512:(gl + 1) * 512],
                               p.ps["ps_z"][:, 0:512])
                p.inc(ins, "a_ze", 1)
                att_cum = (h * 24 + 4 +
                           sum((qg[x] + 1) * 4 for x in range(gl + 1)))

    @block.scalar
    def _(scl):
        p.branch_parity(scl, "spar", lambda par: emit_attn_scalar(scl, par))

    def emit_attn_vector(v, parity):
        qg = q512_groups(parity)
        p.sem("a_tri")
        v.wait_ge(p.sem("a_tri"), 64)
        for h in range(NH):
            for g4 in range(4):
                v.wait_ge(p.sem("a_tp"), 24 * h + g4 + 1)
                ps = p.ps["ps_t"] if (24 * h + g4) % 2 == 0 else p.ps["ps_a"]
                ins = v.tensor_copy(p.sb["vh"][:, g4 * 4:(g4 + 1) * 4, :],
                                    ps[:, 0:512])
                p.inc(ins, "a_tpe", 1)
            for gl in range(2):
                g = qg[gl]
                nkc = g + 1
                for qt in range(4):
                    iq = h * 8 + gl * 4 + qt
                    if iq >= 1:
                        v.wait_ge(p.sem("a_pr"), iq)
                    for kc in range(nkc):
                        v.wait_ge(p.sem("a_sc"), p.cnt["a_sce"] + 1)
                        ins = v.tensor_copy(p.sb["sqt"][:, kc * 512:(kc + 1) * 512],
                                            p.ps["ps_s"][:, 0:512])
                        p.inc(ins, "a_sce", 1)
                    v.drain()
                    v.tensor_tensor(p.sb["sqt"][:, g * 512:(g + 1) * 512],
                                    p.sb["sqt"][:, g * 512:(g + 1) * 512],
                                    p.sb["tri_sb"][:, qt * 512:(qt + 1) * 512],
                                    ALU.add)
                    v.drain()
                    v.reduce_max(p.sb["stt"][:, 9:10], p.sb["sqt"][:, 0:nkc * 512],
                                 AX.X)
                    v.drain()
                    ins = v.tensor_scalar(p.sb["stt"][:, 10:11],
                                          p.sb["stt"][:, 9:10],
                                          -INV_SQRT_HD, None, ALU.mult)
                    p.inc(ins, "a_nm", 1)
                    v.wait_ge(p.sem("a_pr"), iq + 1)
                    v.reciprocal(p.sb["stt"][:, 9:10], p.sb["stt"][:, 11:12])
                    v.drain()
                    ins = v.tensor_scalar(p.sb["prb"][:, qt, 0:nkc * 512],
                                          p.sb["prb"][:, qt, 0:nkc * 512],
                                          p.sb["stt"][:, 9:10], None, ALU.mult)
                    p.inc(ins, "a_prn", 1)
                nkt = nkc * 4
                for kt in range(nkt):
                    v.wait_ge(p.sem("a_tp"), p.cnt["a_tpe"] + 1)
                    nzb_done = p.cnt.setdefault("_v_zbcum", 0)
                    if nzb_done >= 1:
                        v.wait_ge(p.sem("a_zb"), nzb_done)
                    p.cnt["_v_zbcum"] = nzb_done + 1
                    ps = p.ps["ps_t"] if p.cnt["a_tpe"] % 2 == 0 else p.ps["ps_a"]
                    ins = v.tensor_copy(p.sb["atT"][:, :, :], ps[:, 0:512])
                    p.inc(ins, "a_tpe", 1)
        for oc in range(4):
            for tc8 in range(8):
                k = oc * 8 + tc8
                v.wait_ge(p.sem("a_ao"), k + 1)
                ps = p.ps["ps_b"] if k % 2 == 0 else p.ps["ps_c"]
                ins = v.tensor_tensor(p.sb["res"][:, tc8, oc * 512:(oc + 1) * 512],
                                      p.sb["res"][:, tc8, oc * 512:(oc + 1) * 512],
                                      ps[:, 0:512], ALU.add)
                p.inc(ins, "a_aoe", 1)

    @block.vector
    def _(v):
        p.branch_parity(v, "vpar", lambda par: emit_attn_vector(v, par))




def phase_l0_ffn(p):
    dram, block = p.dram, p.block
    NIC = I0 // 512
    p.sbuf("xnf", [128, D], F32, local=True)
    p.sbuf("xqb", [128, D], BF16, local=True)
    p.sbuf("xqi", [128, D], I32, local=True)
    p.sbuf("xqT", [128, ND, TOK], BF16, local=True)
    p.sbuf("w1g", [128, ND, 512], BF16, local=True)
    p.sbuf("w1u", [128, ND, 512], BF16, local=True)
    p.sbuf("gsb", [128, 512], F32, local=True)
    p.sbuf("hsp", [128, 2, 512], F32, local=True)

    @block.vector
    def _(v):
        for n in range(NT_OWN):
            emit_layernorm_vec(p, v, p.sb["res"][:, n, :], p.sb["xnf"][:], "f0", n)
            st = p.sb["stt"]
            v.drain()
            v.reduce_max(st[:, 9:10], p.sb["xnf"][:], AX.X, apply_absolute_value=True)
            v.drain()
            v.tensor_scalar(st[:, 10:11], st[:, 9:10], 1.0 / 127.0, 1e-8,
                            ALU.mult, ALU.max)
            v.drain()
            v.reciprocal(st[:, 11:12], st[:, 10:11])
            v.drain()
            ins = v.nop()
            p.inc(ins, "f_vq", 1)
            v.wait_ge(p.sem("f_sq"), n + 1)
            if n >= 1:
                v.wait_ge(p.sem("f_tp"), n * 4)
            ins = v.tensor_scalar(p.sb["xqb"][:], p.sb["xqi"][:], 512.0,
                                  st[:, 10:11], ALU.subtract, ALU.mult)
            p.inc(ins, "f_xqb", 1)
            for g4 in range(4):
                k = n * 4 + g4
                v.wait_ge(p.sem("f_tp"), k + 1)
                ps = p.ps["ps_t"] if k % 2 == 0 else p.ps["ps_a"]
                ins = v.tensor_copy(
                    p.sb["xqT"][:, g4 * 4:(g4 + 1) * 4, n * 128:(n + 1) * 128],
                    ps[:, 0:512])
                p.inc(ins, "f_tpe", 1)

    @block.scalar
    def _(s):
        for n in range(NT_OWN):
            emit_layernorm_sca(p, s, p.sb["res"][:, n, :], "f0", n)
            s.wait_ge(p.sem("f_vq"), n + 1)
            s.activation(p.sb["sqt"][:], p.sb["xnf"][:], AF.Identity,
                         scale=p.sb["stt"][:, 11:12], bias=p.sb["qsh"][:, 0:1])
            s.drain()
            ins = s.copy(p.sb["xqi"][:], p.sb["sqt"][:])
            p.inc(ins, "f_sq", 1)

    @block.tensor
    def _(t):
        for n in range(NT_OWN):
            t.wait_ge(p.sem("f_xqb"), n + 1)
            for g4 in range(4):
                k = n * 4 + g4
                ps = p.ps["ps_t"] if k % 2 == 0 else p.ps["ps_a"]
                if k >= 2:
                    t.wait_ge(p.sem("f_tpe"), k - 1)
                for j in range(4):
                    dt = g4 * 4 + j
                    ins = t.transpose(ps[:, j * 128:(j + 1) * 128],
                                      p.sb["xqb"][:, dt * 128:(dt + 1) * 128],
                                      p.sb["idb"][:])
                p.inc(ins, "f_tp", 1)

    @block.sync
    def _(sync):
        for ic in range(NIC):
            if ic >= 1:
                sync.wait_ge(p.sem("f_gu"), ic * NT_OWN)
            for ht in range(ND):
                ins = sync.dma_start(
                    out=p.sb["w1g"][:, ht, :].rearrange("pp (o q) -> pp o q", o=4),
                    in_=dram["w1qT"][ht, ic * 4:(ic + 1) * 4, :, :]
                    .rearrange("o pp q -> pp o q"))
                p.inc(ins, "f_w1", 16)
                ins = sync.dma_start(
                    out=p.sb["w1u"][:, ht, :].rearrange("pp (o q) -> pp o q", o=4),
                    in_=dram["w1qT"][ht, 44 + ic * 4:44 + (ic + 1) * 4, :, :]
                    .rearrange("o pp q -> pp o q"))
                p.inc(ins, "f_w1", 16)

    @block.tensor
    def _(t):
        t.wait_ge(p.sem("f_tpe"), NT_OWN * 4)
        for ic in range(NIC):
            t.wait_ge(p.sem("f_w1"), (ic + 1) * 32 * 16)
            for tt in range(NT_OWN):
                k = ic * NT_OWN + tt
                if k >= 1:
                    t.wait_ge(p.sem("f_si"), k)
                    t.wait_ge(p.sem("f_h"), k)
                for ht in range(ND):
                    ins = t.matmul(p.ps["ps_b"][:, 0:512],
                                   p.sb["xqT"][:, ht, tt * 128:(tt + 1) * 128],
                                   p.sb["w1g"][:, ht, :],
                                   start=(ht == 0), stop=(ht == ND - 1))
                p.inc(ins, "f_gg", 1)
                for ht in range(ND):
                    ins = t.matmul(p.ps["ps_c"][:, 0:512],
                                   p.sb["xqT"][:, ht, tt * 128:(tt + 1) * 128],
                                   p.sb["w1u"][:, ht, :],
                                   start=(ht == 0), stop=(ht == ND - 1))
                p.inc(ins, "f_gu", 1)

    @block.scalar
    def _(s):
        for ic in range(NIC):
            for tt in range(NT_OWN):
                k = ic * NT_OWN + tt
                s.wait_ge(p.sem("f_gg"), k + 1)
                ins = s.activation(p.sb["gsb"][:], p.ps["ps_b"][:, 0:512], AF.Silu)
                p.inc(ins, "f_si", 1)

    @block.vector
    def _(v):
        v.memset(p.sb["am"][:, 0:10], 0.0)
        v.drain()
        for ic in range(NIC):
            for tt in range(NT_OWN):
                k = ic * NT_OWN + tt
                v.wait_ge(p.sem("f_si"), k + 1)
                v.wait_ge(p.sem("f_gu"), k + 1)
                if k >= 2:
                    v.wait_ge(p.sem("f_hsp"), (k - 1) * 16)
                ins = v.tensor_tensor(p.sb["hsp"][:, k % 2, :], p.sb["gsb"][:],
                                      p.ps["ps_c"][:, 0:512], ALU.mult)
                p.inc(ins, "f_h", 1)
                v.drain()
                v.reduce_max(p.sb["stt"][:, 9:10], p.sb["hsp"][:, k % 2, :],
                             AX.X, apply_absolute_value=True)
                v.drain()
                v.tensor_tensor(p.sb["am"][:, tt:tt + 1], p.sb["am"][:, tt:tt + 1],
                                p.sb["stt"][:, 9:10], ALU.max)
                v.drain()
                ins = v.nop()
                p.inc(ins, "f_am", 1)

    @block.gpsimd
    def _(g):
        for ic in range(NIC):
            for tt in range(NT_OWN):
                k = ic * NT_OWN + tt
                g.wait_ge(p.sem("f_h"), k + 1)
                g.wait_ge(p.sem("f_am"), k + 1)
                ins = g.dma_start(out=dram["h_spill"][tt, ic, :, :],
                                  in_=p.sb["hsp"][:, k % 2, :])
                p.inc(ins, "f_hsp", 16)
        g.wait_ge(p.sem("f_hsp"), p.val("f_hsp"))
        for tname in ("xqT", "am"):
            if tname in p.taps:
                ins = g.dma_start(out=dram["tap_" + tname], in_=p.sb[tname][:])
                p.inc(ins, "f_tap", 16)
        if p.val("f_tap"):
            g.wait_ge(p.sem("f_tap"), p.val("f_tap"))



def phase_l0_ffn_b(p):
    dram, block = p.dram, p.block
    NIC = I0 // 512
    p.sbuf("hld", [128, 2, 512], F32, local=True)
    p.sbuf("hyb", [128, 512], F32, local=True)
    p.sbuf("hqb", [128, 512], BF16, local=True)
    p.sbuf("hqi", [128, 512], I32, local=True)
    p.sbuf("hqT", [128, 44, 128], BF16, local=True)
    p.sbuf("w2r", [128, 2, 16, 128], BF16, local=True)
    @block.sync
    def _(sync):
        for tt in range(NT_OWN):
            for ic in range(NIC):
                k = tt * NIC + ic
                if k >= 2:
                    sync.wait_ge(p.sem("f_hq"), k - 1)
                ins = sync.dma_start(out=p.sb["hld"][:, k % 2, :],
                                     in_=dram["h_spill"][tt, ic, :, :])
                p.inc(ins, "f_hl", 16)
            for it in range(44):
                kk = tt * 44 + it
                if kk >= 2:
                    prev2 = kk - 2
                    tt2, it2 = divmod(prev2, 44)
                    if it2 == 43:
                        sync.wait_ge(p.sem("f_dn"), (tt2 + 1) * 4)
                    else:
                        sync.wait_ge(p.sem("f_dnw"), prev2 + 1 - tt2)
                ins = sync.dma_start(out=p.sb["w2r"][:, kk % 2, :, :],
                                     in_=dram["w2qT"][it, :, :, :]
                                     .rearrange("o pp q -> pp o q"))
                p.inc(ins, "f_w2", 16)

    @block.vector
    def _(v):
        st = p.sb["stt"]
        for tt in range(NT_OWN):
            v.tensor_scalar(st[:, 10:11], p.sb["am"][:, tt:tt + 1], 1.0 / 127.0,
                            1e-8, ALU.mult, ALU.max)
            v.drain()
            v.reciprocal(st[:, 11:12], st[:, 10:11])
            v.drain()
            ins = v.nop()
            p.inc(ins, "f_bs", 1)
            for ic in range(NIC):
                k = tt * NIC + ic
                v.wait_ge(p.sem("f_hq"), k + 1)
                if k >= 1:
                    v.wait_ge(p.sem("f_htp"), k)
                ins = v.tensor_scalar(p.sb["hqb"][:], p.sb["hqi"][:], 512.0,
                                      st[:, 10:11], ALU.subtract, ALU.mult)
                p.inc(ins, "f_hqb", 1)
                v.wait_ge(p.sem("f_htp"), k + 1)
                ps = p.ps["ps_t"] if k % 2 == 0 else p.ps["ps_a"]
                ins = v.tensor_copy(p.sb["hqT"][:, ic * 4:(ic + 1) * 4, :],
                                    ps[:, 0:512])
                p.inc(ins, "f_hte", 1)
            for oc in range(4):
                v.wait_ge(p.sem("f_dn"), tt * 4 + oc + 1)
                psd = [p.ps["ps_b"], p.ps["ps_c"], p.ps["ps_s"], p.ps["ps_z"]][oc]
                ins = v.tensor_tensor(p.sb["res"][:, tt, oc * 512:(oc + 1) * 512],
                                      p.sb["res"][:, tt, oc * 512:(oc + 1) * 512],
                                      psd[:, 0:512], ALU.add)
                p.inc(ins, "f_res", 1)

    @block.scalar
    def _(s):
        for tt in range(NT_OWN):
            s.wait_ge(p.sem("f_bs"), tt + 1)
            for ic in range(NIC):
                k = tt * NIC + ic
                s.wait_ge(p.sem("f_hl"), (k + 1) * 16)
                s.activation(p.sb["hyb"][:], p.sb["hld"][:, k % 2, :], AF.Identity,
                             scale=p.sb["stt"][:, 11:12], bias=p.sb["qsh"][:, 0:1])
                s.drain()
                ins = s.copy(p.sb["hqi"][:], p.sb["hyb"][:])
                p.inc(ins, "f_hq", 1)

    @block.tensor
    def _(t):
        for tt in range(NT_OWN):
            for ic in range(NIC):
                k = tt * NIC + ic
                t.wait_ge(p.sem("f_hqb"), k + 1)
                ps = p.ps["ps_t"] if k % 2 == 0 else p.ps["ps_a"]
                if k >= 2:
                    t.wait_ge(p.sem("f_hte"), k - 1)
                for j in range(4):
                    ins = t.transpose(ps[:, j * 128:(j + 1) * 128],
                                      p.sb["hqb"][:, j * 128:(j + 1) * 128],
                                      p.sb["idb"][:])
                p.inc(ins, "f_htp", 1)
            t.wait_ge(p.sem("f_hte"), (tt + 1) * NIC)
            if tt >= 1:
                t.wait_ge(p.sem("f_res"), tt * 4)
            for it in range(44):
                kk = tt * 44 + it
                t.wait_ge(p.sem("f_w2"), (kk + 1) * 16)
                for oc in range(4):
                    psd = [p.ps["ps_b"], p.ps["ps_c"], p.ps["ps_s"], p.ps["ps_z"]][oc]
                    ins = t.matmul(psd[:, 0:512],
                                   p.sb["hqT"][:, it, :],
                                   p.sb["w2r"][:, kk % 2, oc * 4:(oc + 1) * 4, :],
                                   start=(it == 0), stop=(it == 43))
                    if it == 43:
                        p.inc(ins, "f_dn", 1)
                    elif oc == 3:
                        p.inc(ins, "f_dnw", 1)

    @block.gpsimd
    def _(g):
        if "hqT" in p.taps:
            g.wait_ge(p.sem("f_dn"), NT_OWN * 4)
            ins = g.dma_start(out=dram["tap_hqT"], in_=p.sb["hqT"][:])
            p.inc(ins, "f_tap2", 16)
            g.wait_ge(p.sem("f_tap2"), p.val("f_tap2"))



def emit_flag_write(p, g, slot, sem):
    g.memset(p.sb["mgw"][:, 0:1], MAGIC0)
    g.memset(p.sb["mgw"][:, 1:2], MAGIC1)
    g.drain()
    ins = g.dma_start(out=p.dram["sh_flag"][slot:slot + 1, :], in_=p.sb["mgw"][0:1, :])
    p.inc(ins, sem, 16)
    g.wait_ge(p.sem(sem), p.val(sem))


_POLL_ID = [0]


def emit_flag_poll(p, g, slot, sem):
    _POLL_ID[0] += 1
    u = str(_POLL_ID[0])
    with (
        g.register("fl0_" + sem + u) as f0,
        g.register("fl1_" + sem + u) as f1,
        g.register("cc_" + sem + u) as c,
        g.register("wt_" + sem + u) as wt,
    ):
        g.reg_mov(c, 1)
        g.reg_mov(wt, p.val(sem))
        with g.While(c):
            ins = g.dma_start(out=p.sb["mgr"][0:1, :],
                              in_=p.dram["sh_flag"][slot:slot + 1, :])
            ins.then_inc(p.sem(sem), 16)
            g.reg_add(wt, wt, 16)
            g.wait_ge(p.sem(sem), wt)
            g.reg_load(f0, p.sb["mgr"][0:1, 0:1])
            g.reg_load(f1, p.sb["mgr"][0:1, 1:2])
            g.reg_alu(c, f0, MAGIC0, ALU.not_equal)
            g.reg_alu(f1, f1, MAGIC1, ALU.not_equal)
            g.reg_alu(c, c, f1, ALU.bitwise_or)
    p.cnt[sem] = -10 ** 9



def phase_l1_norm(p):
    dram, block = p.dram, p.block
    p.sbuf("xnf1", [128, D], F32, local=True)
    p.sbuf("xnb2", [128, D], BF16, local=True)
    p.sbuf("nwr", [128, D], F32, local=True)
    p.sbuf("nwrow", [1, D], F32, local=True)
    p.sbuf("nwrowb", [1, D], BF16, local=True)
    p.sbuf("onesb", [1, 128], BF16, local=True)

    def gmap(par):
        return [0, 1, 2, 3, 12, 13, 14, 15] if par == 0 else list(range(4, 12))

    @block.sync
    def _(sync):
        ins = sync.dma_start(out=p.sb["nwrow"][0:1, :], in_=dram["norm_w"])
        p.inc(ins, "n_ld", 16)

    @block.vector
    def _(v):
        v.wait_ge(p.sem("n_ld"), 16)
        v.memset(p.sb["onesb"][:], 1.0)
        v.drain()
        ins = v.tensor_copy(p.sb["nwrowb"][0:1, :], p.sb["nwrow"][0:1, :])
        p.inc(ins, "n_nw", 1)

    @block.tensor
    def _(t):
        t.wait_ge(p.sem("n_nw"), 1)
        for ch in range(4):
            ins = t.matmul(p.ps["ps_b"][:, 0:512], p.sb["onesb"][0:1, :],
                           p.sb["nwrowb"][0:1, ch * 512:(ch + 1) * 512],
                           start=True, stop=True)
            p.inc(ins, "n_rep", 1)
            t.wait_ge(p.sem("n_repe"), ch + 1)

    @block.vector
    def _(v):
        for ch in range(4):
            v.wait_ge(p.sem("n_rep"), ch + 1)
            ins = v.tensor_copy(p.sb["nwr"][:, ch * 512:(ch + 1) * 512],
                                p.ps["ps_b"][:, 0:512])
            p.inc(ins, "n_repe", 1)

    @block.scalar
    def _(s):
        st = p.sb["stt"]
        for n in range(NT_OWN):
            emit_layernorm_sca(p, s, p.sb["res"][:, n, :], "l1", n)
            s.wait_ge(p.sem("n_msq"), n + 1)
            ins = s.activation(st[:, 10:11], st[:, 9:10], AF.Sqrt)
            p.inc(ins, "n_sq2", 1)

    def emit_norm_vec(v, par):
        gm = gmap(par)
        st = p.sb["stt"]
        for nm in ["n_tp", "n_ev", "n_msq", "n_sq2", "n_xb"]:
            p.sem(nm)
        for n in range(NT_OWN):
            emit_layernorm_vec(p, v, p.sb["res"][:, n, :], p.sb["xnf1"][:], "l1", n)
            v.drain()
            v.tensor_tensor(st[:, 9:10], st[:, 8:9], st[:, 8:9], ALU.mult)
            v.drain()
            v.tensor_tensor(st[:, 9:10], st[:, 9:10], st[:, 5:6], ALU.mult)
            v.drain()
            ins = v.tensor_scalar(st[:, 9:10], st[:, 9:10], 1e-5, None, ALU.add)
            p.inc(ins, "n_msq", 1)
            v.wait_ge(p.sem("n_sq2"), n + 1)
            v.reciprocal(st[:, 11:12], st[:, 10:11])
            v.drain()
            v.tensor_scalar(p.sb["xnf1"][:], p.sb["xnf1"][:], st[:, 11:12], None,
                            ALU.mult)
            v.drain()
            ins = v.tensor_tensor(p.sb["xnb2"][:], p.sb["xnf1"][:], p.sb["nwr"][:],
                                  ALU.mult)
            p.inc(ins, "n_xb", 1)
            gt = gm[n]
            for g4 in range(4):
                k = n * 4 + g4
                v.wait_ge(p.sem("n_tp"), k + 1)
                ps = p.ps["ps_t"] if k % 2 == 0 else p.ps["ps_a"]
                dst = p.sb["allT"][:, g4 * 4:(g4 + 1) * 4, gt * 128:(gt + 1) * 128]
                ins = v.tensor_copy(dst, ps[:, 0:512])
                p.inc(ins, "n_ev", 1)

    @block.tensor
    def _(t):
        for n in range(NT_OWN):
            t.wait_ge(p.sem("n_xb"), n + 1)
            for g4 in range(4):
                k = n * 4 + g4
                ps = p.ps["ps_t"] if k % 2 == 0 else p.ps["ps_a"]
                if k >= 2:
                    t.wait_ge(p.sem("n_ev"), k - 1)
                for j in range(4):
                    dt = g4 * 4 + j
                    ins = t.transpose(ps[:, j * 128:(j + 1) * 128],
                                      p.sb["xnb2"][:, dt * 128:(dt + 1) * 128],
                                      p.sb["idb"][:])
                p.inc(ins, "n_tp", 1)

    @block.vector
    def _(v):
        p.branch_parity(v, "nvpar", lambda par: emit_norm_vec(v, par))

    def emit_exch(g, par):
        p.sem("n_ev")
        rngs = own_rows(par)
        prngs = own_rows(1 - par)
        g.wait_ge(p.sem("n_ev"), NT_OWN * 4)
        nd = 0
        for (r0, r1) in rngs:
            nblk = (r1 - r0) // 512
            for bq in range(nblk):
                c0 = r0 + bq * 512
                ins = g.dma_start(
                    out=dram["sh_xn2T"][:, :, c0:c0 + 512].rearrange(
                        "d p c -> p d c"),
                    in_=p.sb["allT"][:, :, c0:c0 + 512])
                p.inc(ins, "x_wr", 16)
                nd += 1
        assert nd == 2
        g.wait_ge(p.sem("x_wr"), p.val("x_wr"))
        emit_flag_write(p, g, 0 + par, "x_fw")
        emit_flag_poll(p, g, 0 + (1 - par), "x_fp")
        for (r0, r1) in prngs:
            nblk = (r1 - r0) // 512
            for bq in range(nblk):
                c0 = r0 + bq * 512
                ins = g.dma_start(
                    out=p.sb["allT"][:, :, c0:c0 + 512],
                    in_=dram["sh_xn2T"][:, :, c0:c0 + 512].rearrange(
                        "d p c -> p d c"))
                p.inc(ins, "x_rd", 16)
        g.wait_ge(p.sem("x_rd"), p.val("x_rd"))

    @block.gpsimd
    def _(g):
        p.branch_parity(g, "xgpar", lambda par: emit_exch(g, par))



def phase_l1_skipu(p):
    dram, block = p.dram, p.block
    p.sbuf("ud1a", [128, D], F32, local="group")
    p.sbuf("bct", [128, 2, D], F32, local="group")
    p.sbuf("wcol", [128, 2, 16, 128], BF16, local=True)
    p.sbuf("wsml", [128, 2, 3, 128], BF16, local=True)
    p.sbuf("upj", [128, D], F32, local=True)
    p.sbuf("ucv", [128, D], F32, local=True)
    p.sbuf("uet", [128, 2, D], BF16, local=True)
    p.sbuf("skp", [128, 2, D], BF16, local=True)
    p.sbuf("cwsb", [128, E // 128, DC], F32, local=True)
    p.sbuf("cbsb", [128, E // 128], F32, local=True)

    @block.sync
    def _(sync):
        for nm, dst in [("conv_w", "cwsb"), ("conv_b", "cbsb"),
                        ("A_own", "Asb"), ("W_D_own", "wdsb"), ("bd2_own", "bdsb")]:
            ins = sync.dma_start(out=p.sb[dst][:], in_=dram[nm])
            p.inc(ins, "k_cst", 16)
        for et in range(16):
            if et >= 2:
                sync.wait_ge(p.sem("k_skpj"), (et - 1) * 4)
            ins = sync.dma_start(
                out=p.sb["wcol"][:, et % 2, :, :],
                in_=dram["wskT"][:, et, :, :].rearrange("h pp q -> pp h q"))
            p.inc(ins, "k_wld", 16)
        for et in range(32):
            if et < 2:
                sync.wait_ge(p.sem("k_skpj"), 64)
            else:
                sync.wait_ge(p.sem("k_upj"), (et - 1) * 4)
                sync.wait_ge(p.sem("k_smp"), (et - 1) * 12)
            ins = sync.dma_start(
                out=p.sb["wcol"][:, et % 2, :, :],
                in_=dram["winT"][:, et, :, :].rearrange("h pp q -> pp h q"))
            p.inc(ins, "k_wld", 16)
            for wi, wname in enumerate(["wd1T", "wbT", "wcT"]):
                ins = sync.dma_start(out=p.sb["wsml"][:, et % 2, wi, :],
                                     in_=dram[wname][et, 0, :, :])
                p.inc(ins, "k_wld", 16)

    @block.scalar
    def _(s):
        s.wait_ge(p.sem("k_cst"), 80)
        ins = s.activation(p.sb["Asb"][:], p.sb["Asb"][:], AF.Exp)
        p.inc(ins, "k_aexp", 1)

    @block.vector
    def _(v):
        v.wait_ge(p.sem("k_aexp"), 1)
        ins = v.tensor_scalar(p.sb["Asb"][:], p.sb["Asb"][:], -1.0, None, ALU.mult)
        p.inc(ins, "k_aneg", 1)
        v.memset(p.sb["ud1a"][:], 0.0)
        ins = v.memset(p.sb["bct"][:], 0.0)
        p.inc(ins, "k_init", 1)

    @block.tensor
    def _(t):
        for et in range(16):
            t.wait_ge(p.sem("k_wld"), (et + 1) * 16)
            for ch in range(4):
                k = et * 4 + ch
                ps = p.ps["ps_b"] if k % 2 == 0 else p.ps["ps_c"]
                if k >= 2:
                    t.wait_ge(p.sem("k_ske"), k - 1)
                for ht in range(ND):
                    ins = t.matmul(ps[:, 0:512], p.sb["wcol"][:, et % 2, ht, :],
                                   p.sb["allT"][:, ht, ch * 512:(ch + 1) * 512],
                                   start=(ht == 0), stop=(ht == ND - 1))
                p.inc(ins, "k_skpj", 1)

    @block.scalar
    def _(s):
        for et in range(16):
            for ch in range(4):
                k = et * 4 + ch
                s.wait_ge(p.sem("k_skpj"), k + 1)
                if et >= 2 and ch == 0:
                    s.wait_ge(p.sem("k_sksp"), (et - 1) * 16)
                ps = p.ps["ps_b"] if k % 2 == 0 else p.ps["ps_c"]
                ins = s.activation(p.sb["skp"][:, et % 2, ch * 512:(ch + 1) * 512],
                                   ps[:, 0:512], AF.Silu)
                p.inc(ins, "k_ske", 1)

    @block.gpsimd
    def _(g):
        for et in range(16):
            g.wait_ge(p.sem("k_ske"), (et + 1) * 4)
            ins = g.dma_start(out=dram["skipT_spill"][et, :, :],
                              in_=p.sb["skp"][:, et % 2, :])
            p.inc(ins, "k_sksp", 16)

    @block.tensor
    def _(t):
        for et in range(32):
            t.wait_ge(p.sem("k_wld"), 256 + (et + 1) * 64)
            for ch in range(4):
                k = et * 4 + ch
                ps = p.ps["ps_b"] if k % 2 == 0 else p.ps["ps_c"]
                if k >= 2:
                    t.wait_ge(p.sem("k_ue"), k - 1)
                for ht in range(ND):
                    ins = t.matmul(ps[:, 0:512], p.sb["wcol"][:, et % 2, ht, :],
                                   p.sb["allT"][:, ht, ch * 512:(ch + 1) * 512],
                                   start=(ht == 0), stop=(ht == ND - 1))
                p.inc(ins, "k_upj", 1)
            t.wait_ge(p.sem("k_usil"), et + 1)
            for wi in range(3):
                for ch in range(4):
                    kk = (et * 3 + wi) * 4 + ch
                    if kk >= 1:
                        t.wait_ge(p.sem("k_sme"), kk)
                    ins = t.matmul(p.ps["ps_s"][:, 0:512],
                                   p.sb["wsml"][:, et % 2, wi, :],
                                   p.sb["uet"][:, et % 2, ch * 512:(ch + 1) * 512],
                                   start=True, stop=True)
                    p.inc(ins, "k_smp", 1)

    @block.vector
    def _(v):
        for et in range(32):
            for ch in range(4):
                k = et * 4 + ch
                v.wait_ge(p.sem("k_upj"), k + 1)
                ins = v.tensor_copy(p.sb["upj"][:, ch * 512:(ch + 1) * 512],
                                    p.ps["ps_b" if k % 2 == 0 else "ps_c"][:, 0:512])
                p.inc(ins, "k_ue", 1)
            v.drain()
            v.tensor_scalar(p.sb["ucv"][:], p.sb["upj"][:],
                            p.sb["cwsb"][:, et, 3:4], None, ALU.mult)
            v.drain()
            for kk in range(3):
                sh = 3 - kk
                v.scalar_tensor_tensor(p.sb["ucv"][:, sh:D],
                                       p.sb["upj"][:, 0:D - sh],
                                       p.sb["cwsb"][:, et, kk:kk + 1],
                                       p.sb["ucv"][:, sh:D], ALU.mult, ALU.add)
                v.drain()
            ins = v.nop()
            p.inc(ins, "k_ucv", 1)
            for wi, dst in [(0, None), (1, 0), (2, 1)]:
                for ch in range(4):
                    kk = (et * 3 + wi) * 4 + ch
                    v.wait_ge(p.sem("k_smp"), kk + 1)
                    if wi == 0:
                        dd = p.sb["ud1a"][:, ch * 512:(ch + 1) * 512]
                    else:
                        dd = p.sb["bct"][:, dst, ch * 512:(ch + 1) * 512]
                    ins = v.tensor_tensor(dd, dd, p.ps["ps_s"][:, 0:512], ALU.add)
                    p.inc(ins, "k_sme", 1)

    @block.scalar
    def _(s):
        for et in range(32):
            s.wait_ge(p.sem("k_ucv"), et + 1)
            if et >= 2:
                s.wait_ge(p.sem("k_usp"), (et - 1) * 16)
                s.wait_ge(p.sem("k_smp"), (et - 1) * 12)
            ins = s.activation(p.sb["uet"][:, et % 2, :], p.sb["ucv"][:], AF.Silu,
                               bias=p.sb["cbsb"][:, et:et + 1])
            p.inc(ins, "k_usil", 1)

    @block.gpsimd
    def _(g):
        for et in range(32):
            g.wait_ge(p.sem("k_usil"), et + 1)
            ins = g.dma_start(out=dram["uT_spill"][et, :, :],
                              in_=p.sb["uet"][:, et % 2, :])
            p.inc(ins, "k_usp", 16)
        g.wait_ge(p.sem("k_usp"), p.val("k_usp"))
        g.wait_ge(p.sem("k_sksp"), p.val("k_sksp"))



def phase_l1_scan(p):
    dram, block = p.dram, p.block
    TH = T // 2
    p.sbuf("ud1b", [128, D], BF16, local=True)
    p.sbuf("bctb", [128, 2, D], BF16, local=True)
    p.sbuf("wd2s", [128, 16, 128], BF16, local=True)
    p.sbuf("dlt", [128, TH], F32, local=True)
    p.sbuf("sp1", [128, TH], F32, local=True)
    p.sbuf("sp2", [128, TH], F32, local=True)
    p.sbuf("due", [128, TH], BF16, local=True)
    p.sbuf("uets", [128, 2, TH], BF16, local=True)
    p.sbuf("dcy", [128, TH], BF16, local=True)
    p.sbuf("Xb", [128, 2, TH], BF16, local=True)
    p.sbuf("XbC", [128, 2, TH], BF16, local=True)
    p.sbuf("hbf", [128, 2, TH], BF16, local=True)
    p.sbuf("yac", [128, TH], F32, local=True)
    p.sbuf("skl", [128, 2, TH], BF16, local=True)
    p.sbuf("y2b", [128, 2, TH], BF16, local=True)
    p.sbuf("hst", [128, 256], F32, local=True)
    p.sbuf("rp16", [16, 16, 128], BF16, local=True)

    @block.sync
    def _(sync):
        ins = sync.dma_start(out=p.sb["rp16"][:], in_=dram["rep16"])
        p.inc(ins, "c_r16", 16)
        for et in range(16):
            ins = sync.dma_start(out=p.sb["wd2s"][:, et, :],
                                 in_=dram["wd2T"][0, et, :, :])
            p.inc(ins, "c_wd2", 16)
        pass

    def emit_scan_uld(sync, par):
        for nm in ["c_y2", "c_uld"]:
            p.sem(nm)
        for th in range(2):
            for et in range(16):
                k = th * 16 + et
                etg = par * 16 + et
                if k >= 2:
                    sync.wait_ge(p.sem("c_y2"), k - 1)
                ins = sync.dma_start(out=p.sb["uets"][:, k % 2, :],
                                     in_=dram["uT_spill"][etg, :,
                                                          th * TH:(th + 1) * TH])
                p.inc(ins, "c_uld", 16)
                ins = sync.dma_start(out=p.sb["skl"][:, k % 2, :],
                                     in_=dram["skipT_spill"][et, :,
                                                             th * TH:(th + 1) * TH])
                p.inc(ins, "c_uld", 16)

    @block.sync
    def _(sync):
        p.branch_parity(sync, "cspar", lambda par: emit_scan_uld(sync, par))

    @block.vector
    def _(v):
        v.tensor_copy(p.sb["ud1b"][:], p.sb["ud1a"][:])
        v.tensor_copy(p.sb["bctb"][:], p.sb["bct"][:])
        v.memset(p.sb["hst"][:], 0.0)
        v.drain()
        ins = v.nop()
        p.inc(ins, "c_prep", 1)

    @block.tensor
    def _(t):
        t.wait_ge(p.sem("c_prep"), 1)
        t.wait_ge(p.sem("c_r16"), 16)
        nrep = 0
        for th in range(2):
            t.wait_ge(p.sem("c_half"), th)
            for bc in range(2):
                for n in range(NST):
                    for ch in range(2):
                        nrep += 1
                        if nrep >= 3:
                            t.wait_ge(p.sem("c_repe"), nrep - 2)
                        ins = t.matmul(
                            p.ps["ps_s" if nrep % 2 else "ps_z"][:, 0:512],
                            p.sb["rp16"][:, n, :],
                            p.sb["bctb"][0:16, bc,
                                         th * TH + ch * 512:th * TH + (ch + 1) * 512],
                            start=True, stop=True)
                        p.inc(ins, "c_rep", 1)
            t.wait_ge(p.sem("c_wd2"), 256)
            for et in range(16):
                k = th * 16 + et
                for ch in range(2):
                    kk = k * 2 + ch
                    ps = p.ps["ps_b"] if kk % 2 == 0 else p.ps["ps_c"]
                    if kk >= 2:
                        if kk % 2 == 0:
                            t.wait_ge(p.sem("c_dle"), kk // 2)
                        else:
                            t.wait_ge(p.sem("c_sp"), (kk - 1) // 2)
                    ins = t.matmul(ps[:, 0:512], p.sb["wd2s"][:, et, :],
                                   p.sb["ud1b"][:, th * TH + ch * 512:
                                                th * TH + (ch + 1) * 512],
                                   start=True, stop=True)
                    p.inc(ins, "c_dlm", 1)

    @block.vector
    def _(v):
        nrep = 0
        for th in range(2):
            for bc in range(2):
                for n in range(NST):
                    for ch in range(2):
                        nrep += 1
                        v.wait_ge(p.sem("c_rep"), nrep)
                        dst = p.sb["allT"][:, n,
                                           bc * 1024 + ch * 512:bc * 1024 + (ch + 1) * 512]
                        ins = v.tensor_copy(
                            dst, p.ps["ps_s" if nrep % 2 else "ps_z"][:, 0:512])
                        p.inc(ins, "c_repe", 1)
            for et in range(16):
                k = th * 16 + et
                v.wait_ge(p.sem("c_sp"), k + 1)
                v.tensor_tensor(p.sb["dlt"][:], p.sb["sp1"][:], p.sb["sp2"][:],
                                ALU.add)
                v.memset(p.sb["yac"][:], 0.0)
                v.wait_ge(p.sem("c_uld"), (k + 1) * 32)
                ins = v.tensor_tensor(p.sb["due"][:], p.sb["dlt"][:],
                                      p.sb["uets"][:, k % 2, :], ALU.mult)
                p.inc(ins, "c_due", 1)
                v.nop()
                v.tensor_tensor(p.sb["Xb"][:, 0, :], p.sb["due"][:],
                                p.sb["allT"][:, 0, 0:TH], ALU.mult)
                v.nop()
                for n in range(NST):
                    idx = (k * NST + n)
                    v.wait_ge(p.sem("c_dcy"), idx + 1)
                    ins = v.tensor_tensor_scan(
                        p.sb["hbf"][:, n % 2, :], p.sb["dcy"][:],
                        p.sb["Xb"][:, n % 2, :],
                        p.sb["hst"][:, et * 16 + n:et * 16 + n + 1],
                        ALU.mult, ALU.add)
                    p.inc(ins, "c_n", 1)
                    if n > 0:
                        v.tensor_tensor(p.sb["yac"][:], p.sb["yac"][:],
                                        p.sb["XbC"][:, (n - 1) % 2, :], ALU.add)
                    if n < NST - 1:
                        v.tensor_tensor(p.sb["Xb"][:, (n + 1) % 2, :],
                                        p.sb["due"][:],
                                        p.sb["allT"][:, n + 1, 0:TH], ALU.mult)
                    v.tensor_tensor(p.sb["XbC"][:, n % 2, :],
                                    p.sb["hbf"][:, n % 2, :],
                                    p.sb["allT"][:, n, 1024:1024 + TH], ALU.mult)
                    v.tensor_copy(p.sb["hst"][:, et * 16 + n:et * 16 + n + 1],
                                  p.sb["hbf"][:, n % 2, TH - 1:TH])
                v.nop()
                v.tensor_tensor(p.sb["yac"][:], p.sb["yac"][:],
                                p.sb["XbC"][:, (NST - 1) % 2, :], ALU.add)
                v.nop()
                v.scalar_tensor_tensor(p.sb["yac"][:], p.sb["uets"][:, k % 2, :],
                                       p.sb["wdsb"][:, et:et + 1], p.sb["yac"][:],
                                       ALU.mult, ALU.add)
                if k >= 2:
                    v.wait_ge(p.sem("c_y2sp"), (k - 1) * 16)
                v.nop()
                ins = v.tensor_tensor(p.sb["y2b"][:, k % 2, :], p.sb["yac"][:],
                                      p.sb["skl"][:, k % 2, :], ALU.mult)
                p.inc(ins, "c_y2", 1)

    @block.scalar
    def _(s):
        for th in range(2):
            for et in range(16):
                k = th * 16 + et
                for ch in range(2):
                    kk = k * 2 + ch
                    s.wait_ge(p.sem("c_dlm"), kk + 1)
                    if ch == 0 and k > 0:
                        s.wait_ge(p.sem("c_due"), k)
                    ps = p.ps["ps_b"] if kk % 2 == 0 else p.ps["ps_c"]
                    sl = slice(ch * 512, (ch + 1) * 512)
                    s.activation(p.sb["dlt"][:, sl], ps[:, 0:512], AF.Identity,
                                 bias=p.sb["bdsb"][:, et:et + 1])
                    s.drain()
                    s.activation(p.sb["sp1"][:, sl], p.sb["dlt"][:, sl], AF.Relu)
                    s.activation(p.sb["sp2"][:, sl], p.sb["dlt"][:, sl], AF.Abs)
                    s.drain()
                    s.activation(p.sb["sp2"][:, sl], p.sb["sp2"][:, sl], AF.Exp,
                                 scale=p.sb["cstt"][:, 2:3])
                    s.drain()
                    ins = s.activation(p.sb["sp2"][:, sl], p.sb["sp2"][:, sl],
                                       AF.Ln, bias=p.sb["cstt"][:, 3:4])
                    if ch == 1:
                        p.inc(ins, "c_sp", 1)
                    else:
                        p.inc(ins, "c_dle", 1)
                for n in range(NST):
                    idx = (k * NST + n)
                    s.wait_ge(p.sem("c_due"), k + 1)
                    if idx >= 1:
                        s.wait_ge(p.sem("c_n"), idx)
                    ins = s.activation(p.sb["dcy"][:], p.sb["dlt"][:], AF.Exp,
                                       scale=p.sb["Asb"][:, et, n:n + 1])
                    p.inc(ins, "c_dcy", 1)

    if "bctchk" in p.taps:
        @block.gpsimd
        def _(g):
            g.wait_ge(p.sem("c_prep"), 1)
            ins = g.dma_start(out=dram["tap_bctchk"], in_=p.sb["bctb"][:])
            p.inc(ins, "g_tap", 16)
            ins = g.dma_start(out=dram["tap_ud1chk"], in_=p.sb["ud1b"][:])
            p.inc(ins, "g_tap", 16)
            g.wait_ge(p.sem("g_tap"), p.val("g_tap"))

    @block.gpsimd
    def _(g):
        for th in range(2):
            for et in range(16):
                k = th * 16 + et
                g.wait_ge(p.sem("c_y2"), k + 1)
                ins = g.dma_start(out=dram["y2_spill"][et, :, th * TH:(th + 1) * TH],
                                  in_=p.sb["y2b"][:, k % 2, :])
                p.inc(ins, "c_y2sp", 16)
            ins = g.nop()
            p.inc(ins, "c_half", 1)
        g.wait_ge(p.sem("c_y2sp"), p.val("c_y2sp"))



def phase_l1_ffn_a(p):
    dram, block = p.dram, p.block
    p.sbuf("xf", [128, D], F32, local=True)
    p.sbuf("xg", [128, D], F32, local=True)
    p.sbuf("xqb2", [128, D], BF16, local=True)
    p.sbuf("xqi2", [128, D], I32, local=True)
    p.sbuf("xfT", [128, ND, TOK], BF16, local=True)
    p.sbuf("wgc", [128, ND, 512], BF16, local=True)
    p.sbuf("wuc", [128, ND, 512], BF16, local=True)
    p.sbuf("gs2", [128, 512], F32, local=True)
    p.sbuf("hs2", [128, 2, 512], BF16, local=True)

    @block.vector
    def _(v):
        st = p.sb["stt"]
        for n in range(NT_OWN):
            emit_layernorm_vec(p, v, p.sb["res"][:, n, :], p.sb["xf"][:], "q1", n)
            v.drain()
            src, dst = "xf", "xg"
            hh = 1
            while hh < D:
                sv = p.sb[src].rearrange("p (a two b) -> p a two b", two=2, b=hh)
                dv = p.sb[dst].rearrange("p (a two b) -> p a two b", two=2, b=hh)
                v.tensor_tensor(dv[:, :, 0, :], sv[:, :, 0, :], sv[:, :, 1, :],
                                ALU.add)
                v.tensor_tensor(dv[:, :, 1, :], sv[:, :, 0, :], sv[:, :, 1, :],
                                ALU.subtract)
                v.drain()
                src, dst = dst, src
                hh *= 2
            fw = p.sb[src]
            v.tensor_scalar(fw[:], fw[:], float(1.0 / np.sqrt(D)), None, ALU.mult)
            v.drain()
            v.reduce_max(st[:, 9:10], fw[:], AX.X, apply_absolute_value=True)
            v.drain()
            v.tensor_scalar(st[:, 9:10], st[:, 9:10], 1e-5, None, ALU.max)
            v.drain()
            v.reciprocal(st[:, 10:11], st[:, 9:10])
            v.drain()
            v.tensor_scalar(st[:, 10:11], st[:, 10:11], 127.0, None, ALU.mult)
            v.tensor_scalar(st[:, 11:12], st[:, 9:10], 1.0 / 127.0, None, ALU.mult)
            v.drain()
            ins = v.nop()
            p.inc(ins, "q_vq", 1)
            v.wait_ge(p.sem("q_sq"), n + 1)
            if n >= 1:
                v.wait_ge(p.sem("q_tp"), n * 4)
            ins = v.tensor_scalar(p.sb["xqb2"][:], p.sb["xqi2"][:], 512.0,
                                  st[:, 11:12], ALU.subtract, ALU.mult)
            p.inc(ins, "q_xqb", 1)
            for g4 in range(4):
                k = n * 4 + g4
                v.wait_ge(p.sem("q_tp"), k + 1)
                ps = p.ps["ps_t"] if k % 2 == 0 else p.ps["ps_a"]
                ins = v.tensor_copy(
                    p.sb["xfT"][:, g4 * 4:(g4 + 1) * 4, n * 128:(n + 1) * 128],
                    ps[:, 0:512])
                p.inc(ins, "q_tpe", 1)

    @block.scalar
    def _(scl):
        for n in range(NT_OWN):
            emit_layernorm_sca(p, scl, p.sb["res"][:, n, :], "q1", n)
            scl.wait_ge(p.sem("q_vq"), n + 1)
            src = "xg"
            scl.activation(p.sb["sqt"][:], p.sb[src][:], AF.Identity,
                           scale=p.sb["stt"][:, 10:11], bias=p.sb["qsh"][:, 0:1])
            scl.drain()
            ins = scl.copy(p.sb["xqi2"][:], p.sb["sqt"][:])
            p.inc(ins, "q_sq", 1)

    @block.tensor
    def _(t):
        for n in range(NT_OWN):
            t.wait_ge(p.sem("q_xqb"), n + 1)
            for g4 in range(4):
                k = n * 4 + g4
                ps = p.ps["ps_t"] if k % 2 == 0 else p.ps["ps_a"]
                if k >= 2:
                    t.wait_ge(p.sem("q_tpe"), k - 1)
                for j in range(4):
                    dt = g4 * 4 + j
                    ins = t.transpose(ps[:, j * 128:(j + 1) * 128],
                                      p.sb["xqb2"][:, dt * 128:(dt + 1) * 128],
                                      p.sb["idb"][:])
                p.inc(ins, "q_tp", 1)

    NOC = I1 // 512

    @block.sync
    def _(sync):
        for oc in range(NOC):
            if oc >= 1:
                sync.wait_ge(p.sem("q_gu"), oc * NT_OWN)
            for ht in range(ND):
                ins = sync.dma_start(
                    out=p.sb["wgc"][:, ht, :].rearrange("pp (o q) -> pp o q", o=4),
                    in_=dram["wg1T"][ht, oc * 4:(oc + 1) * 4, :, :]
                    .rearrange("o pp q -> pp o q"))
                p.inc(ins, "q_w1", 16)
                ins = sync.dma_start(
                    out=p.sb["wuc"][:, ht, :].rearrange("pp (o q) -> pp o q", o=4),
                    in_=dram["wu1T"][ht, oc * 4:(oc + 1) * 4, :, :]
                    .rearrange("o pp q -> pp o q"))
                p.inc(ins, "q_w1", 16)

    @block.tensor
    def _(t):
        t.wait_ge(p.sem("q_tpe"), NT_OWN * 4)
        for oc in range(NOC):
            t.wait_ge(p.sem("q_w1"), (oc + 1) * 32 * 16)
            for tt in range(NT_OWN):
                k = oc * NT_OWN + tt
                if k >= 1:
                    t.wait_ge(p.sem("q_si"), k)
                    t.wait_ge(p.sem("q_h"), k)
                for ht in range(ND):
                    ins = t.matmul(p.ps["ps_b"][:, 0:512],
                                   p.sb["xfT"][:, ht, tt * 128:(tt + 1) * 128],
                                   p.sb["wgc"][:, ht, :],
                                   start=(ht == 0), stop=(ht == ND - 1))
                p.inc(ins, "q_gg", 1)
                for ht in range(ND):
                    ins = t.matmul(p.ps["ps_c"][:, 0:512],
                                   p.sb["xfT"][:, ht, tt * 128:(tt + 1) * 128],
                                   p.sb["wuc"][:, ht, :],
                                   start=(ht == 0), stop=(ht == ND - 1))
                p.inc(ins, "q_gu", 1)

    @block.scalar
    def _(scl):
        for oc in range(NOC):
            for tt in range(NT_OWN):
                k = oc * NT_OWN + tt
                scl.wait_ge(p.sem("q_gg"), k + 1)
                ins = scl.activation(p.sb["gs2"][:], p.ps["ps_b"][:, 0:512], AF.Silu)
                p.inc(ins, "q_si", 1)

    @block.vector
    def _(v):
        for oc in range(NOC):
            for tt in range(NT_OWN):
                k = oc * NT_OWN + tt
                v.wait_ge(p.sem("q_si"), k + 1)
                v.wait_ge(p.sem("q_gu"), k + 1)
                if k >= 2:
                    v.wait_ge(p.sem("q_hsp"), (k - 1) * 16)
                ins = v.tensor_tensor(p.sb["hs2"][:, k % 2, :], p.sb["gs2"][:],
                                      p.ps["ps_c"][:, 0:512], ALU.mult)
                p.inc(ins, "q_h", 1)

    @block.gpsimd
    def _(g):
        for oc in range(NOC):
            for tt in range(NT_OWN):
                k = oc * NT_OWN + tt
                g.wait_ge(p.sem("q_h"), k + 1)
                ins = g.dma_start(out=dram["h1_spill"][tt, oc, :, :],
                                  in_=p.sb["hs2"][:, k % 2, :])
                p.inc(ins, "q_hsp", 16)
        g.wait_ge(p.sem("q_hsp"), p.val("q_hsp"))


def phase_l1_ffn_b(p):
    dram, block = p.dram, p.block
    p.sbuf("hf", [128, I1], BF16, local=True)
    p.sbuf("hg", [128, I1], BF16, local=True)
    p.sbuf("hqb2", [128, I1], BF16, local=True)
    p.sbuf("hqi2", [128, 2048], I32, local=True)
    p.sbuf("hfT", [128, 64, 128], BF16, local=True)
    p.sbuf("wdr", [128, 2, 16, 128], BF16, local=True)
    NOC = I1 // 512

    @block.sync
    def _(sync):
        for tt in range(NT_OWN):
            for oc in range(NOC):
                k = tt * NOC + oc
                if tt >= 1:
                    sync.wait_ge(p.sem("q_hl2"), 4 * tt)
                ins = sync.dma_start(out=p.sb["hf"][:, oc * 512:(oc + 1) * 512],
                                     in_=dram["h1_spill"][tt, oc, :, :])
                p.inc(ins, "q_hl", 16)
            for it in range(64):
                kk = tt * 64 + it
                if kk >= 2:
                    prev2 = kk - 2
                    tt2, it2 = divmod(prev2, 64)
                    if it2 == 63:
                        sync.wait_ge(p.sem("q_dn"), (tt2 + 1) * 4)
                    else:
                        sync.wait_ge(p.sem("q_dnw"), prev2 + 1 - tt2)
                ins = sync.dma_start(out=p.sb["wdr"][:, kk % 2, :, :],
                                     in_=dram["wdn1T"][it, :, :, :]
                                     .rearrange("o pp q -> pp o q"))
                p.inc(ins, "q_w2", 16)

    @block.vector
    def _(v):
        st = p.sb["stt"]
        for tt in range(NT_OWN):
            v.wait_ge(p.sem("q_hl"), (tt + 1) * NOC * 16)
            src, dst = "hf", "hg"
            hh = 1
            while hh < I1:
                sv = p.sb[src].rearrange("p (a two b) -> p a two b", two=2, b=hh)
                dv = p.sb[dst].rearrange("p (a two b) -> p a two b", two=2, b=hh)
                v.tensor_tensor(dv[:, :, 0, :], sv[:, :, 0, :], sv[:, :, 1, :],
                                ALU.add)
                v.tensor_tensor(dv[:, :, 1, :], sv[:, :, 0, :], sv[:, :, 1, :],
                                ALU.subtract)
                v.drain()
                src, dst = dst, src
                hh *= 2
            fw = p.sb[src]
            v.tensor_scalar(fw[:], fw[:], float(1.0 / np.sqrt(I1)), None, ALU.mult)
            v.drain()
            v.reduce_max(st[:, 9:10], fw[:], AX.X, apply_absolute_value=True)
            v.drain()
            v.tensor_scalar(st[:, 9:10], st[:, 9:10], 1e-5, None, ALU.max)
            v.drain()
            v.reciprocal(st[:, 10:11], st[:, 9:10])
            v.drain()
            v.tensor_scalar(st[:, 10:11], st[:, 10:11], 127.0, None, ALU.mult)
            v.tensor_scalar(st[:, 11:12], st[:, 9:10], 1.0 / 127.0, None, ALU.mult)
            v.drain()
            ins = v.nop()
            p.inc(ins, "q_vq2", 1)
            for sec in range(4):
                ks = tt * 4 + sec
                v.wait_ge(p.sem("q_sq2"), ks + 1)
                ins = v.tensor_scalar(
                    p.sb["hqb2"][:, sec * 2048:(sec + 1) * 2048],
                    p.sb["hqi2"][:], 512.0, st[:, 11:12], ALU.subtract, ALU.mult)
                p.inc(ins, "q_hqb", 1)
                for g4 in range(sec * 4, (sec + 1) * 4):
                    k = tt * 16 + g4
                    v.wait_ge(p.sem("q_tp2"), k + 1)
                    ps = p.ps["ps_t"] if k % 2 == 0 else p.ps["ps_a"]
                    ins = v.tensor_copy(p.sb["hfT"][:, g4 * 4:(g4 + 1) * 4, :],
                                        ps[:, 0:512])
                    p.inc(ins, "q_hte", 1)
            for oc in range(4):
                v.wait_ge(p.sem("q_dn"), tt * 4 + oc + 1)
                psd = [p.ps["ps_b"], p.ps["ps_c"], p.ps["ps_s"], p.ps["ps_z"]][oc]
                ins = v.tensor_tensor(p.sb["res"][:, tt, oc * 512:(oc + 1) * 512],
                                      p.sb["res"][:, tt, oc * 512:(oc + 1) * 512],
                                      psd[:, 0:512], ALU.add)
                p.inc(ins, "q_res", 1)

    @block.scalar
    def _(scl):
        for tt in range(NT_OWN):
            scl.wait_ge(p.sem("q_vq2"), tt + 1)
            for sec in range(4):
                ks = tt * 4 + sec
                if ks >= 1:
                    scl.wait_ge(p.sem("q_hqb"), ks)
                src = "hg"
                scl.activation(p.sb["sqt"][:],
                               p.sb[src][:, sec * 2048:(sec + 1) * 2048],
                               AF.Identity, scale=p.sb["stt"][:, 10:11],
                               bias=p.sb["qsh"][:, 0:1])
                scl.drain()
                ins = scl.copy(p.sb["hqi2"][:], p.sb["sqt"][:])
                p.inc(ins, "q_sq2", 1)
            scl.drain()
            ins = scl.nop()
            p.inc(ins, "q_hl2", 4)

    @block.tensor
    def _(t):
        for tt in range(NT_OWN):
            for g4 in range(16):
                k = tt * 16 + g4
                t.wait_ge(p.sem("q_hqb"), tt * 4 + g4 // 4 + 1)
                ps = p.ps["ps_t"] if k % 2 == 0 else p.ps["ps_a"]
                if k >= 2:
                    t.wait_ge(p.sem("q_hte"), k - 1)
                for j in range(4):
                    ht = g4 * 4 + j
                    ins = t.transpose(ps[:, j * 128:(j + 1) * 128],
                                      p.sb["hqb2"][:, ht * 128:(ht + 1) * 128],
                                      p.sb["idb"][:])
                p.inc(ins, "q_tp2", 1)
            t.wait_ge(p.sem("q_hte"), (tt + 1) * 16)
            if tt >= 1:
                t.wait_ge(p.sem("q_res"), tt * 4)
            for it in range(64):
                kk = tt * 64 + it
                t.wait_ge(p.sem("q_w2"), (kk + 1) * 16)
                for oc in range(4):
                    psd = [p.ps["ps_b"], p.ps["ps_c"], p.ps["ps_s"], p.ps["ps_z"]][oc]
                    ins = t.matmul(psd[:, 0:512],
                                   p.sb["hfT"][:, it, :],
                                   p.sb["wdr"][:, kk % 2, oc * 4:(oc + 1) * 4, :],
                                   start=(it == 0), stop=(it == 63))
                    if it == 63:
                        p.inc(ins, "q_dn", 1)
                    elif oc == 3:
                        p.inc(ins, "q_dnw", 1)



def phase_l1_wout(p):
    dram, block = p.dram, p.block
    p.sbuf("woc", [128, 16, 4, 128], BF16, local=True)
    p.sbuf("pst", [128, 512], F32, local=True)
    p.sbuf("prd", [128, D], F32, local=True)
    p.sbuf("of32", [128, D], F32, local=True)

    @block.sync
    def _(sync):
        for et in range(16):
            ins = sync.dma_start(out=p.sb["allT"][:, et, :],
                                 in_=dram["y2_spill"][et, :, :])
            p.inc(ins, "o_y2", 16)
        for oc in range(4):
            if oc >= 1:
                sync.wait_ge(p.sem("o_mm"), oc * 16)
            for j in range(4):
                ins = sync.dma_start(
                    out=p.sb["woc"][:, :, j, :],
                    in_=dram["woutT"][:, oc * 4 + j, :, :].rearrange(
                        "h pp q -> pp h q"))
                p.inc(ins, "o_wld", 16)

    @block.tensor
    def _(t):
        t.wait_ge(p.sem("o_y2"), 256)
        for oc in range(4):
            t.wait_ge(p.sem("o_wld"), 64 * (oc + 1))
            for gt in range(16):
                k = oc * 16 + gt
                ps = p.ps["ps_b"] if k % 2 == 0 else p.ps["ps_c"]
                if k >= 2:
                    t.wait_ge(p.sem("o_ev"), k - 1)
                for et in range(16):
                    ins = t.matmul(ps[:, 0:512],
                                   p.sb["allT"][:, et, gt * 128:(gt + 1) * 128],
                                   p.sb["woc"][:, et, :, :],
                                   start=(et == 0), stop=(et == 15))
                p.inc(ins, "o_mm", 1)

    def emit_wout_evac(v, par):
        gm = ([0, 1, 2, 3, 12, 13, 14, 15] if par == 0 else list(range(4, 12)))
        pm = ([0, 1, 2, 3, 12, 13, 14, 15] if par == 1 else list(range(4, 12)))
        for nm in ["o_mm", "o_ev", "o_pst"]:
            p.sem(nm)
        for oc in range(4):
            for gt in range(16):
                k = oc * 16 + gt
                v.wait_ge(p.sem("o_mm"), k + 1)
                ps = p.ps["ps_b"] if k % 2 == 0 else p.ps["ps_c"]
                if gt in gm:
                    n = gm.index(gt)
                    ins = v.tensor_tensor(
                        p.sb["res"][:, n, oc * 512:(oc + 1) * 512],
                        p.sb["res"][:, n, oc * 512:(oc + 1) * 512],
                        ps[:, 0:512], ALU.add)
                    p.inc(ins, "o_ev", 1)
                else:
                    if k >= 1:
                        v.wait_ge(p.sem("o_pst"), p.cnt.setdefault("_o_ps", 0))
                    ins = v.tensor_copy(p.sb["pst"][:], ps[:, 0:512])
                    p.inc(ins, "o_ev", 1)
                    p.cnt["_o_ps"] = p.cnt.get("_o_ps", 0) + 1

    @block.vector
    def _(v):
        p.branch_parity(v, "ovpar", lambda par: emit_wout_evac(v, par))

    def emit_wout_ship(g, par):
        gm = ([0, 1, 2, 3, 12, 13, 14, 15] if par == 0 else list(range(4, 12)))
        pm = ([0, 1, 2, 3, 12, 13, 14, 15] if par == 1 else list(range(4, 12)))
        for nm in ["o_ev", "o_pst", "o_shw"]:
            p.sem(nm)
        nev = 0
        nship = 0
        for oc in range(4):
            for gt in range(16):
                k = oc * 16 + gt
                nev += 1
                if gt not in gm:
                    pl = pm.index(gt)
                    g.wait_ge(p.sem("o_ev"), nev)
                    ins = g.dma_start(
                        out=dram["sh_part"][1 - par, pl * 128:(pl + 1) * 128,
                                            oc * 512:(oc + 1) * 512],
                        in_=p.sb["pst"][:])
                    p.inc(ins, "o_shw", 16)
                    nship += 1
                    g.wait_ge(p.sem("o_shw"), p.val("o_shw"))
                    ins = g.nop()
                    p.inc(ins, "o_pst", 1)
        g.wait_ge(p.sem("o_ev"), 64)
        emit_flag_write(p, g, 2 + par, "o_fw")
        emit_flag_poll(p, g, 2 + (1 - par), "o_fp")
        for n in range(NT_OWN):
            if n >= 1:
                g.wait_ge(p.sem("o_add"), n)
            ins = g.dma_start(out=p.sb["prd"][:],
                              in_=dram["sh_part"][par, n * 128:(n + 1) * 128, :])
            p.inc(ins, "o_prd", 16)
            g.wait_ge(p.sem("o_addo"), n)
            if n >= 1:
                g.wait_ge(p.sem("o_out"), n * 16)
            g.wait_ge(p.sem("o_add"), n + 1)
            ins = g.dma_start(out=dram["out_own"][n * 128:(n + 1) * 128, :],
                              in_=p.sb["of32"][:])
            p.inc(ins, "o_out", 16)
        g.wait_ge(p.sem("o_out"), p.val("o_out"))

    @block.gpsimd
    def _(g):
        p.branch_parity(g, "ogpar", lambda par: emit_wout_ship(g, par))

    @block.vector
    def _(v):
        for n in range(NT_OWN):
            v.wait_ge(p.sem("o_prd"), (n + 1) * 16)
            if n >= 1:
                v.wait_ge(p.sem("o_out"), n * 16)
            ins = v.tensor_tensor(p.sb["of32"][:], p.sb["res"][:, n, :],
                                  p.sb["prd"][:], ALU.add)
            p.inc(ins, "o_add", 1)
            ins = v.nop()
            p.inc(ins, "o_addo", 1)

_TAPS["allT"] = ([128, ND, T], BF16)
_TAPS["res"] = ([128, NT_OWN, D], BF16)


WCHK = []


def phase_debug_out(p):
    block, dram = p.block, p.dram

    @block.gpsimd
    def _(g):
        skip_out = getattr(p, "final_written", False)
        if "y2chk" in p.taps:
            for et in range(16):
                ins = g.dma_start(out=p.sb["sqt"][:, 0:2048].rearrange(
                    "p (a b) -> p a b", a=1) if False else p.sb["wchkb2"][:],
                    in_=p.dram["y2_spill"][et, :, :])
                p.inc(ins, "g_tap", 16)
                g.wait_ge(p.sem("g_tap"), p.val("g_tap"))
                ins = g.dma_start(out=p.dram["tap_y2chk"][et, :, :],
                                  in_=p.sb["wchkb2"][:])
                p.inc(ins, "g_tap", 16)
                g.wait_ge(p.sem("g_tap"), p.val("g_tap"))
        for t in p.taps:
            if t == "y2chk" or t == "bctchk":
                continue
            if t.startswith("dchk_"):
                nmm = t[5:]
                n0 = dram[nmm].shape[0]
                for et in range(n0):
                    ins = g.dma_start(out=p.sb["wchkb2"][:],
                                      in_=dram[nmm][et, :, :])
                    p.inc(ins, "g_tap", 16)
                    g.wait_ge(p.sem("g_tap"), p.val("g_tap"))
                    ins = g.dma_start(out=dram["tap_" + t][et, :, :],
                                      in_=p.sb["wchkb2"][:])
                    p.inc(ins, "g_tap", 16)
                    g.wait_ge(p.sem("g_tap"), p.val("g_tap"))
                continue
            if t.startswith("wchk_"):
                wname = t[5:]
                hdim = dram[wname].shape[0]
                for ht in range(hdim):
                    for ot in range(dram[wname].shape[1]):
                        ins = g.dma_start(out=p.sb["wchkb"][:],
                                          in_=dram[wname][ht, ot, :, :])
                        p.inc(ins, "g_tap", 16)
                        g.wait_ge(p.sem("g_tap"), p.val("g_tap"))
                        ins = g.dma_start(out=dram["tap_" + t][ht, ot, :, :],
                                          in_=p.sb["wchkb"][:])
                        p.inc(ins, "g_tap", 16)
                        g.wait_ge(p.sem("g_tap"), p.val("g_tap"))
                continue
            ins = g.dma_start(out=dram["tap_" + t], in_=p.sb[t][:])
            p.inc(ins, "g_tap", 16)
        if not skip_out:
            for n in range(NT_OWN):
                ins = g.dma_start(out=dram["out_own"][n * 128:(n + 1) * 128, :],
                                  in_=p.sb["res"][:, n, :])
                p.inc(ins, "g_out", 16)
            g.wait_ge(p.sem("g_out"), p.val("g_out"))
        if p.taps:
            g.wait_ge(p.sem("g_tap"), p.val("g_tap"))



def build_phases(p, phase_limit):
    phases = [phase_prologue, phase_l0_norm]
    if phase_limit >= 2:
        phases.append(lambda pp: phase_weight_prep(pp, WPREP_L0, "a"))
    if phase_limit >= 3:
        phases.append(lambda pp: phase_weight_prep(
            pp, [w for w in WPREP_REST if w[4] == "rowq"], "b"))
    if phase_limit >= 4:
        phases.append(phase_l0_ffn)
    if phase_limit >= 5:
        phases.append(phase_l0_ffn_b)
    if phase_limit >= 6:
        phases.append(phase_l0_attn)
    if phase_limit >= 7:
        phases.append(phase_l1_norm)
    if phase_limit >= 8:
        phases.append(lambda pp: phase_weight_prep(
            pp, [w for w in WPREP_REST if w[4] == "plain"], "c"))
        if phase_limit >= 9:
            phases.append((phase_l1_skipu, phase_l1_scan))
        else:
            phases.append(phase_l1_skipu)
    if phase_limit >= 10:
        phases.append(phase_bitnet_scale)
        phases.append(lambda pp: phase_weight_prep(
            pp, [w for w in WPREP_REST if w[4] == "bitnet"], "d"))
    if phase_limit >= 11:
        phases.append(phase_l1_ffn_a)
        phases.append(phase_l1_ffn_b)
    if phase_limit >= 12:
        phases.append(phase_l1_wout)
        p.final_written = True
    return phases


def build_program(phase_limit=99, taps=()):
    nc = bass.Bass("TRN2", target_bir_lowering=False, detect_race_conditions=False)
    es = ExitStack()
    dram = {}

    def din(name, shape, dtype=F32):
        dram[name] = nc.dram_tensor(name, shape, dtype, kind="ExternalInput").ap()

    def dout(name, shape, dtype=F32):
        dram[name] = nc.dram_tensor(name, shape, dtype, kind="ExternalOutput").ap()

    def dint(name, shape, dtype=BF16, shared=False):
        h = nc.dram_tensor(name, shape, dtype,
                           addr_space="Shared" if shared else "Local")
        dram[name] = h.ap()

    din("x_own", [TOK, D])
    din("x_all", [T, D])
    din("identb", [128, 128], BF16)
    din("tri", [512, 512])
    din("cst", [128, 16])
    din("pari_in", [1, 2], I32)
    din("rep16", [16, 16, 128], BF16)
    for nm, sh in [("Wq", [D, D]), ("Wk", [D, D]), ("Wv", [D, D]), ("Wo", [D, D]),
                   ("w_gu", [2 * I0, D]), ("w_dn", [D, I0]),
                   ("W_in", [E, D]), ("W_skip_own", [EH, D]),
                   ("W_out_own", [D, EH]), ("Wd1", [128, E]), ("Wd2_own", [EH, 128]),
                   ("WB", [128, E]), ("WC", [128, E]),
                   ("wg1", [I1, D]), ("wu1", [I1, D]), ("wd1", [D, I1])]:
        din(nm, sh)
    din("norm_w", [1, D])
    din("conv_w", [128, E // 128, DC])
    din("conv_b", [128, E // 128])
    din("A_own", [128, EH // 128, NST])
    din("W_D_own", [128, EH // 128])
    din("bd2_own", [128, EH // 128])

    dout("out_own", [TOK, D])
    if "y2chk" in taps:
        dout("tap_y2chk", [16, 128, T], BF16)
    for t in [tt for tt in taps if tt != "y2chk"]:
        if t.startswith("dchk_"):
            nmm = t[5:]
            shp = {"uT_spill": [E // 128, 128, T],
                   "skipT_spill": [EH // 128, 128, T]}[nmm]
            dout("tap_" + t, shp, BF16)
            continue
        if t == "bctchk":
            dout("tap_bctchk", [128, 2, D], BF16)
            dout("tap_ud1chk", [128, D], BF16)
            continue
        if t.startswith("wchk_"):
            wname = t[5:]
            hdim, odim = {"wqT": (D, D), "wkT": (D, D), "wvT": (D, D),
                          "woT": (D, D), "w1qT": (D, 2 * I0), "w2qT": (I0, D),
                          "winT": (D, E), "wskT": (D, EH), "woutT": (EH, D),
                          "wd1T": (E, 128), "wd2T": (128, EH), "wbT": (E, 128),
                          "wcT": (E, 128), "wg1T": (D, I1), "wu1T": (D, I1),
                          "wdn1T": (I1, D)}[wname]
            dout("tap_" + t, [hdim // 128, odim // 128, 128, 128], BF16)
            continue
        shape, dtype = _TAPS[t]
        dout("tap_" + t, shape, dtype)

    def wt_tiled(name, Hdim, Odim):
        dint(name, [Hdim // 128, Odim // 128, 128, 128], BF16)

    for nm, (h, o) in {"wqT": (D, D), "wkT": (D, D), "wvT": (D, D), "woT": (D, D),
                       "w1qT": (D, 2 * I0), "w2qT": (I0, D), "winT": (D, E),
                       "wskT": (D, EH), "woutT": (EH, D), "wd1T": (E, 128),
                       "wd2T": (128, EH), "wbT": (E, 128), "wcT": (E, 128),
                       "wg1T": (D, I1), "wu1T": (D, I1), "wdn1T": (I1, D)}.items():
        wt_tiled(nm, h, o)
    dint("h_spill", [NT_OWN, I0 // 512, 128, 512], F32)
    dint("uT_spill", [E // 128, 128, T], BF16)
    dint("y2_spill", [16, 128, T], BF16)
    dint("h1_spill", [NT_OWN, I1 // 512, 128, 512], BF16)
    dint("skipT_spill", [EH // 128, 128, T], BF16)
    dint("res_spill", [TOK, D], F32)
    dint("sh_xn2T", [ND, 128, T], BF16, shared=True)
    dint("sh_part", [2, TOK, D], F32, shared=True)
    dint("sh_flag", [8, 2], I32, shared=True)

    with es:
        p = P(nc, es, taps)
        p.dram = dram
        p.sbuf("res", [128, NT_OWN, D], BF16)
        p.sbuf("allT", [128, ND, T], BF16)
        p.sbuf("sqt", [128, D], F32)
        p.sbuf("stt", [128, 12], F32)
        p.sbuf("cstt", [128, 16], F32)
        p.sbuf("idb", [128, 128], BF16)
        p.sbuf("prbi", [128, 1], I32)
        p.sbuf("qsh", [128, 1], F32)
        p.sbuf("pari", [1, 2], I32)
        p.sbuf("bns", [128, 8], F32)
        p.sbuf("am", [128, 10], F32)
        p.sbuf("mgw", [1, 2], I32)
        p.sbuf("mgr", [1, 2], I32)
        p.sbuf("Asb", [128, EH // 128, NST], F32)
        p.sbuf("wdsb", [128, EH // 128], F32)
        p.sbuf("bdsb", [128, EH // 128], F32)
        p.sbuf("wchkb", [128, 128], BF16)
        p.sbuf("wchkb2", [128, T], BF16)
        p.psum("ps_t", [128, 512], BF16)
        p.psum("ps_a", [128, 512], BF16)
        p.psum("ps_b", [128, 512])
        p.psum("ps_c", [128, 512])
        p.psum("ps_s", [128, 512])
        p.psum("ps_z", [128, 512])

        p.block = es.enter_context(nc.Block())
        for ph in build_phases(p, phase_limit):
            with ExitStack() as gstk:
                p.gstack = gstk
                if isinstance(ph, tuple):
                    for sub in ph:
                        with ExitStack() as stk:
                            p.stack = stk
                            with nc.named_scope(getattr(sub, "__name__", "sub")):
                                sub(p)
                                p.barrier()
                else:
                    with ExitStack() as stk:
                        p.stack = stk
                        with nc.named_scope(getattr(ph, "__name__", "ph")):
                            ph(p)
                            p.barrier()
        p.stack = es
        phase_debug_out(p)

    return nc



def _host_constants():
    tri = np.triu(np.full((512, 512), NEG_BIG, np.float32), k=1)
    cst = np.zeros((128, 16), np.float32)
    cst[:, 0] = INV_SQRT_HD
    cst[:, 2] = -1.0
    cst[:, 3] = 1.0
    return tri, cst


def make_in_maps(inputs):
    import ml_dtypes
    x = np.asarray(inputs["L0_attn_x"])
    tri, cst = _host_constants()
    identb = np.eye(128, dtype=np.float32).astype(ml_dtypes.bfloat16)
    wb_pad = np.zeros((128, E), np.float32)
    wb_pad[:NST] = inputs["L1_attn_WB"]
    wc_pad = np.zeros((128, E), np.float32)
    wc_pad[:NST] = inputs["L1_attn_WC"]
    maps = []
    for c in range(8):
        b, par = c // 2, c % 2
        rows = own_rows(par)
        x_own = np.concatenate([x[b, r0:r1] for (r0, r1) in rows], axis=0)
        rep16 = np.zeros((16, 16, 128), np.float32)
        for nn in range(16):
            rep16[nn, nn, :] = 1.0
        m = {
            "x_own": x_own, "x_all": x[b],
            "identb": identb, "tri": tri, "cst": cst,
            "rep16": rep16.astype(ml_dtypes.bfloat16),
            "pari_in": np.array([[par, 0]], np.int32),
            "Wq": inputs["L0_attn_Wq"], "Wk": inputs["L0_attn_Wk"],
            "Wv": inputs["L0_attn_Wv"], "Wo": inputs["L0_attn_Wo"],
            "w_gu": inputs["L0_ffn_w_gate_up"], "w_dn": inputs["L0_ffn_w_down"],
            "W_in": inputs["L1_attn_W_in"],
            "W_skip_own": inputs["L1_attn_W_skip"][par * EH:(par + 1) * EH],
            "W_out_own": inputs["L1_attn_W_out"][:, par * EH:(par + 1) * EH],
            "Wd1": inputs["L1_attn_Wd1"],
            "Wd2_own": inputs["L1_attn_Wd2"][par * EH:(par + 1) * EH],
            "WB": wb_pad, "WC": wc_pad,
            "wg1": inputs["L1_ffn_w_gate"], "wu1": inputs["L1_ffn_w_up"],
            "wd1": inputs["L1_ffn_w_down"],
            "norm_w": inputs["L1_attn_norm_w"][None, :],
            "conv_w": inputs["L1_attn_conv_w"][:, 0, :].reshape(
                E // 128, 128, DC).transpose(1, 0, 2),
            "conv_b": inputs["L1_attn_conv_b"].reshape(E // 128, 128).T,
            "A_own": inputs["L1_attn_A_log"][par * EH:(par + 1) * EH].reshape(
                EH // 128, 128, NST).transpose(1, 0, 2),
            "W_D_own": inputs["L1_attn_W_D"][par * EH:(par + 1) * EH].reshape(
                EH // 128, 128).T,
            "bd2_own": inputs["L1_attn_bd2"][par * EH:(par + 1) * EH].reshape(
                EH // 128, 128).T,
        }
        maps.append({k: np.ascontiguousarray(v) for k, v in m.items()})
    return maps


def assemble(results):
    out = np.zeros((B, T, D), np.float32)
    for c in range(8):
        b, par = c // 2, c % 2
        off = 0
        o = results[c]["out_own"]
        for (r0, r1) in own_rows(par):
            out[b, r0:r1] = o[off:off + (r1 - r0)]
            off += r1 - r0
    return out


def kernel(**inputs):
    in_maps = make_in_maps(inputs)
    nc = build_program()
    res = run_bass_kernel_spmd(nc, in_maps, core_ids=list(range(8)))
    return assemble(res.results)


# version 09
# speedup vs baseline: 1.0323x; 1.0323x over previous
import numpy as np
from contextlib import ExitStack
import concourse.bass as bass
import concourse.mybir as mybir
from concourse.bass_utils import run_bass_kernel_spmd

F32 = mybir.dt.float32
BF16 = mybir.dt.bfloat16
I32 = mybir.dt.int32
AF = mybir.ActivationFunctionType
ALU = mybir.AluOpType
AX = mybir.AxisListType

B, T, D = 4, 2048, 2048
TOK = 1024
NH, HD = 16, 128
I0 = 5632
E, NST, DTR, DC = 4096, 16, 128, 4
EH = 2048
I1 = 8192
NT_OWN = TOK // 128
NT_ALL = T // 128
ND = D // 128
MAGIC0 = 0x3A7C01D5
MAGIC1 = 0x5EC0FFEE
INV_SQRT_HD = float(1.0 / np.sqrt(128.0))
NEG_BIG = -1e30
ENGINES = ["tensor", "vector", "scalar", "gpsimd", "sync"]

_TAPS = {}


def own_rows(parity):
    return [(0, 512), (1536, 2048)] if parity == 0 else [(512, 1536)]


def q512_groups(parity):
    return [0, 3] if parity == 0 else [1, 2]


class P:
    def __init__(self, nc, es, taps):
        self.nc, self.es = nc, es
        self.block = None
        self.taps = list(taps)
        self.sems, self.cnt = {}, {}
        self.sb, self.ps, self.dram = {}, {}, {}
        self.phase = 0
        self.stack = es
        self.phase_new = []

    def sem(self, name):
        if name not in self.sems:
            self.sems[name] = self.nc.alloc_semaphore(name=name + "_" + str(self.phase))
            self.cnt[name] = 0
            if not name.startswith("ph"):
                self.phase_new.append(name)
        return self.sems[name]

    def inc(self, instr, name, n=1):
        instr.then_inc(self.sem(name), n)
        self.cnt[name] += n
        return self.cnt[name]

    def val(self, name):
        self.sem(name)
        return self.cnt[name]

    def sbuf(self, name, shape, dtype, local=False):
        if local == "group":
            st = self.gstack
        elif local:
            st = self.stack
        else:
            st = self.es
        h = st.enter_context(self.nc.sbuf_tensor(name, shape, dtype))
        self.sb[name] = h
        return h

    def psum(self, name, shape, dtype=F32, local=False):
        st = self.stack if local else self.es
        h = st.enter_context(self.nc.psum_tensor(name, shape, dtype))
        self.ps[name] = h
        return h

    def barrier(self):
        self.phase += 1
        ph = self.phase
        for nm in ENGINES:
            self.sem("ph_" + nm)
        self.sem("ph2")
        retiring = [n for n in self.phase_new]

        def mk(nm):
            def f(eng):
                if nm != "sync":
                    eng.drain()
                eng.nop().then_inc(self.sems["ph_" + nm], 1)
                for other in ENGINES:
                    eng.wait_ge(self.sems["ph_" + other], ph)
                if nm == "gpsimd":
                    eng.dma_reset()
                    for rn in retiring:
                        eng.sem_clear(self.sems[rn])
                    eng.nop().then_inc(self.sems["ph2"], 1)
                eng.wait_ge(self.sems["ph2"], ph)
            return f
        for nm in ENGINES:
            getattr(self.block, nm)(mk(nm))
            self.cnt["ph_" + nm] = ph
        self.cnt["ph2"] = ph
        for rn in retiring:
            self.nc.release_semaphore(self.sems[rn])
            del self.sems[rn]
            del self.cnt[rn]
        self.phase_new = []

    def branch_parity(self, eng, regname, emit_fn):
        snap = dict(self.cnt)
        with eng.register(regname) as r:
            eng.reg_load(r, self.sb["pari"][0:1, 0:1])
            with eng.If_eq(r, 0):
                emit_fn(0)
            after0 = dict(self.cnt)
            self.cnt = {**{k: 0 for k in after0}, **snap}
            with eng.Else():
                emit_fn(1)
        assert self.cnt == after0, (
            "branch sem mismatch: " + str({
                k: (self.cnt.get(k), after0.get(k))
                for k in set(self.cnt) | set(after0)
                if self.cnt.get(k) != after0.get(k)}))



def emit_layernorm_vec(p, v, xt_ap, out_f32_ap, tag, n):
    st = p.sb["stt"]
    v.reduce_sum(st[:, 0:1], xt_ap, AX.X)
    v.wait_ge(p.sem("sc_sq_" + tag), n + 1)
    v.drain()
    v.tensor_scalar(st[:, 2:3], st[:, 0:1], 1.0 / D, None, ALU.mult)
    v.tensor_scalar(st[:, 3:4], st[:, 1:2], 1.0 / D, None, ALU.mult)
    v.drain()
    v.tensor_tensor(st[:, 4:5], st[:, 2:3], st[:, 2:3], ALU.mult)
    v.drain()
    v.tensor_tensor(st[:, 5:6], st[:, 3:4], st[:, 4:5], ALU.subtract)
    v.drain()
    ins = v.tensor_scalar(st[:, 6:7], st[:, 5:6], 1e-5, None, ALU.add)
    p.inc(ins, "v_var_" + tag, 1)
    v.wait_ge(p.sem("sc_sqrt_" + tag), n + 1)
    v.reciprocal(st[:, 8:9], st[:, 7:8])
    v.drain()
    v.tensor_scalar(out_f32_ap, xt_ap, st[:, 2:3], st[:, 8:9],
                    ALU.subtract, ALU.mult)


def emit_layernorm_sca(p, s, xt_ap, tag, n):
    st = p.sb["stt"]
    ins = s.activation(p.sb["sqt"][:], xt_ap, AF.Square, accum_out=st[:, 1:2])
    p.inc(ins, "sc_sq_" + tag, 1)
    s.wait_ge(p.sem("v_var_" + tag), n + 1)
    ins = s.activation(st[:, 7:8], st[:, 6:7], AF.Sqrt)
    p.inc(ins, "sc_sqrt_" + tag, 1)



def phase_prologue(p):
    dram, block = p.dram, p.block

    @block.sync
    def _(sync):
        for nm, dst in [("cst", "cstt"), ("identb", "idb"), ("pari_in", "pari")]:
            ins = sync.dma_start(out=p.sb[dst][:], in_=dram[nm])
            p.inc(ins, "s_in", 16)

    @block.vector
    def _(v):
        v.wait_ge(p.sem("s_in"), p.val("s_in"))
        v.memset(p.sb["qsh"][:], 0.75)
        v.drain()
        v.tensor_copy(p.sb["prbi"][:], p.sb["qsh"][:])
        v.drain()
        ins = v.tensor_scalar(p.sb["qsh"][:], p.sb["prbi"][:], -0.5, 512.5,
                              ALU.mult, ALU.add)
        p.inc(ins, "v_pro", 1)



def phase_l0_norm(p):
    dram, block = p.dram, p.block
    p.sbuf("xst", [128, 2, D], F32, local=True)
    p.sbuf("xow", [128, 2, D], F32, local=True)
    p.sbuf("xn", [128, D], F32, local=True)
    p.sbuf("xnb", [128, D], BF16, local=True)

    @block.sync
    def _(sync):
        for n in range(NT_ALL):
            if n >= 2:
                sync.wait_ge(p.sem("v_xn_done"), n - 1)
            ins = sync.dma_start(out=p.sb["xst"][:, n % 2, :],
                                 in_=dram["x_all"][n * 128:(n + 1) * 128, :])
            p.inc(ins, "s_xall", 16)

    @block.scalar
    def _(s):
        for n in range(NT_ALL):
            s.wait_ge(p.sem("s_xall"), (n + 1) * 16)
            emit_layernorm_sca(p, s, p.sb["xst"][:, n % 2, :], "l0", n)

    @block.vector
    def _(v):
        for n in range(NT_ALL):
            v.wait_ge(p.sem("s_xall"), (n + 1) * 16)
            emit_layernorm_vec(p, v, p.sb["xst"][:, n % 2, :], p.sb["xn"][:],
                               "l0", n)
            v.drain()
            ins = v.tensor_copy(p.sb["xnb"][:], p.sb["xn"][:])
            p.inc(ins, "v_xnb", 1)
            v.drain()
            ins = v.nop()
            p.inc(ins, "v_xn_done", 1)
            for g in range(4):
                v.wait_ge(p.sem("t_tp"), n * 4 + g + 1)
                ps = p.ps["ps_t"] if g % 2 == 0 else p.ps["ps_a"]
                dst = p.sb["allT"][:, g * 4:(g + 1) * 4, n * 128:(n + 1) * 128]
                ins = v.tensor_copy(dst, ps[:, 0:512])
                p.inc(ins, "v_evac", 1)

    @block.tensor
    def _(t):
        for n in range(NT_ALL):
            t.wait_ge(p.sem("v_xnb"), n + 1)
            for g in range(4):
                k = n * 4 + g
                ps = p.ps["ps_t"] if g % 2 == 0 else p.ps["ps_a"]
                if k >= 2:
                    t.wait_ge(p.sem("v_evac"), k - 1)
                for j in range(4):
                    dt = g * 4 + j
                    ins = t.transpose(ps[:, j * 128:(j + 1) * 128],
                                      p.sb["xnb"][:, dt * 128:(dt + 1) * 128],
                                      p.sb["idb"][:])
                p.inc(ins, "t_tp", 1)

    @block.gpsimd
    def _(g):
        for n in range(NT_OWN):
            if n >= 2:
                g.wait_ge(p.sem("g_res"), n - 1)
            ins = g.dma_start(out=p.sb["xow"][:, n % 2, :],
                              in_=dram["x_own"][n * 128:(n + 1) * 128, :])
            p.inc(ins, "g_xown", 16)
            g.wait_ge(p.sem("g_xown"), p.val("g_xown"))
            ins = g.tensor_copy(p.sb["res"][:, n, :], p.sb["xow"][:, n % 2, :])
            p.inc(ins, "g_res", 1)



def phase_bitnet_scale(p):
    dram, block = p.dram, p.block
    p.sbuf("bst", [128, I1], F32, local=True)
    p.sbuf("babs", [128, I1], F32, local=True)
    p.sbuf("ones128", [128, 128], BF16, local=True)
    p.sbuf("bnsb", [128, 4], BF16, local=True)
    wl = [("wg1", I1, D), ("wu1", I1, D), ("wd1", D, I1)]
    tiles = [(src, H, ot) for (src, O, H) in wl for ot in range(O // 128)]

    @block.sync
    def _(sync):
        for i, (src, H, ot) in enumerate(tiles):
            if i >= 1:
                sync.wait_ge(p.sem("b_abs"), i)
            ins = sync.dma_start(out=p.sb["bst"][:, 0:H],
                                 in_=dram[src][ot * 128:(ot + 1) * 128, :])
            p.inc(ins, "b_ld", 16)

    @block.scalar
    def _(s):
        for i, (src, H, ot) in enumerate(tiles):
            s.wait_ge(p.sem("b_ld"), (i + 1) * 16)
            if i > 0:
                s.wait_ge(p.sem("b_acc"), i)
            ins = s.activation(p.sb["babs"][:, 0:H], p.sb["bst"][:, 0:H],
                               AF.Abs, accum_out=p.sb["stt"][:, 9:10])
            p.inc(ins, "b_abs", 1)

    @block.vector
    def _(v):
        wi = -1
        prev = None
        for i, (src, H, ot) in enumerate(tiles):
            if src != prev:
                wi += 1
                prev = src
                v.memset(p.sb["bns"][:, 2 * wi:2 * wi + 1], 0.0)
                v.drain()
            v.wait_ge(p.sem("b_abs"), i + 1)
            v.tensor_tensor(p.sb["bns"][:, 2 * wi:2 * wi + 1],
                            p.sb["bns"][:, 2 * wi:2 * wi + 1],
                            p.sb["stt"][:, 9:10], ALU.add)
            v.drain()
            ins = v.nop()
            p.inc(ins, "b_acc", 1)

    @block.vector
    def _(v):
        v.wait_ge(p.sem("b_acc"), len(tiles))
        v.memset(p.sb["ones128"][:], 1.0)
        v.drain()
        for wi in range(3):
            v.tensor_copy(p.sb["bnsb"][:, wi:wi + 1],
                          p.sb["bns"][:, 2 * wi:2 * wi + 1])
        v.drain()
        ins = v.nop()
        p.inc(ins, "b_accb", 1)

    @block.tensor
    def _(t):
        t.wait_ge(p.sem("b_accb"), 1)
        for wi in range(3):
            ins = t.matmul(p.ps["ps_s"][:, wi:wi + 1], p.sb["ones128"][:],
                           p.sb["bnsb"][:, wi:wi + 1], start=True, stop=True)
            p.inc(ins, "b_red", 1)

    @block.vector
    def _(v):
        v.wait_ge(p.sem("b_red"), 3)
        for wi in range(3):
            v.tensor_copy(p.sb["bns"][:, 2 * wi:2 * wi + 1],
                          p.ps["ps_s"][:, wi:wi + 1])
        v.drain()
        for wi, (src, O, H) in enumerate(wl):
            v.tensor_scalar(p.sb["bns"][:, 2 * wi:2 * wi + 1],
                            p.sb["bns"][:, 2 * wi:2 * wi + 1],
                            1.0 / (O * H), 1e-5, ALU.mult, ALU.max)
        v.drain()
        for wi in range(3):
            v.reciprocal(p.sb["bns"][:, 2 * wi + 1:2 * wi + 2],
                         p.sb["bns"][:, 2 * wi:2 * wi + 1])
        v.drain()
        ins = v.nop()
        p.inc(ins, "b_done", 1)



WPREP_L0 = [("Wq", "wqT", D, D, "plain"), ("Wk", "wkT", D, D, "plain"),
            ("Wv", "wvT", D, D, "plain"), ("Wo", "woT", D, D, "plain")]
WPREP_REST = [
    ("w_gu", "w1qT", 2 * I0, D, "rowq"),
    ("w_dn", "w2qT", D, I0, "rowq"),
    ("W_in", "winT", E, D, "plain"),
    ("W_skip_own", "wskT", EH, D, "plain"),
    ("W_out_own", "woutT", D, EH, "plain"),
    ("Wd1", "wd1T", 128, E, "plain"),
    ("Wd2_own", "wd2T", EH, 128, "plain"),
    ("WB", "wbT", 128, E, "plain"),
    ("WC", "wcT", 128, E, "plain"),
    ("wg1", "wg1T", I1, D, "bitnet", 0),
    ("wu1", "wu1T", I1, D, "bitnet", 1),
    ("wd1", "wdn1T", D, I1, "bitnet", 2),
]



def _tp_batches(H):
    nh = H // 128
    out = []
    i = 0
    while i < nh:
        c = min(4, nh - i)
        out.append((i, c))
        i += c
    return out

def phase_weight_prep(p, wlist, tag):
    dram, block = p.dram, p.block
    HMAX = max(w[3] for w in wlist)
    SEC = 2048
    p.sbuf("wst" + tag, [128, HMAX], F32, local=True)
    p.sbuf("wbb" + tag, [128, HMAX], BF16, local=True)
    p.sbuf("wyb" + tag, [128, SEC], F32, local=True)
    p.sbuf("wib" + tag, [128, SEC], I32, local=True)
    p.sbuf("wtg" + tag, [128, HMAX // 128, 128], BF16, local=True)
    p.sbuf("wqs" + tag, [128, 8], F32, local=True)
    S_LD, S_RDY, S_TP, S_EV, S_ST = ("wpld" + tag, "wprdy" + tag, "wptp" + tag,
                                     "wpev" + tag, "wpst" + tag)
    S_VQ, S_SQ = "wpvq" + tag, "wpsq" + tag

    tasks = []
    for w in wlist:
        src, dst, O, H, mode = w[:5]
        bix = w[5] if len(w) > 5 else None
        tasks.extend((src, dst, O, H, mode, bix, ot) for ot in range(O // 128))

    def secs(H):
        out = []
        w0 = 0
        while w0 < H:
            w = min(SEC, H - w0)
            out.append((w0, w))
            w0 += w
        return out
    ev_upto, tp_upto = [], []
    tot = tot2 = 0
    for (src, dst, O, H, mode, bix, ot) in tasks:
        tot += len(_tp_batches(H))
        tot2 += H // 128
        ev_upto.append(tot)
        tp_upto.append(tot2)

    @block.sync
    def _(sync):
        for i, (src, dst, O, H, mode, bix, ot) in enumerate(tasks):
            if i >= 1:
                sync.wait_ge(p.sem(S_RDY), i)
            ins = sync.dma_start(out=p.sb["wst" + tag][:, 0:H],
                                 in_=dram[src][ot * 128:(ot + 1) * 128, :])
            p.inc(ins, S_LD, 16)

    @block.vector
    def _(v):
        nsq = 0
        k = 0
        for i, (src, dst, O, H, mode, bix, ot) in enumerate(tasks):
            if mode == "rowq":
                v.wait_ge(p.sem(S_LD), (i + 1) * 16)
                wq = p.sb["wqs" + tag]
                v.reduce_max(wq[:, 0:1], p.sb["wst" + tag][:, 0:H],
                             AX.X, apply_absolute_value=True)
                v.drain()
                v.tensor_scalar(wq[:, 1:2], wq[:, 0:1], 1.0 / 127.0, 1e-8,
                                ALU.mult, ALU.max)
                v.drain()
                v.reciprocal(wq[:, 2:3], wq[:, 1:2])
                v.drain()
                ins = v.nop()
                p.inc(ins, S_VQ, 1)
            if mode in ("rowq", "bitnet"):
                sc = (p.sb["wqs" + tag][:, 1:2] if mode == "rowq"
                      else p.sb["bns"][:, 2 * bix:2 * bix + 1])
                for (w0, w) in secs(H):
                    nsq += 1
                    v.wait_ge(p.sem(S_SQ), nsq)
                    if i >= 1 and w0 == 0:
                        v.wait_ge(p.sem(S_TP), ev_upto[i - 1])
                    if mode == "bitnet":
                        v.tensor_scalar(p.sb["wyb" + tag][:, 0:w],
                                        p.sb["wib" + tag][:, 0:w],
                                        511.0, 513.0, ALU.max, ALU.min)
                        v.drain()
                        ins = v.tensor_scalar(
                            p.sb["wbb" + tag][:, w0:w0 + w],
                            p.sb["wyb" + tag][:, 0:w],
                            512.0, sc, ALU.subtract, ALU.mult)
                    else:
                        ins = v.tensor_scalar(
                            p.sb["wbb" + tag][:, w0:w0 + w],
                            p.sb["wib" + tag][:, 0:w],
                            512.0, sc, ALU.subtract, ALU.mult)
                    p.inc(ins, "wsec" + tag, 1)
                v.drain()
                ins = v.nop()
                p.inc(ins, S_RDY, 1)
            if i >= 1:
                v.wait_ge(p.sem(S_ST), i * 16)
            for (h0, cnt) in _tp_batches(H):
                v.wait_ge(p.sem(S_TP), k + 1)
                ps = p.ps["ps_t"] if k % 2 == 0 else p.ps["ps_a"]
                ins = v.tensor_copy(p.sb["wtg" + tag][:, h0:h0 + cnt, :],
                                    ps[:, 0:cnt * 128])
                p.inc(ins, S_EV, 1)
                k += 1

    @block.scalar
    def _(scl):
        nvq = nsq = 0
        first_bitnet = True
        for i, (src, dst, O, H, mode, bix, ot) in enumerate(tasks):
            if mode == "plain":
                scl.wait_ge(p.sem(S_LD), (i + 1) * 16)
                if i >= 1:
                    scl.wait_ge(p.sem(S_TP), ev_upto[i - 1])
                ins = scl.copy(p.sb["wbb" + tag][:, 0:H],
                               p.sb["wst" + tag][:, 0:H])
                p.inc(ins, S_RDY, 1)
            else:
                if mode == "rowq":
                    nvq += 1
                    scl.wait_ge(p.sem(S_VQ), nvq)
                    sc = p.sb["wqs" + tag][:, 2:3]
                else:
                    scl.wait_ge(p.sem(S_LD), (i + 1) * 16)
                    first_bitnet = False
                    sc = p.sb["bns"][:, 2 * bix + 1:2 * bix + 2]
                for (w0, w) in secs(H):
                    if nsq >= 1:
                        scl.wait_ge(p.sem("wsec" + tag), nsq)
                    scl.activation(p.sb["wyb" + tag][:, 0:w],
                                   p.sb["wst" + tag][:, w0:w0 + w],
                                   AF.Identity, scale=sc,
                                   bias=p.sb["qsh"][:, 0:1])
                    scl.drain()
                    ins = scl.copy(p.sb["wib" + tag][:, 0:w],
                                   p.sb["wyb" + tag][:, 0:w])
                    nsq += 1
                    p.inc(ins, S_SQ, 1)

    @block.tensor
    def _(t):
        k = 0
        for i, (src, dst, O, H, mode, bix, ot) in enumerate(tasks):
            t.wait_ge(p.sem(S_RDY), i + 1)
            for (h0, cnt) in _tp_batches(H):
                ps = p.ps["ps_t"] if k % 2 == 0 else p.ps["ps_a"]
                if k >= 2:
                    t.wait_ge(p.sem(S_EV), k - 1)
                for j in range(cnt):
                    ht = h0 + j
                    ins = t.transpose(ps[:, j * 128:(j + 1) * 128],
                                      p.sb["wbb" + tag][:, ht * 128:(ht + 1) * 128],
                                      p.sb["idb"][:])
                p.inc(ins, S_TP, 1)
                k += 1

    @block.gpsimd
    def _(g):
        for i, (src, dst, O, H, mode, bix, ot) in enumerate(tasks):
            g.wait_ge(p.sem(S_EV), ev_upto[i])
            ins = g.dma_start(out=dram[dst][:, ot, :, :].rearrange("h p q -> p h q"),
                              in_=p.sb["wtg" + tag][:, 0:H // 128, :])
            p.inc(ins, S_ST, 16)
        g.wait_ge(p.sem(S_ST), p.val(S_ST))




def phase_l0_attn(p):
    dram, block = p.dram, p.block
    p.sbuf("tri_sb", [128, 4 * 512], F32, local=True)
    p.sbuf("wbuf", [128, 8192], BF16, local=True)
    p.sbuf("kT", [128, T], BF16, local=True)
    p.sbuf("vT", [128, T], BF16, local=True)
    p.sbuf("vh", [128, 16, 128], BF16, local=True)
    p.sbuf("qTh", [128, TOK], BF16, local=True)
    p.sbuf("prb", [128, 4, T], BF16, local=True)
    p.sbuf("atT", [128, 4, 128], BF16, local=True)
    p.sbuf("zT", [128, NH, TOK], BF16, local=True)

    def wslice(i):
        return p.sb["wbuf"][:, i * 2048:(i + 1) * 2048].rearrange(
            "p (t q) -> p t q", t=16)

    def wo_view():
        return p.sb["wbuf"][:, 0:8192].rearrange("p (h o q) -> p h o q", h=16, o=4)

    @block.sync
    def _(sync):
        for qt in range(4):
            ins = sync.dma_start(out=p.sb["tri_sb"][:, qt * 512:(qt + 1) * 512],
                                 in_=dram["tri"][qt * 128:(qt + 1) * 128, :])
            p.inc(ins, "a_tri", 16)
        for h in range(NH):
            if h >= 1:
                sync.wait_ge(p.sem("a_wdone"), h)
            for wi, wname in enumerate(["wqT", "wkT", "wvT"]):
                ins = sync.dma_start(
                    out=wslice(wi),
                    in_=dram[wname][:, h, :, :].rearrange("t p q -> p t q"))
                p.inc(ins, "s_whd", 16)
        sync.wait_ge(p.sem("a_wdone"), NH)
        for oc in range(4):
            if oc >= 1:
                sync.wait_ge(p.sem("a_ao"), oc * 8)
            for j in range(4):
                ins = sync.dma_start(
                    out=wo_view()[:, :, j, :],
                    in_=dram["woT"][:, oc * 4 + j, :, :].rearrange(
                        "h p q -> p h q"))
                p.inc(ins, "s_wo", 16)

    def emit_attn_tensor(t, parity):
        qg = q512_groups(parity)
        for nm in ["a_pj", "a_pje", "a_tp", "a_tpe", "a_sc", "a_sce",
                   "a_zb", "a_ze", "a_prn", "a_nm", "a_pr", "a_ao", "a_aoe",
                   "a_wdone", "s_whd", "s_wo"]:
            p.sem(nm)
        for h in range(NH):
            t.wait_ge(p.sem("s_whd"), 48 * (h + 1))
            for wi, dstbuf in [(1, "kT"), (2, "vT")]:
                for kc in range(4):
                    ps = p.ps["ps_b"] if p.cnt["a_pj"] % 2 == 0 else p.ps["ps_c"]
                    if p.cnt["a_pj"] >= 2:
                        t.wait_ge(p.sem("a_pje"), p.cnt["a_pj"] - 1)
                    for ht in range(ND):
                        ins = t.matmul(ps[:, 0:512], wslice(wi)[:, ht, :],
                                       p.sb["allT"][:, ht, kc * 512:(kc + 1) * 512],
                                       start=(ht == 0), stop=(ht == ND - 1))
                    p.inc(ins, "a_pj", 1)
            for tc in range(2):
                g = qg[tc]
                ps = p.ps["ps_b"] if p.cnt["a_pj"] % 2 == 0 else p.ps["ps_c"]
                if p.cnt["a_pj"] >= 2:
                    t.wait_ge(p.sem("a_pje"), p.cnt["a_pj"] - 1)
                for ht in range(ND):
                    ins = t.matmul(ps[:, 0:512], wslice(0)[:, ht, :],
                                   p.sb["allT"][:, ht, g * 512:(g + 1) * 512],
                                   start=(ht == 0), stop=(ht == ND - 1))
                p.inc(ins, "a_pj", 1)
            t.drain()
            ins = t.nop()
            p.inc(ins, "a_wdone", 1)
            t.wait_ge(p.sem("a_pje"), 10 * h + 8)
            for g4 in range(4):
                ps = p.ps["ps_t"] if p.cnt["a_tp"] % 2 == 0 else p.ps["ps_a"]
                if p.cnt["a_tp"] >= 2:
                    t.wait_ge(p.sem("a_tpe"), p.cnt["a_tp"] - 1)
                for j in range(4):
                    kt = g4 * 4 + j
                    ins = t.transpose(ps[:, j * 128:(j + 1) * 128],
                                      p.sb["vT"][:, kt * 128:(kt + 1) * 128],
                                      p.sb["idb"][:])
                p.inc(ins, "a_tp", 1)
            t.wait_ge(p.sem("a_pje"), 10 * h + 10)
            for gl in range(2):
                g = qg[gl]
                nkc = g + 1
                for qt in range(4):
                    for kc in range(nkc):
                        m = p.cnt["a_sc"]
                        if m >= 1:
                            t.wait_ge(p.sem("a_sce"), m)
                        ins = t.matmul(
                            p.ps["ps_s"][:, 0:512],
                            p.sb["qTh"][:, (gl * 512 + qt * 128):(gl * 512 + qt * 128 + 128)],
                            p.sb["kT"][:, kc * 512:(kc + 1) * 512],
                            start=True, stop=True)
                        p.inc(ins, "a_sc", 1)
                t.wait_ge(p.sem("a_prn"), h * 8 + gl * 4 + 4)
                zgrp = h * 2 + gl
                if zgrp >= 1:
                    t.wait_ge(p.sem("a_ze"), zgrp)
                nkt = nkc * 4
                for kt in range(nkt):
                    ps = p.ps["ps_t"] if p.cnt["a_tp"] % 2 == 0 else p.ps["ps_a"]
                    if p.cnt["a_tp"] >= 2:
                        t.wait_ge(p.sem("a_tpe"), p.cnt["a_tp"] - 1)
                    for qt in range(4):
                        ins = t.transpose(ps[:, qt * 128:(qt + 1) * 128],
                                          p.sb["prb"][:, qt, kt * 128:(kt + 1) * 128],
                                          p.sb["idb"][:])
                    p.inc(ins, "a_tp", 1)
                    t.wait_ge(p.sem("a_tpe"), p.cnt["a_tp"])
                    for qt in range(4):
                        ins = t.matmul(p.ps["ps_z"][:, qt * 128:(qt + 1) * 128],
                                       p.sb["vh"][:, kt, :],
                                       p.sb["atT"][:, qt, :],
                                       start=(kt == 0 and qt == 0),
                                       stop=(kt == nkt - 1 and qt == 3))
                    p.inc(ins, "a_zb", 1)
        for oc in range(4):
            t.wait_ge(p.sem("s_wo"), 64 * (oc + 1))
            for tc8 in range(8):
                ps = p.ps["ps_b"] if p.cnt["a_ao"] % 2 == 0 else p.ps["ps_c"]
                if p.cnt["a_ao"] >= 2:
                    t.wait_ge(p.sem("a_aoe"), p.cnt["a_ao"] - 1)
                for h in range(NH):
                    ins = t.matmul(ps[:, 0:512],
                                   p.sb["zT"][:, h, tc8 * 128:(tc8 + 1) * 128],
                                   wo_view()[:, h, :, :],
                                   start=(h == 0), stop=(h == NH - 1))
                p.inc(ins, "a_ao", 1)

    @block.tensor
    def _(t):
        p.branch_parity(t, "tpar", lambda par: emit_attn_tensor(t, par))

    def emit_attn_scalar(scl, parity):
        qg = q512_groups(parity)
        zb = 0
        tp0 = 0
        att_cum = 0
        for h in range(NH):
            base = 10 * h
            for kc in range(4):
                scl.wait_ge(p.sem("a_pj"), base + kc + 1)
                ps = p.ps["ps_b"] if (base + kc) % 2 == 0 else p.ps["ps_c"]
                ins = scl.copy(p.sb["kT"][:, kc * 512:(kc + 1) * 512], ps[:, 0:512])
                p.inc(ins, "a_pje", 1)
            for kc in range(4):
                scl.wait_ge(p.sem("a_pj"), base + 4 + kc + 1)
                ps = p.ps["ps_b"] if (base + 4 + kc) % 2 == 0 else p.ps["ps_c"]
                ins = scl.copy(p.sb["vT"][:, kc * 512:(kc + 1) * 512], ps[:, 0:512])
                p.inc(ins, "a_pje", 1)
            for tc in range(2):
                scl.wait_ge(p.sem("a_pj"), base + 8 + tc + 1)
                ps = p.ps["ps_b"] if (base + 8 + tc) % 2 == 0 else p.ps["ps_c"]
                ins = scl.copy(p.sb["qTh"][:, tc * 512:(tc + 1) * 512], ps[:, 0:512])
                p.inc(ins, "a_pje", 1)
            for gl in range(2):
                g = qg[gl]
                w = (g + 1) * 512
                for qt in range(4):
                    scl.wait_ge(p.sem("a_nm"), h * 8 + gl * 4 + qt + 1)
                    if att_cum > 0:
                        scl.wait_ge(p.sem("a_tp"), tp0 + att_cum)
                    ins = scl.activation(p.sb["prb"][:, qt, 0:w],
                                         p.sb["sqt"][:, 0:w],
                                         AF.Exp, scale=p.sb["cstt"][:, 0:1],
                                         bias=p.sb["stt"][:, 10:11],
                                         accum_out=p.sb["stt"][:, 11:12])
                    p.inc(ins, "a_pr", 1)
                zb += (g + 1) * 4
                scl.wait_ge(p.sem("a_zb"), zb)
                ins = scl.copy(p.sb["zT"][:, h, gl * 512:(gl + 1) * 512],
                               p.ps["ps_z"][:, 0:512])
                p.inc(ins, "a_ze", 1)
                att_cum = (h * 24 + 4 +
                           sum((qg[x] + 1) * 4 for x in range(gl + 1)))

    @block.scalar
    def _(scl):
        p.branch_parity(scl, "spar", lambda par: emit_attn_scalar(scl, par))

    def emit_attn_vector(v, parity):
        qg = q512_groups(parity)
        p.sem("a_tri")
        v.wait_ge(p.sem("a_tri"), 64)
        for h in range(NH):
            for g4 in range(4):
                v.wait_ge(p.sem("a_tp"), 24 * h + g4 + 1)
                ps = p.ps["ps_t"] if (24 * h + g4) % 2 == 0 else p.ps["ps_a"]
                ins = v.tensor_copy(p.sb["vh"][:, g4 * 4:(g4 + 1) * 4, :],
                                    ps[:, 0:512])
                p.inc(ins, "a_tpe", 1)
            for gl in range(2):
                g = qg[gl]
                nkc = g + 1
                for qt in range(4):
                    iq = h * 8 + gl * 4 + qt
                    if iq >= 1:
                        v.wait_ge(p.sem("a_pr"), iq)
                    for kc in range(nkc):
                        v.wait_ge(p.sem("a_sc"), p.cnt["a_sce"] + 1)
                        ins = v.tensor_copy(p.sb["sqt"][:, kc * 512:(kc + 1) * 512],
                                            p.ps["ps_s"][:, 0:512])
                        p.inc(ins, "a_sce", 1)
                    v.drain()
                    v.tensor_tensor(p.sb["sqt"][:, g * 512:(g + 1) * 512],
                                    p.sb["sqt"][:, g * 512:(g + 1) * 512],
                                    p.sb["tri_sb"][:, qt * 512:(qt + 1) * 512],
                                    ALU.add)
                    v.drain()
                    v.reduce_max(p.sb["stt"][:, 9:10], p.sb["sqt"][:, 0:nkc * 512],
                                 AX.X)
                    v.drain()
                    ins = v.tensor_scalar(p.sb["stt"][:, 10:11],
                                          p.sb["stt"][:, 9:10],
                                          -INV_SQRT_HD, None, ALU.mult)
                    p.inc(ins, "a_nm", 1)
                    v.wait_ge(p.sem("a_pr"), iq + 1)
                    v.reciprocal(p.sb["stt"][:, 9:10], p.sb["stt"][:, 11:12])
                    v.drain()
                    ins = v.tensor_scalar(p.sb["prb"][:, qt, 0:nkc * 512],
                                          p.sb["prb"][:, qt, 0:nkc * 512],
                                          p.sb["stt"][:, 9:10], None, ALU.mult)
                    p.inc(ins, "a_prn", 1)
                nkt = nkc * 4
                for kt in range(nkt):
                    v.wait_ge(p.sem("a_tp"), p.cnt["a_tpe"] + 1)
                    nzb_done = p.cnt.setdefault("_v_zbcum", 0)
                    if nzb_done >= 1:
                        v.wait_ge(p.sem("a_zb"), nzb_done)
                    p.cnt["_v_zbcum"] = nzb_done + 1
                    ps = p.ps["ps_t"] if p.cnt["a_tpe"] % 2 == 0 else p.ps["ps_a"]
                    ins = v.tensor_copy(p.sb["atT"][:, :, :], ps[:, 0:512])
                    p.inc(ins, "a_tpe", 1)
        for oc in range(4):
            for tc8 in range(8):
                k = oc * 8 + tc8
                v.wait_ge(p.sem("a_ao"), k + 1)
                ps = p.ps["ps_b"] if k % 2 == 0 else p.ps["ps_c"]
                ins = v.tensor_tensor(p.sb["res"][:, tc8, oc * 512:(oc + 1) * 512],
                                      p.sb["res"][:, tc8, oc * 512:(oc + 1) * 512],
                                      ps[:, 0:512], ALU.add)
                p.inc(ins, "a_aoe", 1)

    @block.vector
    def _(v):
        p.branch_parity(v, "vpar", lambda par: emit_attn_vector(v, par))




def phase_l0_ffn(p):
    dram, block = p.dram, p.block
    NIC = I0 // 512
    p.sbuf("xnf", [128, D], F32, local=True)
    p.sbuf("xqb", [128, D], BF16, local=True)
    p.sbuf("xqi", [128, D], I32, local=True)
    p.sbuf("xqT", [128, ND, TOK], BF16, local=True)
    p.sbuf("w1g", [128, ND, 512], BF16, local=True)
    p.sbuf("w1u", [128, ND, 512], BF16, local=True)
    p.sbuf("gsb", [128, 512], F32, local=True)
    p.sbuf("hsp", [128, 2, 512], F32, local=True)

    @block.vector
    def _(v):
        for n in range(NT_OWN):
            emit_layernorm_vec(p, v, p.sb["res"][:, n, :], p.sb["xnf"][:], "f0", n)
            st = p.sb["stt"]
            v.drain()
            v.reduce_max(st[:, 9:10], p.sb["xnf"][:], AX.X, apply_absolute_value=True)
            v.drain()
            v.tensor_scalar(st[:, 10:11], st[:, 9:10], 1.0 / 127.0, 1e-8,
                            ALU.mult, ALU.max)
            v.drain()
            v.reciprocal(st[:, 11:12], st[:, 10:11])
            v.drain()
            ins = v.nop()
            p.inc(ins, "f_vq", 1)
            v.wait_ge(p.sem("f_sq"), n + 1)
            if n >= 1:
                v.wait_ge(p.sem("f_tp"), n * 4)
            ins = v.tensor_scalar(p.sb["xqb"][:], p.sb["xqi"][:], 512.0,
                                  st[:, 10:11], ALU.subtract, ALU.mult)
            p.inc(ins, "f_xqb", 1)
            for g4 in range(4):
                k = n * 4 + g4
                v.wait_ge(p.sem("f_tp"), k + 1)
                ps = p.ps["ps_t"] if k % 2 == 0 else p.ps["ps_a"]
                ins = v.tensor_copy(
                    p.sb["xqT"][:, g4 * 4:(g4 + 1) * 4, n * 128:(n + 1) * 128],
                    ps[:, 0:512])
                p.inc(ins, "f_tpe", 1)

    @block.scalar
    def _(s):
        for n in range(NT_OWN):
            emit_layernorm_sca(p, s, p.sb["res"][:, n, :], "f0", n)
            s.wait_ge(p.sem("f_vq"), n + 1)
            s.activation(p.sb["sqt"][:], p.sb["xnf"][:], AF.Identity,
                         scale=p.sb["stt"][:, 11:12], bias=p.sb["qsh"][:, 0:1])
            s.drain()
            ins = s.copy(p.sb["xqi"][:], p.sb["sqt"][:])
            p.inc(ins, "f_sq", 1)

    @block.tensor
    def _(t):
        for n in range(NT_OWN):
            t.wait_ge(p.sem("f_xqb"), n + 1)
            for g4 in range(4):
                k = n * 4 + g4
                ps = p.ps["ps_t"] if k % 2 == 0 else p.ps["ps_a"]
                if k >= 2:
                    t.wait_ge(p.sem("f_tpe"), k - 1)
                for j in range(4):
                    dt = g4 * 4 + j
                    ins = t.transpose(ps[:, j * 128:(j + 1) * 128],
                                      p.sb["xqb"][:, dt * 128:(dt + 1) * 128],
                                      p.sb["idb"][:])
                p.inc(ins, "f_tp", 1)

    @block.sync
    def _(sync):
        for ic in range(NIC):
            if ic >= 1:
                sync.wait_ge(p.sem("f_gu"), ic * NT_OWN)
            for ht in range(ND):
                ins = sync.dma_start(
                    out=p.sb["w1g"][:, ht, :].rearrange("pp (o q) -> pp o q", o=4),
                    in_=dram["w1qT"][ht, ic * 4:(ic + 1) * 4, :, :]
                    .rearrange("o pp q -> pp o q"))
                p.inc(ins, "f_w1", 16)
                ins = sync.dma_start(
                    out=p.sb["w1u"][:, ht, :].rearrange("pp (o q) -> pp o q", o=4),
                    in_=dram["w1qT"][ht, 44 + ic * 4:44 + (ic + 1) * 4, :, :]
                    .rearrange("o pp q -> pp o q"))
                p.inc(ins, "f_w1", 16)

    @block.tensor
    def _(t):
        t.wait_ge(p.sem("f_tpe"), NT_OWN * 4)
        for ic in range(NIC):
            t.wait_ge(p.sem("f_w1"), (ic + 1) * 32 * 16)
            for tt in range(NT_OWN):
                k = ic * NT_OWN + tt
                if k >= 1:
                    t.wait_ge(p.sem("f_si"), k)
                    t.wait_ge(p.sem("f_h"), k)
                for ht in range(ND):
                    ins = t.matmul(p.ps["ps_b"][:, 0:512],
                                   p.sb["xqT"][:, ht, tt * 128:(tt + 1) * 128],
                                   p.sb["w1g"][:, ht, :],
                                   start=(ht == 0), stop=(ht == ND - 1))
                p.inc(ins, "f_gg", 1)
                for ht in range(ND):
                    ins = t.matmul(p.ps["ps_c"][:, 0:512],
                                   p.sb["xqT"][:, ht, tt * 128:(tt + 1) * 128],
                                   p.sb["w1u"][:, ht, :],
                                   start=(ht == 0), stop=(ht == ND - 1))
                p.inc(ins, "f_gu", 1)

    @block.scalar
    def _(s):
        for ic in range(NIC):
            for tt in range(NT_OWN):
                k = ic * NT_OWN + tt
                s.wait_ge(p.sem("f_gg"), k + 1)
                ins = s.activation(p.sb["gsb"][:], p.ps["ps_b"][:, 0:512], AF.Silu)
                p.inc(ins, "f_si", 1)

    @block.vector
    def _(v):
        v.memset(p.sb["am"][:, 0:10], 0.0)
        v.drain()
        for ic in range(NIC):
            for tt in range(NT_OWN):
                k = ic * NT_OWN + tt
                v.wait_ge(p.sem("f_si"), k + 1)
                v.wait_ge(p.sem("f_gu"), k + 1)
                if k >= 2:
                    v.wait_ge(p.sem("f_hsp"), (k - 1) * 16)
                ins = v.tensor_tensor(p.sb["hsp"][:, k % 2, :], p.sb["gsb"][:],
                                      p.ps["ps_c"][:, 0:512], ALU.mult)
                p.inc(ins, "f_h", 1)
                v.drain()
                v.reduce_max(p.sb["stt"][:, 9:10], p.sb["hsp"][:, k % 2, :],
                             AX.X, apply_absolute_value=True)
                v.drain()
                v.tensor_tensor(p.sb["am"][:, tt:tt + 1], p.sb["am"][:, tt:tt + 1],
                                p.sb["stt"][:, 9:10], ALU.max)
                v.drain()
                ins = v.nop()
                p.inc(ins, "f_am", 1)

    @block.gpsimd
    def _(g):
        for ic in range(NIC):
            for tt in range(NT_OWN):
                k = ic * NT_OWN + tt
                g.wait_ge(p.sem("f_h"), k + 1)
                g.wait_ge(p.sem("f_am"), k + 1)
                ins = g.dma_start(out=dram["h_spill"][tt, ic, :, :],
                                  in_=p.sb["hsp"][:, k % 2, :])
                p.inc(ins, "f_hsp", 16)
        g.wait_ge(p.sem("f_hsp"), p.val("f_hsp"))
        for tname in ("xqT", "am"):
            if tname in p.taps:
                ins = g.dma_start(out=dram["tap_" + tname], in_=p.sb[tname][:])
                p.inc(ins, "f_tap", 16)
        if p.val("f_tap"):
            g.wait_ge(p.sem("f_tap"), p.val("f_tap"))



def phase_l0_ffn_b(p):
    dram, block = p.dram, p.block
    NIC = I0 // 512
    p.sbuf("hld", [128, 2, 512], F32, local=True)
    p.sbuf("hyb", [128, 512], F32, local=True)
    p.sbuf("hqb", [128, 512], BF16, local=True)
    p.sbuf("hqi", [128, 512], I32, local=True)
    p.sbuf("hqT", [128, 44, 128], BF16, local=True)
    p.sbuf("w2r", [128, 2, 16, 128], BF16, local=True)
    @block.sync
    def _(sync):
        for tt in range(NT_OWN):
            for ic in range(NIC):
                k = tt * NIC + ic
                if k >= 2:
                    sync.wait_ge(p.sem("f_hq"), k - 1)
                ins = sync.dma_start(out=p.sb["hld"][:, k % 2, :],
                                     in_=dram["h_spill"][tt, ic, :, :])
                p.inc(ins, "f_hl", 16)
            for it in range(44):
                kk = tt * 44 + it
                if kk >= 2:
                    prev2 = kk - 2
                    tt2, it2 = divmod(prev2, 44)
                    if it2 == 43:
                        sync.wait_ge(p.sem("f_dn"), (tt2 + 1) * 4)
                    else:
                        sync.wait_ge(p.sem("f_dnw"), prev2 + 1 - tt2)
                ins = sync.dma_start(out=p.sb["w2r"][:, kk % 2, :, :],
                                     in_=dram["w2qT"][it, :, :, :]
                                     .rearrange("o pp q -> pp o q"))
                p.inc(ins, "f_w2", 16)

    @block.vector
    def _(v):
        st = p.sb["stt"]
        for tt in range(NT_OWN):
            v.tensor_scalar(st[:, 10:11], p.sb["am"][:, tt:tt + 1], 1.0 / 127.0,
                            1e-8, ALU.mult, ALU.max)
            v.drain()
            v.reciprocal(st[:, 11:12], st[:, 10:11])
            v.drain()
            ins = v.nop()
            p.inc(ins, "f_bs", 1)
            for ic in range(NIC):
                k = tt * NIC + ic
                v.wait_ge(p.sem("f_hq"), k + 1)
                if k >= 1:
                    v.wait_ge(p.sem("f_htp"), k)
                ins = v.tensor_scalar(p.sb["hqb"][:], p.sb["hqi"][:], 512.0,
                                      st[:, 10:11], ALU.subtract, ALU.mult)
                p.inc(ins, "f_hqb", 1)
                v.wait_ge(p.sem("f_htp"), k + 1)
                ps = p.ps["ps_t"] if k % 2 == 0 else p.ps["ps_a"]
                ins = v.tensor_copy(p.sb["hqT"][:, ic * 4:(ic + 1) * 4, :],
                                    ps[:, 0:512])
                p.inc(ins, "f_hte", 1)
            for oc in range(4):
                v.wait_ge(p.sem("f_dn"), tt * 4 + oc + 1)
                psd = [p.ps["ps_b"], p.ps["ps_c"], p.ps["ps_s"], p.ps["ps_z"]][oc]
                ins = v.tensor_tensor(p.sb["res"][:, tt, oc * 512:(oc + 1) * 512],
                                      p.sb["res"][:, tt, oc * 512:(oc + 1) * 512],
                                      psd[:, 0:512], ALU.add)
                p.inc(ins, "f_res", 1)

    @block.scalar
    def _(s):
        for tt in range(NT_OWN):
            s.wait_ge(p.sem("f_bs"), tt + 1)
            for ic in range(NIC):
                k = tt * NIC + ic
                s.wait_ge(p.sem("f_hl"), (k + 1) * 16)
                s.activation(p.sb["hyb"][:], p.sb["hld"][:, k % 2, :], AF.Identity,
                             scale=p.sb["stt"][:, 11:12], bias=p.sb["qsh"][:, 0:1])
                s.drain()
                ins = s.copy(p.sb["hqi"][:], p.sb["hyb"][:])
                p.inc(ins, "f_hq", 1)

    @block.tensor
    def _(t):
        for tt in range(NT_OWN):
            for ic in range(NIC):
                k = tt * NIC + ic
                t.wait_ge(p.sem("f_hqb"), k + 1)
                ps = p.ps["ps_t"] if k % 2 == 0 else p.ps["ps_a"]
                if k >= 2:
                    t.wait_ge(p.sem("f_hte"), k - 1)
                for j in range(4):
                    ins = t.transpose(ps[:, j * 128:(j + 1) * 128],
                                      p.sb["hqb"][:, j * 128:(j + 1) * 128],
                                      p.sb["idb"][:])
                p.inc(ins, "f_htp", 1)
            t.wait_ge(p.sem("f_hte"), (tt + 1) * NIC)
            if tt >= 1:
                t.wait_ge(p.sem("f_res"), tt * 4)
            for it in range(44):
                kk = tt * 44 + it
                t.wait_ge(p.sem("f_w2"), (kk + 1) * 16)
                for oc in range(4):
                    psd = [p.ps["ps_b"], p.ps["ps_c"], p.ps["ps_s"], p.ps["ps_z"]][oc]
                    ins = t.matmul(psd[:, 0:512],
                                   p.sb["hqT"][:, it, :],
                                   p.sb["w2r"][:, kk % 2, oc * 4:(oc + 1) * 4, :],
                                   start=(it == 0), stop=(it == 43))
                    if it == 43:
                        p.inc(ins, "f_dn", 1)
                    elif oc == 3:
                        p.inc(ins, "f_dnw", 1)

    @block.gpsimd
    def _(g):
        if "hqT" in p.taps:
            g.wait_ge(p.sem("f_dn"), NT_OWN * 4)
            ins = g.dma_start(out=dram["tap_hqT"], in_=p.sb["hqT"][:])
            p.inc(ins, "f_tap2", 16)
            g.wait_ge(p.sem("f_tap2"), p.val("f_tap2"))



def emit_flag_write(p, g, slot, sem):
    g.memset(p.sb["mgw"][:, 0:1], MAGIC0)
    g.memset(p.sb["mgw"][:, 1:2], MAGIC1)
    g.drain()
    ins = g.dma_start(out=p.dram["sh_flag"][slot:slot + 1, :], in_=p.sb["mgw"][0:1, :])
    p.inc(ins, sem, 16)
    g.wait_ge(p.sem(sem), p.val(sem))


_POLL_ID = [0]


def emit_flag_poll(p, g, slot, sem):
    _POLL_ID[0] += 1
    u = str(_POLL_ID[0])
    with (
        g.register("fl0_" + sem + u) as f0,
        g.register("fl1_" + sem + u) as f1,
        g.register("cc_" + sem + u) as c,
        g.register("wt_" + sem + u) as wt,
    ):
        g.reg_mov(c, 1)
        g.reg_mov(wt, p.val(sem))
        with g.While(c):
            ins = g.dma_start(out=p.sb["mgr"][0:1, :],
                              in_=p.dram["sh_flag"][slot:slot + 1, :])
            ins.then_inc(p.sem(sem), 16)
            g.reg_add(wt, wt, 16)
            g.wait_ge(p.sem(sem), wt)
            g.reg_load(f0, p.sb["mgr"][0:1, 0:1])
            g.reg_load(f1, p.sb["mgr"][0:1, 1:2])
            g.reg_alu(c, f0, MAGIC0, ALU.not_equal)
            g.reg_alu(f1, f1, MAGIC1, ALU.not_equal)
            g.reg_alu(c, c, f1, ALU.bitwise_or)
    p.cnt[sem] = -10 ** 9



def phase_l1_norm(p):
    dram, block = p.dram, p.block
    p.sbuf("xnf1", [128, D], F32, local=True)
    p.sbuf("xnb2", [128, D], BF16, local=True)
    p.sbuf("nwr", [128, D], F32, local=True)
    p.sbuf("nwrow", [1, D], F32, local=True)
    p.sbuf("nwrowb", [1, D], BF16, local=True)
    p.sbuf("onesb", [1, 128], BF16, local=True)

    def gmap(par):
        return [0, 1, 2, 3, 12, 13, 14, 15] if par == 0 else list(range(4, 12))

    @block.sync
    def _(sync):
        ins = sync.dma_start(out=p.sb["nwrow"][0:1, :], in_=dram["norm_w"])
        p.inc(ins, "n_ld", 16)

    @block.vector
    def _(v):
        v.wait_ge(p.sem("n_ld"), 16)
        v.memset(p.sb["onesb"][:], 1.0)
        v.drain()
        ins = v.tensor_copy(p.sb["nwrowb"][0:1, :], p.sb["nwrow"][0:1, :])
        p.inc(ins, "n_nw", 1)

    @block.tensor
    def _(t):
        t.wait_ge(p.sem("n_nw"), 1)
        for ch in range(4):
            ins = t.matmul(p.ps["ps_b"][:, 0:512], p.sb["onesb"][0:1, :],
                           p.sb["nwrowb"][0:1, ch * 512:(ch + 1) * 512],
                           start=True, stop=True)
            p.inc(ins, "n_rep", 1)
            t.wait_ge(p.sem("n_repe"), ch + 1)

    @block.vector
    def _(v):
        for ch in range(4):
            v.wait_ge(p.sem("n_rep"), ch + 1)
            ins = v.tensor_copy(p.sb["nwr"][:, ch * 512:(ch + 1) * 512],
                                p.ps["ps_b"][:, 0:512])
            p.inc(ins, "n_repe", 1)

    @block.scalar
    def _(s):
        st = p.sb["stt"]
        for n in range(NT_OWN):
            emit_layernorm_sca(p, s, p.sb["res"][:, n, :], "l1", n)
            s.wait_ge(p.sem("n_msq"), n + 1)
            ins = s.activation(st[:, 10:11], st[:, 9:10], AF.Sqrt)
            p.inc(ins, "n_sq2", 1)

    def emit_norm_vec(v, par):
        gm = gmap(par)
        st = p.sb["stt"]
        for nm in ["n_tp", "n_ev", "n_msq", "n_sq2", "n_xb"]:
            p.sem(nm)
        for n in range(NT_OWN):
            emit_layernorm_vec(p, v, p.sb["res"][:, n, :], p.sb["xnf1"][:], "l1", n)
            v.drain()
            v.tensor_tensor(st[:, 9:10], st[:, 8:9], st[:, 8:9], ALU.mult)
            v.drain()
            v.tensor_tensor(st[:, 9:10], st[:, 9:10], st[:, 5:6], ALU.mult)
            v.drain()
            ins = v.tensor_scalar(st[:, 9:10], st[:, 9:10], 1e-5, None, ALU.add)
            p.inc(ins, "n_msq", 1)
            v.wait_ge(p.sem("n_sq2"), n + 1)
            v.reciprocal(st[:, 11:12], st[:, 10:11])
            v.drain()
            v.tensor_scalar(p.sb["xnf1"][:], p.sb["xnf1"][:], st[:, 11:12], None,
                            ALU.mult)
            v.drain()
            ins = v.tensor_tensor(p.sb["xnb2"][:], p.sb["xnf1"][:], p.sb["nwr"][:],
                                  ALU.mult)
            p.inc(ins, "n_xb", 1)
            gt = gm[n]
            for g4 in range(4):
                k = n * 4 + g4
                v.wait_ge(p.sem("n_tp"), k + 1)
                ps = p.ps["ps_t"] if k % 2 == 0 else p.ps["ps_a"]
                dst = p.sb["allT"][:, g4 * 4:(g4 + 1) * 4, gt * 128:(gt + 1) * 128]
                ins = v.tensor_copy(dst, ps[:, 0:512])
                p.inc(ins, "n_ev", 1)

    @block.tensor
    def _(t):
        for n in range(NT_OWN):
            t.wait_ge(p.sem("n_xb"), n + 1)
            for g4 in range(4):
                k = n * 4 + g4
                ps = p.ps["ps_t"] if k % 2 == 0 else p.ps["ps_a"]
                if k >= 2:
                    t.wait_ge(p.sem("n_ev"), k - 1)
                for j in range(4):
                    dt = g4 * 4 + j
                    ins = t.transpose(ps[:, j * 128:(j + 1) * 128],
                                      p.sb["xnb2"][:, dt * 128:(dt + 1) * 128],
                                      p.sb["idb"][:])
                p.inc(ins, "n_tp", 1)

    @block.vector
    def _(v):
        p.branch_parity(v, "nvpar", lambda par: emit_norm_vec(v, par))

    def emit_exch(g, par):
        p.sem("n_ev")
        rngs = own_rows(par)
        prngs = own_rows(1 - par)
        g.wait_ge(p.sem("n_ev"), NT_OWN * 4)
        nd = 0
        for (r0, r1) in rngs:
            nblk = (r1 - r0) // 512
            for bq in range(nblk):
                c0 = r0 + bq * 512
                ins = g.dma_start(
                    out=dram["sh_xn2T"][:, :, c0:c0 + 512].rearrange(
                        "d p c -> p d c"),
                    in_=p.sb["allT"][:, :, c0:c0 + 512])
                p.inc(ins, "x_wr", 16)
                nd += 1
        assert nd == 2
        g.wait_ge(p.sem("x_wr"), p.val("x_wr"))
        emit_flag_write(p, g, 0 + par, "x_fw")
        emit_flag_poll(p, g, 0 + (1 - par), "x_fp")
        for (r0, r1) in prngs:
            nblk = (r1 - r0) // 512
            for bq in range(nblk):
                c0 = r0 + bq * 512
                ins = g.dma_start(
                    out=p.sb["allT"][:, :, c0:c0 + 512],
                    in_=dram["sh_xn2T"][:, :, c0:c0 + 512].rearrange(
                        "d p c -> p d c"))
                p.inc(ins, "x_rd", 16)
        g.wait_ge(p.sem("x_rd"), p.val("x_rd"))

    @block.gpsimd
    def _(g):
        p.branch_parity(g, "xgpar", lambda par: emit_exch(g, par))



def phase_l1_skipu(p):
    dram, block = p.dram, p.block
    p.sbuf("ud1a", [128, D], F32, local="group")
    p.sbuf("bct", [128, 2, D], F32, local="group")
    p.sbuf("wcol", [128, 2, 16, 128], BF16, local=True)
    p.sbuf("wsml", [128, 2, 3, 128], BF16, local=True)
    p.sbuf("upj", [128, D], F32, local=True)
    p.sbuf("ucv", [128, D], F32, local=True)
    p.sbuf("uet", [128, 2, D], BF16, local=True)
    p.sbuf("skp", [128, 2, D], BF16, local=True)
    p.sbuf("cwsb", [128, E // 128, DC], F32, local=True)
    p.sbuf("cbsb", [128, E // 128], F32, local=True)

    @block.sync
    def _(sync):
        for nm, dst in [("conv_w", "cwsb"), ("conv_b", "cbsb"),
                        ("A_own", "Asb"), ("W_D_own", "wdsb"), ("bd2_own", "bdsb")]:
            ins = sync.dma_start(out=p.sb[dst][:], in_=dram[nm])
            p.inc(ins, "k_cst", 16)
        for et in range(16):
            if et >= 2:
                sync.wait_ge(p.sem("k_skpj"), (et - 1) * 4)
            ins = sync.dma_start(
                out=p.sb["wcol"][:, et % 2, :, :],
                in_=dram["wskT"][:, et, :, :].rearrange("h pp q -> pp h q"))
            p.inc(ins, "k_wld", 16)
        for et in range(32):
            if et < 2:
                sync.wait_ge(p.sem("k_skpj"), 64)
            else:
                sync.wait_ge(p.sem("k_upj"), (et - 1) * 4)
                sync.wait_ge(p.sem("k_smp"), (et - 1) * 12)
            ins = sync.dma_start(
                out=p.sb["wcol"][:, et % 2, :, :],
                in_=dram["winT"][:, et, :, :].rearrange("h pp q -> pp h q"))
            p.inc(ins, "k_wld", 16)
            for wi, wname in enumerate(["wd1T", "wbT", "wcT"]):
                ins = sync.dma_start(out=p.sb["wsml"][:, et % 2, wi, :],
                                     in_=dram[wname][et, 0, :, :])
                p.inc(ins, "k_wld", 16)

    @block.scalar
    def _(s):
        s.wait_ge(p.sem("k_cst"), 80)
        ins = s.activation(p.sb["Asb"][:], p.sb["Asb"][:], AF.Exp)
        p.inc(ins, "k_aexp", 1)

    @block.vector
    def _(v):
        v.wait_ge(p.sem("k_aexp"), 1)
        ins = v.tensor_scalar(p.sb["Asb"][:], p.sb["Asb"][:], -1.0, None, ALU.mult)
        p.inc(ins, "k_aneg", 1)
        v.memset(p.sb["ud1a"][:], 0.0)
        ins = v.memset(p.sb["bct"][:], 0.0)
        p.inc(ins, "k_init", 1)

    @block.tensor
    def _(t):
        for et in range(16):
            t.wait_ge(p.sem("k_wld"), (et + 1) * 16)
            for ch in range(4):
                k = et * 4 + ch
                ps = p.ps["ps_b"] if k % 2 == 0 else p.ps["ps_c"]
                if k >= 2:
                    t.wait_ge(p.sem("k_ske"), k - 1)
                for ht in range(ND):
                    ins = t.matmul(ps[:, 0:512], p.sb["wcol"][:, et % 2, ht, :],
                                   p.sb["allT"][:, ht, ch * 512:(ch + 1) * 512],
                                   start=(ht == 0), stop=(ht == ND - 1))
                p.inc(ins, "k_skpj", 1)

    @block.scalar
    def _(s):
        for et in range(16):
            for ch in range(4):
                k = et * 4 + ch
                s.wait_ge(p.sem("k_skpj"), k + 1)
                if et >= 2 and ch == 0:
                    s.wait_ge(p.sem("k_sksp"), (et - 1) * 16)
                ps = p.ps["ps_b"] if k % 2 == 0 else p.ps["ps_c"]
                ins = s.activation(p.sb["skp"][:, et % 2, ch * 512:(ch + 1) * 512],
                                   ps[:, 0:512], AF.Silu)
                p.inc(ins, "k_ske", 1)

    @block.gpsimd
    def _(g):
        for et in range(16):
            g.wait_ge(p.sem("k_ske"), (et + 1) * 4)
            ins = g.dma_start(out=dram["skipT_spill"][et, :, :],
                              in_=p.sb["skp"][:, et % 2, :])
            p.inc(ins, "k_sksp", 16)

    @block.tensor
    def _(t):
        for et in range(32):
            t.wait_ge(p.sem("k_wld"), 256 + (et + 1) * 64)
            for ch in range(4):
                k = et * 4 + ch
                ps = p.ps["ps_b"] if k % 2 == 0 else p.ps["ps_c"]
                if k >= 2:
                    t.wait_ge(p.sem("k_ue"), k - 1)
                for ht in range(ND):
                    ins = t.matmul(ps[:, 0:512], p.sb["wcol"][:, et % 2, ht, :],
                                   p.sb["allT"][:, ht, ch * 512:(ch + 1) * 512],
                                   start=(ht == 0), stop=(ht == ND - 1))
                p.inc(ins, "k_upj", 1)
            t.wait_ge(p.sem("k_usil"), et + 1)
            for wi in range(3):
                for ch in range(4):
                    kk = (et * 3 + wi) * 4 + ch
                    if kk >= 1:
                        t.wait_ge(p.sem("k_sme"), kk)
                    ins = t.matmul(p.ps["ps_s"][:, 0:512],
                                   p.sb["wsml"][:, et % 2, wi, :],
                                   p.sb["uet"][:, et % 2, ch * 512:(ch + 1) * 512],
                                   start=True, stop=True)
                    p.inc(ins, "k_smp", 1)

    @block.vector
    def _(v):
        for et in range(32):
            for ch in range(4):
                k = et * 4 + ch
                v.wait_ge(p.sem("k_upj"), k + 1)
                ins = v.tensor_copy(p.sb["upj"][:, ch * 512:(ch + 1) * 512],
                                    p.ps["ps_b" if k % 2 == 0 else "ps_c"][:, 0:512])
                p.inc(ins, "k_ue", 1)
            v.drain()
            v.tensor_scalar(p.sb["ucv"][:], p.sb["upj"][:],
                            p.sb["cwsb"][:, et, 3:4], None, ALU.mult)
            v.drain()
            for kk in range(3):
                sh = 3 - kk
                v.scalar_tensor_tensor(p.sb["ucv"][:, sh:D],
                                       p.sb["upj"][:, 0:D - sh],
                                       p.sb["cwsb"][:, et, kk:kk + 1],
                                       p.sb["ucv"][:, sh:D], ALU.mult, ALU.add)
                v.drain()
            ins = v.nop()
            p.inc(ins, "k_ucv", 1)
            for wi, dst in [(0, None), (1, 0), (2, 1)]:
                for ch in range(4):
                    kk = (et * 3 + wi) * 4 + ch
                    v.wait_ge(p.sem("k_smp"), kk + 1)
                    if wi == 0:
                        dd = p.sb["ud1a"][:, ch * 512:(ch + 1) * 512]
                    else:
                        dd = p.sb["bct"][:, dst, ch * 512:(ch + 1) * 512]
                    ins = v.tensor_tensor(dd, dd, p.ps["ps_s"][:, 0:512], ALU.add)
                    p.inc(ins, "k_sme", 1)

    @block.scalar
    def _(s):
        for et in range(32):
            s.wait_ge(p.sem("k_ucv"), et + 1)
            if et >= 2:
                s.wait_ge(p.sem("k_usp"), (et - 1) * 16)
                s.wait_ge(p.sem("k_smp"), (et - 1) * 12)
            ins = s.activation(p.sb["uet"][:, et % 2, :], p.sb["ucv"][:], AF.Silu,
                               bias=p.sb["cbsb"][:, et:et + 1])
            p.inc(ins, "k_usil", 1)

    @block.gpsimd
    def _(g):
        for et in range(32):
            g.wait_ge(p.sem("k_usil"), et + 1)
            ins = g.dma_start(out=dram["uT_spill"][et, :, :],
                              in_=p.sb["uet"][:, et % 2, :])
            p.inc(ins, "k_usp", 16)
        g.wait_ge(p.sem("k_usp"), p.val("k_usp"))
        g.wait_ge(p.sem("k_sksp"), p.val("k_sksp"))



def phase_l1_scan(p):
    dram, block = p.dram, p.block
    TH = T // 2
    p.sbuf("ud1b", [128, D], BF16, local=True)
    p.sbuf("bctb", [128, 2, D], BF16, local=True)
    p.sbuf("wd2s", [128, 16, 128], BF16, local=True)
    p.sbuf("dlt", [128, TH], F32, local=True)
    p.sbuf("sp1", [128, TH], F32, local=True)
    p.sbuf("sp2", [128, TH], F32, local=True)
    p.sbuf("due", [128, TH], BF16, local=True)
    p.sbuf("uets", [128, 2, TH], BF16, local=True)
    p.sbuf("dcy", [128, TH], BF16, local=True)
    p.sbuf("Xb", [128, 2, TH], BF16, local=True)
    p.sbuf("XbC", [128, 2, TH], BF16, local=True)
    p.sbuf("hbf", [128, 2, TH], BF16, local=True)
    p.sbuf("yac", [128, TH], F32, local=True)
    p.sbuf("skl", [128, 2, TH], BF16, local=True)
    p.sbuf("y2b", [128, 2, TH], BF16, local=True)
    p.sbuf("hst", [128, 256], F32, local=True)
    p.sbuf("rp16", [16, 16, 128], BF16, local=True)

    @block.sync
    def _(sync):
        ins = sync.dma_start(out=p.sb["rp16"][:], in_=dram["rep16"])
        p.inc(ins, "c_r16", 16)
        for et in range(16):
            ins = sync.dma_start(out=p.sb["wd2s"][:, et, :],
                                 in_=dram["wd2T"][0, et, :, :])
            p.inc(ins, "c_wd2", 16)
        pass

    def emit_scan_uld(sync, par):
        for nm in ["c_y2", "c_uld"]:
            p.sem(nm)
        for th in range(2):
            for et in range(16):
                k = th * 16 + et
                etg = par * 16 + et
                if k >= 2:
                    sync.wait_ge(p.sem("c_y2"), k - 1)
                ins = sync.dma_start(out=p.sb["uets"][:, k % 2, :],
                                     in_=dram["uT_spill"][etg, :,
                                                          th * TH:(th + 1) * TH])
                p.inc(ins, "c_uld", 16)
                ins = sync.dma_start(out=p.sb["skl"][:, k % 2, :],
                                     in_=dram["skipT_spill"][et, :,
                                                             th * TH:(th + 1) * TH])
                p.inc(ins, "c_uld", 16)

    @block.sync
    def _(sync):
        p.branch_parity(sync, "cspar", lambda par: emit_scan_uld(sync, par))

    @block.vector
    def _(v):
        v.tensor_copy(p.sb["ud1b"][:], p.sb["ud1a"][:])
        v.tensor_copy(p.sb["bctb"][:], p.sb["bct"][:])
        v.memset(p.sb["hst"][:], 0.0)
        v.drain()
        ins = v.nop()
        p.inc(ins, "c_prep", 1)

    @block.tensor
    def _(t):
        t.wait_ge(p.sem("c_prep"), 1)
        t.wait_ge(p.sem("c_r16"), 16)
        nrep = 0
        for th in range(2):
            t.wait_ge(p.sem("c_half"), th)
            for bc in range(2):
                for n in range(NST):
                    for ch in range(2):
                        nrep += 1
                        if nrep >= 3:
                            t.wait_ge(p.sem("c_repe"), nrep - 2)
                        ins = t.matmul(
                            p.ps["ps_s" if nrep % 2 else "ps_z"][:, 0:512],
                            p.sb["rp16"][:, n, :],
                            p.sb["bctb"][0:16, bc,
                                         th * TH + ch * 512:th * TH + (ch + 1) * 512],
                            start=True, stop=True)
                        p.inc(ins, "c_rep", 1)
            t.wait_ge(p.sem("c_wd2"), 256)
            for et in range(16):
                k = th * 16 + et
                for ch in range(2):
                    kk = k * 2 + ch
                    ps = p.ps["ps_b"] if kk % 2 == 0 else p.ps["ps_c"]
                    if kk >= 2:
                        if kk % 2 == 0:
                            t.wait_ge(p.sem("c_dle"), kk // 2)
                        else:
                            t.wait_ge(p.sem("c_sp"), (kk - 1) // 2)
                    ins = t.matmul(ps[:, 0:512], p.sb["wd2s"][:, et, :],
                                   p.sb["ud1b"][:, th * TH + ch * 512:
                                                th * TH + (ch + 1) * 512],
                                   start=True, stop=True)
                    p.inc(ins, "c_dlm", 1)

    @block.vector
    def _(v):
        nrep = 0
        for th in range(2):
            for bc in range(2):
                for n in range(NST):
                    for ch in range(2):
                        nrep += 1
                        v.wait_ge(p.sem("c_rep"), nrep)
                        dst = p.sb["allT"][:, n,
                                           bc * 1024 + ch * 512:bc * 1024 + (ch + 1) * 512]
                        ins = v.tensor_copy(
                            dst, p.ps["ps_s" if nrep % 2 else "ps_z"][:, 0:512])
                        p.inc(ins, "c_repe", 1)
            for et in range(16):
                k = th * 16 + et
                v.wait_ge(p.sem("c_sp"), k + 1)
                v.tensor_tensor(p.sb["dlt"][:], p.sb["sp1"][:], p.sb["sp2"][:],
                                ALU.add)
                v.memset(p.sb["yac"][:], 0.0)
                v.wait_ge(p.sem("c_uld"), (k + 1) * 32)
                ins = v.tensor_tensor(p.sb["due"][:], p.sb["dlt"][:],
                                      p.sb["uets"][:, k % 2, :], ALU.mult)
                p.inc(ins, "c_due", 1)
                v.nop()
                v.tensor_tensor(p.sb["Xb"][:, 0, :], p.sb["due"][:],
                                p.sb["allT"][:, 0, 0:TH], ALU.mult)
                v.nop()
                for n in range(NST):
                    idx = (k * NST + n)
                    v.wait_ge(p.sem("c_dcy"), idx + 1)
                    ins = v.tensor_tensor_scan(
                        p.sb["hbf"][:, n % 2, :], p.sb["dcy"][:],
                        p.sb["Xb"][:, n % 2, :],
                        p.sb["hst"][:, et * 16 + n:et * 16 + n + 1],
                        ALU.mult, ALU.add)
                    p.inc(ins, "c_n", 1)
                    if n > 0:
                        v.tensor_tensor(p.sb["yac"][:], p.sb["yac"][:],
                                        p.sb["XbC"][:, (n - 1) % 2, :], ALU.add)
                    if n < NST - 1:
                        v.tensor_tensor(p.sb["Xb"][:, (n + 1) % 2, :],
                                        p.sb["due"][:],
                                        p.sb["allT"][:, n + 1, 0:TH], ALU.mult)
                    v.tensor_tensor(p.sb["XbC"][:, n % 2, :],
                                    p.sb["hbf"][:, n % 2, :],
                                    p.sb["allT"][:, n, 1024:1024 + TH], ALU.mult)
                    v.tensor_copy(p.sb["hst"][:, et * 16 + n:et * 16 + n + 1],
                                  p.sb["hbf"][:, n % 2, TH - 1:TH])
                v.nop()
                v.tensor_tensor(p.sb["yac"][:], p.sb["yac"][:],
                                p.sb["XbC"][:, (NST - 1) % 2, :], ALU.add)
                v.nop()
                v.scalar_tensor_tensor(p.sb["yac"][:], p.sb["uets"][:, k % 2, :],
                                       p.sb["wdsb"][:, et:et + 1], p.sb["yac"][:],
                                       ALU.mult, ALU.add)
                if k >= 2:
                    v.wait_ge(p.sem("c_y2sp"), (k - 1) * 16)
                v.nop()
                ins = v.tensor_tensor(p.sb["y2b"][:, k % 2, :], p.sb["yac"][:],
                                      p.sb["skl"][:, k % 2, :], ALU.mult)
                p.inc(ins, "c_y2", 1)

    @block.scalar
    def _(s):
        for th in range(2):
            for et in range(16):
                k = th * 16 + et
                for ch in range(2):
                    kk = k * 2 + ch
                    s.wait_ge(p.sem("c_dlm"), kk + 1)
                    if ch == 0 and k > 0:
                        s.wait_ge(p.sem("c_due"), k)
                    ps = p.ps["ps_b"] if kk % 2 == 0 else p.ps["ps_c"]
                    sl = slice(ch * 512, (ch + 1) * 512)
                    s.activation(p.sb["dlt"][:, sl], ps[:, 0:512], AF.Identity,
                                 bias=p.sb["bdsb"][:, et:et + 1])
                    s.drain()
                    s.activation(p.sb["sp1"][:, sl], p.sb["dlt"][:, sl], AF.Relu)
                    s.activation(p.sb["sp2"][:, sl], p.sb["dlt"][:, sl], AF.Abs)
                    s.drain()
                    s.activation(p.sb["sp2"][:, sl], p.sb["sp2"][:, sl], AF.Exp,
                                 scale=p.sb["cstt"][:, 2:3])
                    s.drain()
                    ins = s.activation(p.sb["sp2"][:, sl], p.sb["sp2"][:, sl],
                                       AF.Ln, bias=p.sb["cstt"][:, 3:4])
                    if ch == 1:
                        p.inc(ins, "c_sp", 1)
                    else:
                        p.inc(ins, "c_dle", 1)
                for n in range(NST):
                    idx = (k * NST + n)
                    s.wait_ge(p.sem("c_due"), k + 1)
                    if idx >= 1:
                        s.wait_ge(p.sem("c_n"), idx)
                    ins = s.activation(p.sb["dcy"][:], p.sb["dlt"][:], AF.Exp,
                                       scale=p.sb["Asb"][:, et, n:n + 1])
                    p.inc(ins, "c_dcy", 1)

    if "bctchk" in p.taps:
        @block.gpsimd
        def _(g):
            g.wait_ge(p.sem("c_prep"), 1)
            ins = g.dma_start(out=dram["tap_bctchk"], in_=p.sb["bctb"][:])
            p.inc(ins, "g_tap", 16)
            ins = g.dma_start(out=dram["tap_ud1chk"], in_=p.sb["ud1b"][:])
            p.inc(ins, "g_tap", 16)
            g.wait_ge(p.sem("g_tap"), p.val("g_tap"))

    @block.gpsimd
    def _(g):
        for th in range(2):
            for et in range(16):
                k = th * 16 + et
                g.wait_ge(p.sem("c_y2"), k + 1)
                ins = g.dma_start(out=dram["y2_spill"][et, :, th * TH:(th + 1) * TH],
                                  in_=p.sb["y2b"][:, k % 2, :])
                p.inc(ins, "c_y2sp", 16)
            ins = g.nop()
            p.inc(ins, "c_half", 1)
        g.wait_ge(p.sem("c_y2sp"), p.val("c_y2sp"))



def phase_l1_ffn_a(p):
    dram, block = p.dram, p.block
    p.sbuf("xf", [128, D], F32, local=True)
    p.sbuf("xg", [128, D], F32, local=True)
    p.sbuf("xqb2", [128, D], BF16, local=True)
    p.sbuf("xqi2", [128, D], I32, local=True)
    p.sbuf("xfT", [128, ND, TOK], BF16, local=True)
    p.sbuf("wgc", [128, ND, 512], BF16, local=True)
    p.sbuf("wuc", [128, ND, 512], BF16, local=True)
    p.sbuf("gs2", [128, 512], F32, local=True)
    p.sbuf("hs2", [128, 2, 512], BF16, local=True)

    @block.vector
    def _(v):
        st = p.sb["stt"]
        for n in range(NT_OWN):
            emit_layernorm_vec(p, v, p.sb["res"][:, n, :], p.sb["xf"][:], "q1", n)
            v.drain()
            src, dst = "xf", "xg"
            hh = 1
            while hh < D:
                sv = p.sb[src].rearrange("p (a two b) -> p a two b", two=2, b=hh)
                dv = p.sb[dst].rearrange("p (a two b) -> p a two b", two=2, b=hh)
                v.tensor_tensor(dv[:, :, 0, :], sv[:, :, 0, :], sv[:, :, 1, :],
                                ALU.add)
                v.tensor_tensor(dv[:, :, 1, :], sv[:, :, 0, :], sv[:, :, 1, :],
                                ALU.subtract)
                v.drain()
                src, dst = dst, src
                hh *= 2
            fw = p.sb[src]
            v.tensor_scalar(fw[:], fw[:], float(1.0 / np.sqrt(D)), None, ALU.mult)
            v.drain()
            v.reduce_max(st[:, 9:10], fw[:], AX.X, apply_absolute_value=True)
            v.drain()
            v.tensor_scalar(st[:, 9:10], st[:, 9:10], 1e-5, None, ALU.max)
            v.drain()
            v.reciprocal(st[:, 10:11], st[:, 9:10])
            v.drain()
            v.tensor_scalar(st[:, 10:11], st[:, 10:11], 127.0, None, ALU.mult)
            v.tensor_scalar(st[:, 11:12], st[:, 9:10], 1.0 / 127.0, None, ALU.mult)
            v.drain()
            ins = v.nop()
            p.inc(ins, "q_vq", 1)
            v.wait_ge(p.sem("q_sq"), n + 1)
            if n >= 1:
                v.wait_ge(p.sem("q_tp"), n * 4)
            ins = v.tensor_scalar(p.sb["xqb2"][:], p.sb["xqi2"][:], 512.0,
                                  st[:, 11:12], ALU.subtract, ALU.mult)
            p.inc(ins, "q_xqb", 1)
            for g4 in range(4):
                k = n * 4 + g4
                v.wait_ge(p.sem("q_tp"), k + 1)
                ps = p.ps["ps_t"] if k % 2 == 0 else p.ps["ps_a"]
                ins = v.tensor_copy(
                    p.sb["xfT"][:, g4 * 4:(g4 + 1) * 4, n * 128:(n + 1) * 128],
                    ps[:, 0:512])
                p.inc(ins, "q_tpe", 1)

    @block.scalar
    def _(scl):
        for n in range(NT_OWN):
            emit_layernorm_sca(p, scl, p.sb["res"][:, n, :], "q1", n)
            scl.wait_ge(p.sem("q_vq"), n + 1)
            src = "xg"
            scl.activation(p.sb["sqt"][:], p.sb[src][:], AF.Identity,
                           scale=p.sb["stt"][:, 10:11], bias=p.sb["qsh"][:, 0:1])
            scl.drain()
            ins = scl.copy(p.sb["xqi2"][:], p.sb["sqt"][:])
            p.inc(ins, "q_sq", 1)

    @block.tensor
    def _(t):
        for n in range(NT_OWN):
            t.wait_ge(p.sem("q_xqb"), n + 1)
            for g4 in range(4):
                k = n * 4 + g4
                ps = p.ps["ps_t"] if k % 2 == 0 else p.ps["ps_a"]
                if k >= 2:
                    t.wait_ge(p.sem("q_tpe"), k - 1)
                for j in range(4):
                    dt = g4 * 4 + j
                    ins = t.transpose(ps[:, j * 128:(j + 1) * 128],
                                      p.sb["xqb2"][:, dt * 128:(dt + 1) * 128],
                                      p.sb["idb"][:])
                p.inc(ins, "q_tp", 1)

    NOC = I1 // 512

    @block.sync
    def _(sync):
        for oc in range(NOC):
            if oc >= 1:
                sync.wait_ge(p.sem("q_gu"), oc * NT_OWN)
            for ht in range(ND):
                ins = sync.dma_start(
                    out=p.sb["wgc"][:, ht, :].rearrange("pp (o q) -> pp o q", o=4),
                    in_=dram["wg1T"][ht, oc * 4:(oc + 1) * 4, :, :]
                    .rearrange("o pp q -> pp o q"))
                p.inc(ins, "q_w1", 16)
                ins = sync.dma_start(
                    out=p.sb["wuc"][:, ht, :].rearrange("pp (o q) -> pp o q", o=4),
                    in_=dram["wu1T"][ht, oc * 4:(oc + 1) * 4, :, :]
                    .rearrange("o pp q -> pp o q"))
                p.inc(ins, "q_w1", 16)

    @block.tensor
    def _(t):
        t.wait_ge(p.sem("q_tpe"), NT_OWN * 4)
        for oc in range(NOC):
            t.wait_ge(p.sem("q_w1"), (oc + 1) * 32 * 16)
            for tt in range(NT_OWN):
                k = oc * NT_OWN + tt
                if k >= 1:
                    t.wait_ge(p.sem("q_si"), k)
                    t.wait_ge(p.sem("q_h"), k)
                for ht in range(ND):
                    ins = t.matmul(p.ps["ps_b"][:, 0:512],
                                   p.sb["xfT"][:, ht, tt * 128:(tt + 1) * 128],
                                   p.sb["wgc"][:, ht, :],
                                   start=(ht == 0), stop=(ht == ND - 1))
                p.inc(ins, "q_gg", 1)
                for ht in range(ND):
                    ins = t.matmul(p.ps["ps_c"][:, 0:512],
                                   p.sb["xfT"][:, ht, tt * 128:(tt + 1) * 128],
                                   p.sb["wuc"][:, ht, :],
                                   start=(ht == 0), stop=(ht == ND - 1))
                p.inc(ins, "q_gu", 1)

    @block.scalar
    def _(scl):
        for oc in range(NOC):
            for tt in range(NT_OWN):
                k = oc * NT_OWN + tt
                scl.wait_ge(p.sem("q_gg"), k + 1)
                ins = scl.activation(p.sb["gs2"][:], p.ps["ps_b"][:, 0:512], AF.Silu)
                p.inc(ins, "q_si", 1)

    @block.vector
    def _(v):
        for oc in range(NOC):
            for tt in range(NT_OWN):
                k = oc * NT_OWN + tt
                v.wait_ge(p.sem("q_si"), k + 1)
                v.wait_ge(p.sem("q_gu"), k + 1)
                if k >= 2:
                    v.wait_ge(p.sem("q_hsp"), (k - 1) * 16)
                ins = v.tensor_tensor(p.sb["hs2"][:, k % 2, :], p.sb["gs2"][:],
                                      p.ps["ps_c"][:, 0:512], ALU.mult)
                p.inc(ins, "q_h", 1)

    @block.gpsimd
    def _(g):
        for oc in range(NOC):
            for tt in range(NT_OWN):
                k = oc * NT_OWN + tt
                g.wait_ge(p.sem("q_h"), k + 1)
                ins = g.dma_start(out=dram["h1_spill"][tt, oc, :, :],
                                  in_=p.sb["hs2"][:, k % 2, :])
                p.inc(ins, "q_hsp", 16)
        g.wait_ge(p.sem("q_hsp"), p.val("q_hsp"))


def phase_l1_ffn_b(p):
    dram, block = p.dram, p.block
    p.sbuf("hf", [128, I1], BF16, local=True)
    p.sbuf("hg", [128, I1], BF16, local=True)
    p.sbuf("hqb2", [128, I1], BF16, local=True)
    p.sbuf("hqi2", [128, 2048], I32, local=True)
    p.sbuf("hfT", [128, 64, 128], BF16, local=True)
    p.sbuf("wdr", [128, 2, 16, 128], BF16, local=True)
    NOC = I1 // 512

    @block.sync
    def _(sync):
        for tt in range(NT_OWN):
            for oc in range(NOC):
                k = tt * NOC + oc
                if tt >= 1:
                    sync.wait_ge(p.sem("q_hl2"), 4 * tt)
                ins = sync.dma_start(out=p.sb["hf"][:, oc * 512:(oc + 1) * 512],
                                     in_=dram["h1_spill"][tt, oc, :, :])
                p.inc(ins, "q_hl", 16)
            for it in range(64):
                kk = tt * 64 + it
                if kk >= 2:
                    prev2 = kk - 2
                    tt2, it2 = divmod(prev2, 64)
                    if it2 == 63:
                        sync.wait_ge(p.sem("q_dn"), (tt2 + 1) * 4)
                    else:
                        sync.wait_ge(p.sem("q_dnw"), prev2 + 1 - tt2)
                ins = sync.dma_start(out=p.sb["wdr"][:, kk % 2, :, :],
                                     in_=dram["wdn1T"][it, :, :, :]
                                     .rearrange("o pp q -> pp o q"))
                p.inc(ins, "q_w2", 16)

    @block.vector
    def _(v):
        st = p.sb["stt"]
        for tt in range(NT_OWN):
            v.wait_ge(p.sem("q_hl"), (tt + 1) * NOC * 16)
            src, dst = "hf", "hg"
            hh = 1
            while hh < I1:
                sv = p.sb[src].rearrange("p (a two b) -> p a two b", two=2, b=hh)
                dv = p.sb[dst].rearrange("p (a two b) -> p a two b", two=2, b=hh)
                v.tensor_tensor(dv[:, :, 0, :], sv[:, :, 0, :], sv[:, :, 1, :],
                                ALU.add)
                v.tensor_tensor(dv[:, :, 1, :], sv[:, :, 0, :], sv[:, :, 1, :],
                                ALU.subtract)
                v.drain()
                src, dst = dst, src
                hh *= 2
            fw = p.sb[src]
            v.tensor_scalar(fw[:], fw[:], float(1.0 / np.sqrt(I1)), None, ALU.mult)
            v.drain()
            v.reduce_max(st[:, 9:10], fw[:], AX.X, apply_absolute_value=True)
            v.drain()
            v.tensor_scalar(st[:, 9:10], st[:, 9:10], 1e-5, None, ALU.max)
            v.drain()
            v.reciprocal(st[:, 10:11], st[:, 9:10])
            v.drain()
            v.tensor_scalar(st[:, 10:11], st[:, 10:11], 127.0, None, ALU.mult)
            v.tensor_scalar(st[:, 11:12], st[:, 9:10], 1.0 / 127.0, None, ALU.mult)
            v.drain()
            ins = v.nop()
            p.inc(ins, "q_vq2", 1)
            for sec in range(4):
                ks = tt * 4 + sec
                v.wait_ge(p.sem("q_sq2"), ks + 1)
                ins = v.tensor_scalar(
                    p.sb["hqb2"][:, sec * 2048:(sec + 1) * 2048],
                    p.sb["hqi2"][:], 512.0, st[:, 11:12], ALU.subtract, ALU.mult)
                p.inc(ins, "q_hqb", 1)
                for g4 in range(sec * 4, (sec + 1) * 4):
                    k = tt * 16 + g4
                    v.wait_ge(p.sem("q_tp2"), k + 1)
                    ps = p.ps["ps_t"] if k % 2 == 0 else p.ps["ps_a"]
                    ins = v.tensor_copy(p.sb["hfT"][:, g4 * 4:(g4 + 1) * 4, :],
                                        ps[:, 0:512])
                    p.inc(ins, "q_hte", 1)
            for oc in range(4):
                v.wait_ge(p.sem("q_dn"), tt * 4 + oc + 1)
                psd = [p.ps["ps_b"], p.ps["ps_c"], p.ps["ps_s"], p.ps["ps_z"]][oc]
                ins = v.tensor_tensor(p.sb["res"][:, tt, oc * 512:(oc + 1) * 512],
                                      p.sb["res"][:, tt, oc * 512:(oc + 1) * 512],
                                      psd[:, 0:512], ALU.add)
                p.inc(ins, "q_res", 1)

    @block.scalar
    def _(scl):
        for tt in range(NT_OWN):
            scl.wait_ge(p.sem("q_vq2"), tt + 1)
            for sec in range(4):
                ks = tt * 4 + sec
                if ks >= 1:
                    scl.wait_ge(p.sem("q_hqb"), ks)
                src = "hg"
                scl.activation(p.sb["sqt"][:],
                               p.sb[src][:, sec * 2048:(sec + 1) * 2048],
                               AF.Identity, scale=p.sb["stt"][:, 10:11],
                               bias=p.sb["qsh"][:, 0:1])
                scl.drain()
                ins = scl.copy(p.sb["hqi2"][:], p.sb["sqt"][:])
                p.inc(ins, "q_sq2", 1)
            scl.drain()
            ins = scl.nop()
            p.inc(ins, "q_hl2", 4)

    @block.tensor
    def _(t):
        for tt in range(NT_OWN):
            for g4 in range(16):
                k = tt * 16 + g4
                t.wait_ge(p.sem("q_hqb"), tt * 4 + g4 // 4 + 1)
                ps = p.ps["ps_t"] if k % 2 == 0 else p.ps["ps_a"]
                if k >= 2:
                    t.wait_ge(p.sem("q_hte"), k - 1)
                for j in range(4):
                    ht = g4 * 4 + j
                    ins = t.transpose(ps[:, j * 128:(j + 1) * 128],
                                      p.sb["hqb2"][:, ht * 128:(ht + 1) * 128],
                                      p.sb["idb"][:])
                p.inc(ins, "q_tp2", 1)
            t.wait_ge(p.sem("q_hte"), (tt + 1) * 16)
            if tt >= 1:
                t.wait_ge(p.sem("q_res"), tt * 4)
            for it in range(64):
                kk = tt * 64 + it
                t.wait_ge(p.sem("q_w2"), (kk + 1) * 16)
                for oc in range(4):
                    psd = [p.ps["ps_b"], p.ps["ps_c"], p.ps["ps_s"], p.ps["ps_z"]][oc]
                    ins = t.matmul(psd[:, 0:512],
                                   p.sb["hfT"][:, it, :],
                                   p.sb["wdr"][:, kk % 2, oc * 4:(oc + 1) * 4, :],
                                   start=(it == 0), stop=(it == 63))
                    if it == 63:
                        p.inc(ins, "q_dn", 1)
                    elif oc == 3:
                        p.inc(ins, "q_dnw", 1)



def phase_l1_wout(p):
    dram, block = p.dram, p.block
    p.sbuf("woc", [128, 16, 4, 128], BF16, local=True)
    p.sbuf("pst", [128, 512], F32, local=True)
    p.sbuf("prd", [128, D], F32, local=True)
    p.sbuf("of32", [128, D], F32, local=True)

    @block.sync
    def _(sync):
        for et in range(16):
            ins = sync.dma_start(out=p.sb["allT"][:, et, :],
                                 in_=dram["y2_spill"][et, :, :])
            p.inc(ins, "o_y2", 16)
        for oc in range(4):
            if oc >= 1:
                sync.wait_ge(p.sem("o_mm"), oc * 16)
            for j in range(4):
                ins = sync.dma_start(
                    out=p.sb["woc"][:, :, j, :],
                    in_=dram["woutT"][:, oc * 4 + j, :, :].rearrange(
                        "h pp q -> pp h q"))
                p.inc(ins, "o_wld", 16)

    @block.tensor
    def _(t):
        t.wait_ge(p.sem("o_y2"), 256)
        for oc in range(4):
            t.wait_ge(p.sem("o_wld"), 64 * (oc + 1))
            for gt in range(16):
                k = oc * 16 + gt
                ps = p.ps["ps_b"] if k % 2 == 0 else p.ps["ps_c"]
                if k >= 2:
                    t.wait_ge(p.sem("o_ev"), k - 1)
                for et in range(16):
                    ins = t.matmul(ps[:, 0:512],
                                   p.sb["allT"][:, et, gt * 128:(gt + 1) * 128],
                                   p.sb["woc"][:, et, :, :],
                                   start=(et == 0), stop=(et == 15))
                p.inc(ins, "o_mm", 1)

    def emit_wout_evac(v, par):
        gm = ([0, 1, 2, 3, 12, 13, 14, 15] if par == 0 else list(range(4, 12)))
        pm = ([0, 1, 2, 3, 12, 13, 14, 15] if par == 1 else list(range(4, 12)))
        for nm in ["o_mm", "o_ev", "o_pst"]:
            p.sem(nm)
        for oc in range(4):
            for gt in range(16):
                k = oc * 16 + gt
                v.wait_ge(p.sem("o_mm"), k + 1)
                ps = p.ps["ps_b"] if k % 2 == 0 else p.ps["ps_c"]
                if gt in gm:
                    n = gm.index(gt)
                    ins = v.tensor_tensor(
                        p.sb["res"][:, n, oc * 512:(oc + 1) * 512],
                        p.sb["res"][:, n, oc * 512:(oc + 1) * 512],
                        ps[:, 0:512], ALU.add)
                    p.inc(ins, "o_ev", 1)
                else:
                    if k >= 1:
                        v.wait_ge(p.sem("o_pst"), p.cnt.setdefault("_o_ps", 0))
                    ins = v.tensor_copy(p.sb["pst"][:], ps[:, 0:512])
                    p.inc(ins, "o_ev", 1)
                    p.cnt["_o_ps"] = p.cnt.get("_o_ps", 0) + 1

    @block.vector
    def _(v):
        p.branch_parity(v, "ovpar", lambda par: emit_wout_evac(v, par))

    def emit_wout_ship(g, par):
        gm = ([0, 1, 2, 3, 12, 13, 14, 15] if par == 0 else list(range(4, 12)))
        pm = ([0, 1, 2, 3, 12, 13, 14, 15] if par == 1 else list(range(4, 12)))
        for nm in ["o_ev", "o_pst", "o_shw"]:
            p.sem(nm)
        nev = 0
        nship = 0
        for oc in range(4):
            for gt in range(16):
                k = oc * 16 + gt
                nev += 1
                if gt not in gm:
                    pl = pm.index(gt)
                    g.wait_ge(p.sem("o_ev"), nev)
                    ins = g.dma_start(
                        out=dram["sh_part"][1 - par, pl * 128:(pl + 1) * 128,
                                            oc * 512:(oc + 1) * 512],
                        in_=p.sb["pst"][:])
                    p.inc(ins, "o_shw", 16)
                    nship += 1
                    g.wait_ge(p.sem("o_shw"), p.val("o_shw"))
                    ins = g.nop()
                    p.inc(ins, "o_pst", 1)
        g.wait_ge(p.sem("o_ev"), 64)
        emit_flag_write(p, g, 2 + par, "o_fw")
        emit_flag_poll(p, g, 2 + (1 - par), "o_fp")
        for n in range(NT_OWN):
            if n >= 1:
                g.wait_ge(p.sem("o_add"), n)
            ins = g.dma_start(out=p.sb["prd"][:],
                              in_=dram["sh_part"][par, n * 128:(n + 1) * 128, :])
            p.inc(ins, "o_prd", 16)
            g.wait_ge(p.sem("o_addo"), n)
            if n >= 1:
                g.wait_ge(p.sem("o_out"), n * 16)
            g.wait_ge(p.sem("o_add"), n + 1)
            ins = g.dma_start(out=dram["out_own"][n * 128:(n + 1) * 128, :],
                              in_=p.sb["of32"][:])
            p.inc(ins, "o_out", 16)
        g.wait_ge(p.sem("o_out"), p.val("o_out"))

    @block.gpsimd
    def _(g):
        p.branch_parity(g, "ogpar", lambda par: emit_wout_ship(g, par))

    @block.vector
    def _(v):
        for n in range(NT_OWN):
            v.wait_ge(p.sem("o_prd"), (n + 1) * 16)
            if n >= 1:
                v.wait_ge(p.sem("o_out"), n * 16)
            ins = v.tensor_tensor(p.sb["of32"][:], p.sb["res"][:, n, :],
                                  p.sb["prd"][:], ALU.add)
            p.inc(ins, "o_add", 1)
            ins = v.nop()
            p.inc(ins, "o_addo", 1)

_TAPS["allT"] = ([128, ND, T], BF16)
_TAPS["res"] = ([128, NT_OWN, D], BF16)


WCHK = []


def phase_debug_out(p):
    block, dram = p.block, p.dram

    @block.gpsimd
    def _(g):
        skip_out = getattr(p, "final_written", False)
        if "y2chk" in p.taps:
            for et in range(16):
                ins = g.dma_start(out=p.sb["sqt"][:, 0:2048].rearrange(
                    "p (a b) -> p a b", a=1) if False else p.sb["wchkb2"][:],
                    in_=p.dram["y2_spill"][et, :, :])
                p.inc(ins, "g_tap", 16)
                g.wait_ge(p.sem("g_tap"), p.val("g_tap"))
                ins = g.dma_start(out=p.dram["tap_y2chk"][et, :, :],
                                  in_=p.sb["wchkb2"][:])
                p.inc(ins, "g_tap", 16)
                g.wait_ge(p.sem("g_tap"), p.val("g_tap"))
        for t in p.taps:
            if t == "y2chk" or t == "bctchk":
                continue
            if t.startswith("dchk_"):
                nmm = t[5:]
                n0 = dram[nmm].shape[0]
                for et in range(n0):
                    ins = g.dma_start(out=p.sb["wchkb2"][:],
                                      in_=dram[nmm][et, :, :])
                    p.inc(ins, "g_tap", 16)
                    g.wait_ge(p.sem("g_tap"), p.val("g_tap"))
                    ins = g.dma_start(out=dram["tap_" + t][et, :, :],
                                      in_=p.sb["wchkb2"][:])
                    p.inc(ins, "g_tap", 16)
                    g.wait_ge(p.sem("g_tap"), p.val("g_tap"))
                continue
            if t.startswith("wchk_"):
                wname = t[5:]
                hdim = dram[wname].shape[0]
                for ht in range(hdim):
                    for ot in range(dram[wname].shape[1]):
                        ins = g.dma_start(out=p.sb["wchkb"][:],
                                          in_=dram[wname][ht, ot, :, :])
                        p.inc(ins, "g_tap", 16)
                        g.wait_ge(p.sem("g_tap"), p.val("g_tap"))
                        ins = g.dma_start(out=dram["tap_" + t][ht, ot, :, :],
                                          in_=p.sb["wchkb"][:])
                        p.inc(ins, "g_tap", 16)
                        g.wait_ge(p.sem("g_tap"), p.val("g_tap"))
                continue
            ins = g.dma_start(out=dram["tap_" + t], in_=p.sb[t][:])
            p.inc(ins, "g_tap", 16)
        if not skip_out:
            for n in range(NT_OWN):
                ins = g.dma_start(out=dram["out_own"][n * 128:(n + 1) * 128, :],
                                  in_=p.sb["res"][:, n, :])
                p.inc(ins, "g_out", 16)
            g.wait_ge(p.sem("g_out"), p.val("g_out"))
        if p.taps:
            g.wait_ge(p.sem("g_tap"), p.val("g_tap"))



def build_phases(p, phase_limit):
    phases = [phase_prologue, phase_l0_norm]
    if phase_limit >= 2:
        phases.append(lambda pp: phase_weight_prep(pp, WPREP_L0, "a"))
    if phase_limit >= 3:
        phases.append(lambda pp: phase_weight_prep(
            pp, [w for w in WPREP_REST if w[4] == "rowq"], "b"))
    if phase_limit >= 4:
        phases.append(phase_l0_ffn)
    if phase_limit >= 5:
        phases.append(phase_l0_ffn_b)
    if phase_limit >= 6:
        phases.append(phase_l0_attn)
    if phase_limit >= 7:
        phases.append(phase_l1_norm)
    if phase_limit >= 8:
        phases.append(lambda pp: phase_weight_prep(
            pp, [w for w in WPREP_REST if w[4] == "plain"], "c"))
        if phase_limit >= 9:
            phases.append((phase_l1_skipu, phase_l1_scan))
        else:
            phases.append(phase_l1_skipu)
    if phase_limit >= 10:
        phases.append(phase_bitnet_scale)
        phases.append(lambda pp: phase_weight_prep(
            pp, [w for w in WPREP_REST if w[4] == "bitnet"], "d"))
    if phase_limit >= 11:
        phases.append(phase_l1_ffn_a)
        phases.append(phase_l1_ffn_b)
    if phase_limit >= 12:
        phases.append(phase_l1_wout)
        p.final_written = True
    return phases


def build_program(phase_limit=99, taps=()):
    nc = bass.Bass("TRN2", target_bir_lowering=False, detect_race_conditions=False)
    es = ExitStack()
    dram = {}

    def din(name, shape, dtype=F32):
        dram[name] = nc.dram_tensor(name, shape, dtype, kind="ExternalInput").ap()

    def dout(name, shape, dtype=F32):
        dram[name] = nc.dram_tensor(name, shape, dtype, kind="ExternalOutput").ap()

    def dint(name, shape, dtype=BF16, shared=False):
        h = nc.dram_tensor(name, shape, dtype,
                           addr_space="Shared" if shared else "Local")
        dram[name] = h.ap()

    din("x_own", [TOK, D])
    din("x_all", [T, D])
    din("identb", [128, 128], BF16)
    din("tri", [512, 512])
    din("cst", [128, 16])
    din("pari_in", [1, 2], I32)
    din("rep16", [16, 16, 128], BF16)
    for nm, sh in [("Wq", [D, D]), ("Wk", [D, D]), ("Wv", [D, D]), ("Wo", [D, D]),
                   ("w_gu", [2 * I0, D]), ("w_dn", [D, I0]),
                   ("W_in", [E, D]), ("W_skip_own", [EH, D]),
                   ("W_out_own", [D, EH]), ("Wd1", [128, E]), ("Wd2_own", [EH, 128]),
                   ("WB", [128, E]), ("WC", [128, E]),
                   ("wg1", [I1, D]), ("wu1", [I1, D]), ("wd1", [D, I1])]:
        din(nm, sh)
    din("norm_w", [1, D])
    din("conv_w", [128, E // 128, DC])
    din("conv_b", [128, E // 128])
    din("A_own", [128, EH // 128, NST])
    din("W_D_own", [128, EH // 128])
    din("bd2_own", [128, EH // 128])

    dout("out_own", [TOK, D])
    if "y2chk" in taps:
        dout("tap_y2chk", [16, 128, T], BF16)
    for t in [tt for tt in taps if tt != "y2chk"]:
        if t.startswith("dchk_"):
            nmm = t[5:]
            shp = {"uT_spill": [E // 128, 128, T],
                   "skipT_spill": [EH // 128, 128, T]}[nmm]
            dout("tap_" + t, shp, BF16)
            continue
        if t == "bctchk":
            dout("tap_bctchk", [128, 2, D], BF16)
            dout("tap_ud1chk", [128, D], BF16)
            continue
        if t.startswith("wchk_"):
            wname = t[5:]
            hdim, odim = {"wqT": (D, D), "wkT": (D, D), "wvT": (D, D),
                          "woT": (D, D), "w1qT": (D, 2 * I0), "w2qT": (I0, D),
                          "winT": (D, E), "wskT": (D, EH), "woutT": (EH, D),
                          "wd1T": (E, 128), "wd2T": (128, EH), "wbT": (E, 128),
                          "wcT": (E, 128), "wg1T": (D, I1), "wu1T": (D, I1),
                          "wdn1T": (I1, D)}[wname]
            dout("tap_" + t, [hdim // 128, odim // 128, 128, 128], BF16)
            continue
        shape, dtype = _TAPS[t]
        dout("tap_" + t, shape, dtype)

    def wt_tiled(name, Hdim, Odim):
        dint(name, [Hdim // 128, Odim // 128, 128, 128], BF16)

    for nm, (h, o) in {"wqT": (D, D), "wkT": (D, D), "wvT": (D, D), "woT": (D, D),
                       "w1qT": (D, 2 * I0), "w2qT": (I0, D), "winT": (D, E),
                       "wskT": (D, EH), "woutT": (EH, D), "wd1T": (E, 128),
                       "wd2T": (128, EH), "wbT": (E, 128), "wcT": (E, 128),
                       "wg1T": (D, I1), "wu1T": (D, I1), "wdn1T": (I1, D)}.items():
        wt_tiled(nm, h, o)
    dint("h_spill", [NT_OWN, I0 // 512, 128, 512], F32)
    dint("uT_spill", [E // 128, 128, T], BF16)
    dint("y2_spill", [16, 128, T], BF16)
    dint("h1_spill", [NT_OWN, I1 // 512, 128, 512], BF16)
    dint("skipT_spill", [EH // 128, 128, T], BF16)
    dint("res_spill", [TOK, D], F32)
    dint("sh_xn2T", [ND, 128, T], BF16, shared=True)
    dint("sh_part", [2, TOK, D], F32, shared=True)
    dint("sh_flag", [8, 2], I32, shared=True)

    with es:
        p = P(nc, es, taps)
        p.dram = dram
        p.sbuf("res", [128, NT_OWN, D], BF16)
        p.sbuf("allT", [128, ND, T], BF16)
        p.sbuf("sqt", [128, D], F32)
        p.sbuf("stt", [128, 12], F32)
        p.sbuf("cstt", [128, 16], F32)
        p.sbuf("idb", [128, 128], BF16)
        p.sbuf("prbi", [128, 1], I32)
        p.sbuf("qsh", [128, 1], F32)
        p.sbuf("pari", [1, 2], I32)
        p.sbuf("bns", [128, 8], F32)
        p.sbuf("am", [128, 10], F32)
        p.sbuf("mgw", [1, 2], I32)
        p.sbuf("mgr", [1, 2], I32)
        p.sbuf("Asb", [128, EH // 128, NST], F32)
        p.sbuf("wdsb", [128, EH // 128], F32)
        p.sbuf("bdsb", [128, EH // 128], F32)
        p.sbuf("wchkb", [128, 128], BF16)
        p.sbuf("wchkb2", [128, T], BF16)
        p.psum("ps_t", [128, 512], BF16)
        p.psum("ps_a", [128, 512], BF16)
        p.psum("ps_b", [128, 512])
        p.psum("ps_c", [128, 512])
        p.psum("ps_s", [128, 512])
        p.psum("ps_z", [128, 512])
        p.psum("ps_w", [128, 128])

        p.block = es.enter_context(nc.Block())
        for ph in build_phases(p, phase_limit):
            with ExitStack() as gstk:
                p.gstack = gstk
                if isinstance(ph, tuple):
                    for sub in ph:
                        with ExitStack() as stk:
                            p.stack = stk
                            with nc.named_scope(getattr(sub, "__name__", "sub")):
                                sub(p)
                                p.barrier()
                else:
                    with ExitStack() as stk:
                        p.stack = stk
                        with nc.named_scope(getattr(ph, "__name__", "ph")):
                            ph(p)
                            p.barrier()
        p.stack = es
        phase_debug_out(p)

    return nc



def _host_constants():
    tri = np.triu(np.full((512, 512), NEG_BIG, np.float32), k=1)
    cst = np.zeros((128, 16), np.float32)
    cst[:, 0] = INV_SQRT_HD
    cst[:, 2] = -1.0
    cst[:, 3] = 1.0
    return tri, cst


def make_in_maps(inputs):
    import ml_dtypes
    x = np.asarray(inputs["L0_attn_x"])
    tri, cst = _host_constants()
    identb = np.eye(128, dtype=np.float32).astype(ml_dtypes.bfloat16)
    wb_pad = np.zeros((128, E), np.float32)
    wb_pad[:NST] = inputs["L1_attn_WB"]
    wc_pad = np.zeros((128, E), np.float32)
    wc_pad[:NST] = inputs["L1_attn_WC"]
    maps = []
    for c in range(8):
        b, par = c // 2, c % 2
        rows = own_rows(par)
        x_own = np.concatenate([x[b, r0:r1] for (r0, r1) in rows], axis=0)
        rep16 = np.zeros((16, 16, 128), np.float32)
        for nn in range(16):
            rep16[nn, nn, :] = 1.0
        m = {
            "x_own": x_own, "x_all": x[b],
            "identb": identb, "tri": tri, "cst": cst,
            "rep16": rep16.astype(ml_dtypes.bfloat16),
            "pari_in": np.array([[par, 0]], np.int32),
            "Wq": inputs["L0_attn_Wq"], "Wk": inputs["L0_attn_Wk"],
            "Wv": inputs["L0_attn_Wv"], "Wo": inputs["L0_attn_Wo"],
            "w_gu": inputs["L0_ffn_w_gate_up"], "w_dn": inputs["L0_ffn_w_down"],
            "W_in": inputs["L1_attn_W_in"],
            "W_skip_own": inputs["L1_attn_W_skip"][par * EH:(par + 1) * EH],
            "W_out_own": inputs["L1_attn_W_out"][:, par * EH:(par + 1) * EH],
            "Wd1": inputs["L1_attn_Wd1"],
            "Wd2_own": inputs["L1_attn_Wd2"][par * EH:(par + 1) * EH],
            "WB": wb_pad, "WC": wc_pad,
            "wg1": inputs["L1_ffn_w_gate"], "wu1": inputs["L1_ffn_w_up"],
            "wd1": inputs["L1_ffn_w_down"],
            "norm_w": inputs["L1_attn_norm_w"][None, :],
            "conv_w": inputs["L1_attn_conv_w"][:, 0, :].reshape(
                E // 128, 128, DC).transpose(1, 0, 2),
            "conv_b": inputs["L1_attn_conv_b"].reshape(E // 128, 128).T,
            "A_own": inputs["L1_attn_A_log"][par * EH:(par + 1) * EH].reshape(
                EH // 128, 128, NST).transpose(1, 0, 2),
            "W_D_own": inputs["L1_attn_W_D"][par * EH:(par + 1) * EH].reshape(
                EH // 128, 128).T,
            "bd2_own": inputs["L1_attn_bd2"][par * EH:(par + 1) * EH].reshape(
                EH // 128, 128).T,
        }
        maps.append({k: np.ascontiguousarray(v) for k, v in m.items()})
    return maps


def assemble(results):
    out = np.zeros((B, T, D), np.float32)
    for c in range(8):
        b, par = c // 2, c % 2
        off = 0
        o = results[c]["out_own"]
        for (r0, r1) in own_rows(par):
            out[b, r0:r1] = o[off:off + (r1 - r0)]
            off += r1 - r0
    return out


def kernel(**inputs):
    in_maps = make_in_maps(inputs)
    nc = build_program()
    res = run_bass_kernel_spmd(nc, in_maps, core_ids=list(range(8)))
    return assemble(res.results)
